# version 05
# baseline (speedup 1.0000x reference)
import math
import numpy as np
import ml_dtypes
import concourse.bass as bass
import concourse.mybir as mybir
from concourse.bass_utils import run_bass_kernel_spmd

F32 = mybir.dt.float32
BF16 = mybir.dt.bfloat16
ALU = mybir.AluOpType
ACT = mybir.ActivationFunctionType
AX = mybir.AxisListType

M1 = 12582912.0
M2 = 6755399441055744.0

ENGINES = ('t', 'v', 's', 'g', 'y')


class Op:
    __slots__ = ('eng', 'fn', 'reads', 'writes', 'idx', 'is_dma', 'tag',
                 'seq', 'signals', 'waits', 'drain_before', 'accum', 'label',
                 '_sigvals', 'clock')

    def __init__(self, eng, fn, reads, writes, is_dma, tag, seq, accum, label):
        self.eng = eng
        self.fn = fn
        self.reads = reads
        self.writes = writes
        self.is_dma = is_dma
        self.tag = tag
        self.seq = seq
        self.accum = accum
        self.label = label
        self.idx = -1
        self.signals = []
        self.waits = []
        self.drain_before = False
        self.clock = {}


class Prog:
    def __init__(self, nc):
        self.nc = nc
        self.ops = {e: [] for e in ENGINES}
        self.all_ops = []
        self.last_write = {}
        self.readers = {}
        self.dma_count = {}
        self.dma_written_res = {}
        self.barrier_count = 0
        self.pair_used = set()
        self.tag_used = set()
        self.gidx = {}
        self.pair_water = {}
        self.dma_water = {}
        self.drain_cover = {e: -1 for e in ENGINES}

    def _collect(self, targets, op):
        eprods = []
        dprods = []
        for t in targets:
            if t is None or t[0] == 'B':
                continue
            if t[0] == 'E':
                eprods.append(t[1])
            else:
                dprods.append(t)
        eprods.sort(key=lambda p: -self.gidx[id(p)])
        ck = op.clock
        for prod in eprods:
            if prod.eng == op.eng and op.eng == 't':
                continue
            if ck.get(prod.eng, -1) >= prod.idx:
                continue
            key = (prod.eng, op.eng)
            self.pair_used.add(key)
            prod.signals.append(key)
            op.waits.append(('P', key, prod))
            for k, v in prod.clock.items():
                if ck.get(k, -1) < v:
                    ck[k] = v
            if ck.get(prod.eng, -1) < prod.idx:
                ck[prod.eng] = prod.idx
        for (_, tag, seq, dop) in dprods:
            if ck.get(('D', tag), 0) >= seq:
                continue
            self.tag_used.add(tag)
            op.waits.append(('D', tag, 16 * seq))
            ck[('D', tag)] = seq
            if dop is not None:
                for k, v in dop.clock.items():
                    if ck.get(k, -1) < v:
                        ck[k] = v

    def record(self, eng, fn, reads=(), writes=(), is_dma=False, dma_res=None,
               accum=None, label=''):
        reads = list(reads)
        writes = list(writes)
        tag = seq = None
        if is_dma:
            assert dma_res is not None
            tag = dma_res
            seq = self.dma_count.get(tag, 0) + 1
            self.dma_count[tag] = seq
        op = Op(eng, fn, reads, writes, is_dma, tag, seq, accum, label)
        op.idx = len(self.ops[eng])
        self.gidx[id(op)] = len(self.all_ops)
        targets = []
        for r in reads:
            targets.append(self.last_write.get(r))
        for r in writes:
            lw = self.last_write.get(r)
            if lw is not None and not (accum is not None and lw[0] == 'E'
                                       and lw[1].accum == accum):
                targets.append(lw)
            for rd in self.readers.get(r, ()):
                targets.append(rd)
        self._collect(targets, op)
        self.ops[eng].append(op)
        self.all_ops.append(op)
        me = ('D', tag, seq, op) if is_dma else ('E', op)
        for r in reads:
            self.readers.setdefault(r, []).append(me)
        for r in writes:
            self.last_write[r] = me
            self.readers[r] = []
        return op

    def dma(self, queue_eng, out_ap, in_ap, reads, writes, label=''):
        def fn(eng, _o=out_ap, _i=in_ap):
            return eng.dma_start(out=_o, in_=_i)
        return self.record(queue_eng, fn, reads=reads, writes=writes,
                           is_dma=True, dma_res=writes[0], label=label)

    def barrier(self):
        self.barrier_count += 1
        k = self.barrier_count
        pending = [(tag, 16 * cnt) for tag, cnt in self.dma_count.items()]
        for tag, val in pending:
            self.tag_used.add(tag)

        def mk(eng_name):
            def fn(eng, _k=k, _pending=pending, _e=eng_name):
                if _e == 'y':
                    for tag, val in _pending:
                        if self.dma_water2.get(('y', tag), 0) < val:
                            eng.wait_ge(self._tag_sem[tag], val)
                            self.dma_water2[('y', tag)] = val
                eng.drain()
                eng.nop().then_inc(self._bar_sem, 1)
                eng.wait_ge(self._bar_sem, 5 * _k)
                return None
            return fn
        for e in ENGINES:
            op = Op(e, mk(e), [], [], False, None, None, None, f'barrier{k}')
            op.idx = len(self.ops[e])
            self.ops[e].append(op)
            self.all_ops.append(op)
        for r in self.last_write:
            self.last_write[r] = ('B',)
        self.readers = {}
        for e in ENGINES:
            self.drain_cover[e] = len(self.ops[e]) - 1

    def materialize(self, stack):
        nc = self.nc
        self._bar_sem = stack.enter_context(nc.semaphore('bar'))
        self._pair_sem = {}
        for key in sorted(self.pair_used):
            self._pair_sem[key] = stack.enter_context(
                nc.semaphore(f'p_{key[0]}{key[1]}'))
        self._tag_sem = {}
        for tag in sorted(self.tag_used):
            self._tag_sem[tag] = stack.enter_context(nc.semaphore(f'd_{tag}'))
        pair_counter = {k: 0 for k in self.pair_used}
        for e in ENGINES:
            for op in self.ops[e]:
                sig = sorted(set(op.signals))
                op.signals = sig
                op._sigvals = {}
                for key in sig:
                    pair_counter[key] += 1
                    op._sigvals[key] = pair_counter[key]
        blk = self.nc.Block()
        self._block = blk

        def emit_engine(eng_handle, ename):
            for op in self.ops[ename]:
                for w in op.waits:
                    if w[0] == 'P':
                        _, key, prod = w
                        val = prod._sigvals[key]
                        if self.pair_water2.get((ename, key), 0) < val:
                            eng_handle.wait_ge(self._pair_sem[key], val)
                            self.pair_water2[(ename, key)] = val
                    else:
                        _, tag, val = w
                        if self.dma_water2.get((ename, tag), 0) < val:
                            eng_handle.wait_ge(self._tag_sem[tag], val)
                            self.dma_water2[(ename, tag)] = val
                if op.drain_before:
                    eng_handle.drain()
                inst = op.fn(eng_handle)
                if op.is_dma:
                    assert inst is not None, f'dma op {op.label} returned None'
                    inst.then_inc(self._tag_sem[op.tag], 16)
                    if op.signals:
                        for key in op.signals:
                            eng_handle.nop().then_inc(self._pair_sem[key], 1)
                elif op.signals:
                    if inst is not None:
                        inst.then_inc(self._pair_sem[op.signals[0]], 1)
                        rest = op.signals[1:]
                    else:
                        rest = op.signals
                    if rest:
                        eng_handle.drain()
                        for key in rest:
                            eng_handle.nop().then_inc(self._pair_sem[key], 1)

        self.pair_water2 = {}
        self.dma_water2 = {}
        with blk as block:
            block.tensor(lambda e: emit_engine(e, 't'))
            block.vector(lambda e: emit_engine(e, 'v'))
            block.scalar(lambda e: emit_engine(e, 's'))
            block.gpsimd(lambda e: emit_engine(e, 'g'))
            block.sync(lambda e: emit_engine(e, 'y'))


class Buf:
    __slots__ = ('mem', 'byte_start', 'cols', 'name', 'dtype', 'freed', 't')

    def __init__(self, mem, byte_start, cols, name, dtype):
        self.mem = mem
        self.byte_start = byte_start
        self.cols = cols
        self.name = name
        self.dtype = dtype
        self.freed = False
        mem.uid += 1
        self.t = mem.nc.alloc_sbuf_tensor_at(
            f'c{mem.uid}_{name}', [128, cols], dtype,
            offset=mem.addr + byte_start)

    def ap(self, c0=0, c1=None, p0=0, p1=128):
        assert not self.freed, self.name
        if c1 is None:
            c1 = self.cols
        assert 0 <= c0 <= c1 <= self.cols, (self.name, c0, c1, self.cols)
        return self.t.ap()[p0:p1, c0:c1]

    def ap3(self, start, stride, nblk, width, p0=0, p1=128):
        assert not self.freed, self.name
        assert start + (nblk - 1) * stride + width <= self.cols
        assert p0 == 0 and p1 == 128
        import dataclasses
        full = self.t.ap()
        row_stride = full.ap[0][0]
        return dataclasses.replace(
            full, offset=full.offset + start,
            ap=type(full.ap)([[row_stride, 128], [stride, nblk], [1, width]]))

    def free(self):
        assert not self.freed
        self.mem._free(self.byte_start, self.cols * (4 if self.dtype == F32 else 2))
        self.freed = True


class Mem:
    def __init__(self, nc, nbytes):
        self.nc = nc
        self.nbytes = nbytes
        self.uid = 0
        base = nc.alloc_sbuf_tensor('arena8', [128, nbytes], mybir.dt.uint8)
        self.addr = nc.lookup_mloc(base).addr
        assert self.addr % 64 == 0, self.addr
        self.free_list = [(0, nbytes)]

    def _alloc(self, n, align=64):
        for i, (st, sz) in enumerate(self.free_list):
            a = (st + align - 1) // align * align
            if a + n <= st + sz:
                repl = [p for p in ((st, a - st), (a + n, st + sz - a - n))
                        if p[1] > 0]
                self.free_list[i:i + 1] = repl
                return a
        used = self.nbytes - sum(sz for _, sz in self.free_list)
        raise MemoryError(f'arena: cannot alloc {n}B (used={used}B, '
                          f'free={self.free_list[:6]})')

    def _free(self, start, n):
        self.free_list.append((start, n))
        self.free_list.sort()
        merged = []
        for st, sz in self.free_list:
            if merged and merged[-1][0] + merged[-1][1] == st:
                merged[-1] = (merged[-1][0], merged[-1][1] + sz)
            else:
                merged.append((st, sz))
        self.free_list = merged

    def f32(self, name, cols):
        return Buf(self, self._alloc(cols * 4, 64 if cols > 8 else 32),
                   cols, name, F32)

    def bf16(self, name, cols):
        return Buf(self, self._alloc(cols * 2, 64 if cols > 16 else 32),
                   cols, name, BF16)


def ceil_div(a, b):
    return (a + b - 1) // b




def ceil_div(a, b):
    return (a + b - 1) // b


def _ring_pieces(g0, length, RK):
    out = []
    g = g0
    rem = length
    while rem > 0:
        r = g % RK
        ln = min(rem, RK - r)
        out.append((g, r, ln))
        g += ln
        rem -= ln
    return out


def _chunk_rows(d_ap, nchunk):
    return d_ap.rearrange('o (j p) -> (o p) j', p=128)


def build_core(cfg):
    S = cfg['S']
    D = cfg['D']
    H = cfg['H']
    NSEQ = cfg['NSEQ']
    TF = cfg['TF']
    NT = S * NSEQ
    n_dc = D // 128
    n_hc = H // 128
    TPG = TF // 128
    NHG = S // TF
    ntile_seq = S // 128
    ntiles = NT // 128
    RK = TF + 128
    PH = cfg.get('PHASES', 'full')

    nc = bass.Bass('TRN2', target_bir_lowering=False,
                   dynamic_dma_scratch_size=512)

    d_x = nc.dram_tensor('x_in', [NT, D], F32, kind='ExternalInput').ap()
    d_w1 = nc.dram_tensor('w1', [H, D], F32, kind='ExternalInput').ap()
    d_w2 = nc.dram_tensor('w2', [D, H], F32, kind='ExternalInput').ap()
    d_cw = [nc.dram_tensor(f'cw{i}', [128, H // 128], F32,
                           kind='ExternalInput').ap() for i in range(3)]
    d_cb = nc.dram_tensor('cb', [128, H // 128], F32,
                          kind='ExternalInput').ap()
    d_wx = {w: nc.dram_tensor('w' + w, [D, D], F32, kind='ExternalInput').ap()
            for w in 'qkvo'}
    d_bx = {w: nc.dram_tensor('b' + w, [1, D], F32, kind='ExternalInput').ap()
            for w in 'qkvo'}
    d_ident = nc.dram_tensor('ident', [128, 128], BF16,
                             kind='ExternalInput').ap()
    d_out = nc.dram_tensor('out', [NT, D], F32, kind='ExternalOutput').ap()

    d_x1 = nc.dram_tensor('x1d', [NT, D], F32).ap()
    d_x2 = nc.dram_tensor('x2d', [NT, D], F32).ap()
    d_m1T = nc.dram_tensor('m1Td', [128, n_dc * H], BF16).ap()
    d_m2T = nc.dram_tensor('m2Td', [128, n_hc * D], BF16).ap()
    d_lnh = nc.dram_tensor('lnhd', [ntiles, n_dc, 128, 128], BF16).ap()
    d_lnl = nc.dram_tensor('lnld', [ntiles, n_dc, 128, 128], BF16).ap()
    d_qh = nc.dram_tensor('qhd', [ntiles, n_dc, 128, 128], BF16).ap()
    d_ql = nc.dram_tensor('qld', [ntiles, n_dc, 128, 128], BF16).ap()
    d_wT = {}
    for w in 'qkvo':
        for hl in 'hl':
            d_wT[w + hl] = nc.dram_tensor(f'w{w}T{hl}d', [128, n_dc * D],
                                          BF16).ap()

    psum_mm = [nc.alloc_psum_tensor(f'pmm{i}', [128, 512], F32)
               for i in range(6)]
    psum_tb = nc.alloc_psum_tensor('ptb', [128, 1024], BF16)
    psum_st = nc.alloc_psum_tensor('pst', [128, 512], F32)

    mem = Mem(nc, (nc.sbuf_bytes_remaining - 128) // 64 * 64)
    P = Prog(nc)

    ident = mem.bf16('ident', 128)
    onesr = mem.f32('onesr', 128)
    onesc = mem.f32('onesc', 1)
    magic = mem.f32('magic', 1)
    kapb = mem.f32('kap', 1)
    cdesc1 = mem.f32('cdesc1', 1)
    cdesc2 = mem.f32('cdesc2', 1)
    convc = [mem.f32(f'convc{i}', n_hc) for i in range(3)]
    convb = mem.f32('convb', n_hc)
    eps_b = mem.f32('epsb', 1)

    identf = mem.f32('identf', 128)
    i_ap = ident.ap()
    if_ap = identf.ap()
    ones_row = onesr.ap(p0=0, p1=1)

    P.dma('y', i_ap, d_ident, reads=['d_ident'], writes=['ident'],
          label='ld_ident')
    P.record('v', lambda e, o=if_ap, i=i_ap: e.tensor_copy(o, i),
             reads=['ident'], writes=['identf'])
    for i in range(3):
        P.dma('y', convc[i].ap(), d_cw[i], reads=[f'd_cw{i}'],
              writes=[f'convc{i}'], label=f'ld_cw{i}')
    P.dma('y', convb.ap(), d_cb, reads=['d_cb'], writes=['convb'],
          label='ld_cb')
    P.record('v', lambda e, o=onesr.ap(p0=0, p1=1): e.memset(o, 1.0),
             writes=['ones1'])
    P.record('v', lambda e, o=onesc.ap(): e.memset(o, 1.0), writes=['onesc'])
    P.record('v', lambda e, o=eps_b.ap(): e.memset(o, 1e-5), writes=['epsb'])
    kap_ap = kapb.ap()
    magic_ap = magic.ap()
    P.record('v', lambda e, o=kap_ap: e.memset(o, 1.5), writes=['kap'])
    P.record('v', lambda e, o=kap_ap: e.tensor_scalar(o, o, M2, M2, ALU.add,
             ALU.subtract), reads=['kap'], writes=['kap'])
    P.record('v', lambda e, o=magic_ap, i=kap_ap: e.tensor_scalar(
        o, i, (M2 - M1) / 2.0, M1, ALU.mult, ALU.add),
        reads=['kap'], writes=['magic'])

    def bcast_row(row_ap, row_res, out_ap, out_res, n):
        ps = psum_st.ap()[:, 0:n]
        P.record('t', lambda e, o=ps, l=ones_row, r=row_ap: e.matmul(
            o, l, r, start=True, stop=True), reads=['ones1', row_res],
            writes=['ps_st'], label=f'bc_{out_res}')
        P.record('v', lambda e, o=out_ap, i=ps: e.tensor_copy(o, i),
                 reads=['ps_st'], writes=[out_res])

    def col2row(col_ap, col_res, row_ap, row_res, wp=128):
        ps = psum_st.ap()[0:1, 0:wp]
        P.record('t', lambda e, o=ps, i=col_ap, idn=if_ap[0:wp, 0:wp]:
                 e.transpose(o, i, idn),
                 reads=[col_res, 'identf'], writes=['ps_st'])
        P.record('v', lambda e, o=row_ap, i=ps: e.tensor_copy(o, i),
                 reads=['ps_st'], writes=[row_res])

    def row2col(row_ap, row_res, col_ap, col_res, wp=128):
        ps = psum_st.ap()[0:wp, 0:1]
        P.record('t', lambda e, o=ps, i=row_ap, idn=if_ap[0:1, 0:1]:
                 e.transpose(o, i, idn),
                 reads=[row_res, 'identf'], writes=['ps_st'])
        P.record('v', lambda e, o=col_ap, i=ps: e.tensor_copy(o, i),
                 reads=['ps_st'], writes=[col_res])

    def tr_tile(src_aps, src_res, dst3_ap, dst_res, nblk, key, evac='v'):
        for c in range(nblk):
            pt = psum_tb.ap()[:, c * 128:(c + 1) * 128]
            P.record('t', lambda e, o=pt, i=src_aps[c], idn=i_ap:
                     e.transpose(o, i, idn), reads=[src_res, 'ident'],
                     writes=['ps_tb'], accum=key)
        src = psum_tb.ap()[:, 0:nblk * 128].rearrange('p (c n) -> p c n',
                                                      c=nblk)
        P.record(evac, lambda e, o=dst3_ap, i=src: e.tensor_copy(o, i),
                 reads=['ps_tb'], writes=[dst_res])

    def emit_wquant(d_w, rows, cols, d_mT, dmT_res, cdesc_buf, cdesc_res,
                    wname):
        n_rt = rows // 128
        n_cc = cols // 128
        stage = mem.f32(f'{wname}_stage', n_rt * cols)
        acc = mem.f32(f'{wname}_acc', 1)
        red = mem.f32(f'{wname}_red', 1)
        scl = mem.f32(f'{wname}_scl', 1)
        mq = mem.bf16(f'{wname}_mq', cols)
        mT_sb = mem.bf16(f'{wname}_mT', n_cc * rows)
        sres = f'{wname}_stage'
        for t in range(n_rt):
            P.dma('y', stage.ap(t * cols, (t + 1) * cols),
                  d_w[t * 128:(t + 1) * 128, :], reads=[f'd_{wname}'],
                  writes=[sres], label=f'{wname}_ld{t}')
        for t in range(n_rt):
            sap = stage.ap(t * cols, (t + 1) * cols)
            P.record('v', lambda e, o=red.ap(), i=sap: e.reduce_sum(
                o, i, AX.X, apply_absolute_value=True),
                reads=[sres], writes=[f'{wname}_red'])
            if t == 0:
                P.record('v', lambda e, o=acc.ap(), i=red.ap():
                         e.tensor_copy(o, i), reads=[f'{wname}_red'],
                         writes=[f'{wname}_acc'])
            else:
                P.record('v', lambda e, o=acc.ap(), i=red.ap():
                         e.tensor_add(o, o, i),
                         reads=[f'{wname}_red', f'{wname}_acc'],
                         writes=[f'{wname}_acc'])
        ps = psum_st.ap()[0:1, 0:1]
        P.record('t', lambda e, o=ps, l=acc.ap(), r=onesc.ap(): e.matmul(
            o, l, r, start=True, stop=True),
            reads=[f'{wname}_acc', 'onesc'], writes=['ps_st'])
        sc1 = scl.ap(p0=0, p1=1)[:, 0:1]
        P.record('v', lambda e, o=sc1, i=ps: e.tensor_scalar(
            o, i, 1.0 / (rows * cols), 1e-5, ALU.mult, ALU.max),
            reads=['ps_st'], writes=[f'{wname}_c'])
        P.record('v', lambda e, o=sc1: e.reciprocal(o, o),
                 reads=[f'{wname}_c'], writes=[f'{wname}_c'])
        bcast_row(sc1, f'{wname}_c', scl.ap(), f'{wname}_s128', 1)
        P.record('v', lambda e, o=cdesc_buf.ap(), i=scl.ap():
                 e.reciprocal(o, i), reads=[f'{wname}_s128'],
                 writes=[cdesc_res])
        for t in range(n_rt):
            sap = stage.ap(t * cols, (t + 1) * cols)
            P.record('v', lambda e, o=sap, s=scl.ap(), m=magic_ap:
                     e.tensor_scalar(o, o, s, m, ALU.mult, ALU.add),
                     reads=[sres, f'{wname}_s128', 'magic'], writes=[sres])
            P.record('v', lambda e, o=sap, m=magic_ap: e.tensor_scalar(
                o, o, m, None, ALU.subtract), reads=[sres, 'magic'],
                writes=[sres])
            P.record('v', lambda e, o=mq.ap(), i=sap: e.tensor_scalar(
                o, i, -1.0, 1.0, ALU.max, ALU.min),
                reads=[sres], writes=[f'{wname}_mq'])
            nb = 8 if n_cc >= 8 else n_cc
            for cb in range(0, n_cc, nb):
                blks = [mq.ap(c * 128, (c + 1) * 128)
                        for c in range(cb, cb + nb)]
                dst3 = mT_sb.ap3(cb * rows + t * 128, rows, nb, 128)
                tr_tile(blks, f'{wname}_mq', dst3, f'{wname}_mT', nb,
                        (wname, t, cb))
        P.dma('s', d_mT, mT_sb.ap(), reads=[f'{wname}_mT'], writes=[dmT_res],
              label=f'{wname}_spill')
        stage.free(); acc.free(); red.free(); scl.free(); mq.free()
        mT_sb.free()

    def emit_ffn(in_ap, out_ap, in_res, out_res, fid):
        m1T = mem.bf16(f'{fid}_m1T', n_dc * H)
        m2T = mem.bf16(f'{fid}_m2T', n_hc * D)
        nq = min(4, n_dc)
        for q in range(nq):
            lo, hi = q * (n_dc // nq) * H, (q + 1) * (n_dc // nq) * H
            P.dma('y', m1T.ap(lo, hi), d_m1T[:, lo:hi], reads=['d_w1mT'],
                  writes=[f'{fid}m1Tq{q}'], label=f'{fid}_m1Tld{q}')
        nq2 = min(4, n_hc)
        for q in range(nq2):
            lo, hi = q * (n_hc // nq2) * D, (q + 1) * (n_hc // nq2) * D
            P.dma('y', m2T.ap(lo, hi), d_m2T[:, lo:hi], reads=['d_w2mT'],
                  writes=[f'{fid}m2Tq{q}'], label=f'{fid}_m2Tld{q}')

        xr = [mem.f32(f'{fid}_xr{i}', D) for i in range(3)]
        sq = [mem.f32(f'{fid}_sq{i}', D) for i in range(1)]
        stt_ = [mem.f32(f'{fid}_stt{i}', 24) for i in range(2)]
        k1tm = [mem.bf16(f'{fid}_k1tm{i}', D) for i in range(1)]
        k1fm = [mem.bf16(f'{fid}_k1fm{i}', n_dc * TF) for i in range(2)]
        g1row = mem.f32(f'{fid}_g1row', TF)
        g1bc = [mem.f32(f'{fid}_g1bc{i}', TF) for i in range(1)]
        Bs = [mem.f32(f'{fid}_Bs{i}', TF + 3) for i in range(2)]
        carry = mem.f32(f'{fid}_carry', 2 * n_hc)
        ygrp = mem.f32(f'{fid}_ygrp', n_hc * (TF + 1))
        yab = [mem.f32(f'{fid}_yab{i}', TF + 2) for i in range(1)]
        maxacc = mem.f32(f'{fid}_maxacc', TF + 2)
        s2row = mem.f32(f'{fid}_s2row', TF + 2)
        fring = mem.f32(f'{fid}_fring', RK)
        s2bc = [mem.f32(f'{fid}_s2bc{i}', TF + 2) for i in range(1)]
        frowb = mem.f32(f'{fid}_frow', 128)
        k2r = mem.bf16(f'{fid}_k2r', n_hc * RK)
        ftm = [mem.f32(f'{fid}_ftm{i}', 1) for i in range(2)]

        gctr = 0
        for seq in range(NSEQ):
            done_tiles = 0
            for hg in range(NHG):
                t0 = hg * TF
                ws = max(t0 - 1, 0)
                we = S if hg == NHG - 1 else t0 + TF - 1
                W = we - ws
                ridx = gctr % 2
                for i in range(TPG):
                    a_glob = (seq * S + t0) // 128 + i
                    xi = xr[(gctr * TPG + i) % 3]
                    xires = f'{fid}_xr{(gctr * TPG + i) % 3}'
                    P.dma('y', xi.ap(), in_ap[a_glob * 128:(a_glob + 1) * 128, :],
                          reads=[in_res], writes=[xires],
                          label=f'{fid}_xld{a_glob}')
                    st_i = stt_[i % 2]
                    rS = f'{fid}_stt{i % 2}'
                    sq_i, rQ = sq[0], f'{fid}_sq0'
                    k1_i, rK = k1tm[0], f'{fid}_k1tm0'
                    cm = lambda c: st_i.ap(c, c + 1)
                    P.record('v', lambda e, o=cm(0), x=xi.ap(): e.reduce_sum(
                        o, x, AX.X), reads=[xires], writes=[rS + 'a'])
                    P.record('v', lambda e, o=cm(1), i_=cm(0): e.tensor_scalar(
                        o, i_, -1.0 / D, None, ALU.mult), reads=[rS + 'a'],
                        writes=[rS + 'm'])
                    P.record('s', lambda e, o=sq_i.ap(), x=xi.ap(), b=cm(1),
                             a=cm(2): e.activation(o, x, ACT.Square, bias=b,
                             accum_out=a),
                             reads=[xires, rS + 'm'], writes=[rQ, rS + 'v'])
                    P.record('v', lambda e, o=cm(3), i_=cm(2): e.tensor_scalar(
                        o, i_, 1.0 / D, None, ALU.mult), reads=[rS + 'v'],
                        writes=[rS + 'v2'])
                    P.record('s', lambda e, o=cm(4), i_=cm(3), b=eps_b.ap():
                             e.activation(o, i_, ACT.Sqrt, bias=b),
                             reads=[rS + 'v2', 'epsb'], writes=[rS + 'sd'])
                    P.record('v', lambda e, o=cm(5), i_=cm(4): e.reciprocal(o, i_),
                             reads=[rS + 'sd'], writes=[rS + 'r'])
                    P.record('v', lambda e, o=cm(6), i_=xi.ap(): e.reduce_max(
                        o, i_, AX.X), reads=[xires], writes=[rS + 'x'])
                    P.record('v', lambda e, o=cm(11), i_=xi.ap():
                             e.tensor_reduce(o, i_, AX.X, ALU.min),
                             reads=[xires], writes=[rS + 'n'])
                    P.record('v', lambda e, o=cm(6), m=cm(1):
                             e.tensor_scalar(o, o, m, None, ALU.add),
                             reads=[rS + 'x', rS + 'm'], writes=[rS + 'x'])
                    P.record('v', lambda e, o=cm(11), m=cm(1):
                             e.tensor_scalar(o, o, m, -1.0, ALU.add, ALU.mult),
                             reads=[rS + 'n', rS + 'm'], writes=[rS + 'n'])
                    P.record('v', lambda e, o=cm(6), i1=cm(11):
                             e.tensor_tensor(o, o, i1, ALU.max),
                             reads=[rS + 'x', rS + 'n'], writes=[rS + 'x'])
                    P.record('v', lambda e, o=cm(7), i_=cm(6), r=cm(5):
                             e.tensor_scalar(o, i_, r, 1e-5, ALU.mult, ALU.max),
                             reads=[rS + 'x', rS + 'r'], writes=[rS + 'u'])
                    P.record('v', lambda e, o=cm(8), i_=cm(7): e.reciprocal(o, i_),
                             reads=[rS + 'u'], writes=[rS + 'u2'])
                    P.record('v', lambda e, o=cm(9), i_=cm(8), r=cm(5):
                             e.tensor_scalar(o, i_, 127.0, r, ALU.mult, ALU.mult),
                             reads=[rS + 'u2', rS + 'r'], writes=[rS + 's1'])
                    P.record('v', lambda e, o=sq_i.ap(), x=xi.ap(), m=cm(1),
                             s=cm(9): e.tensor_scalar(o, x, m, s, ALU.add,
                             ALU.mult), reads=[xires, rS + 'm', rS + 's1'],
                             writes=[rQ])
                    P.record('v', lambda e, o=k1_i.ap(), i_=sq_i.ap(),
                             m=magic_ap: e.tensor_scalar(o, i_, m, m, ALU.add,
                             ALU.subtract), reads=[rQ, 'magic'], writes=[rK])
                    P.record('v', lambda e, o=cm(10), i_=cm(7), c=cdesc1.ap():
                             e.tensor_scalar(o, i_, c, 1.0 / 127.0, ALU.mult,
                             ALU.mult), reads=[rS + 'u', 'w1_cdesc'],
                             writes=[rS + 'g'])
                    col2row(cm(10), rS + 'g',
                            g1row.ap(i * 128, (i + 1) * 128, p0=0, p1=1),
                            f'{fid}_g1row', 128)
                    blks = [k1_i.ap(dc * 128, (dc + 1) * 128)
                            for dc in range(n_dc)]
                    dst3 = k1fm[ridx].ap3(i * 128, TF, n_dc, 128)
                    tr_tile(blks, rK, dst3, f'{fid}_k1fm{ridx}', n_dc,
                            (fid, 'k1t', gctr, i))
                bcast_row(g1row.ap(0, TF, p0=0, p1=1), f'{fid}_g1row',
                          g1bc[0].ap(0, TF), f'{fid}_g1bc0', TF)
                for j in range(n_hc):
                    pb = psum_mm[j % 2]
                    pbres = f'pb{j % 2}'
                    for dc in range(n_dc):
                        P.record('t', lambda e, o=pb.ap()[:, 0:TF],
                                 l=m1T.ap(dc * H + j * 128,
                                          dc * H + (j + 1) * 128),
                                 r=k1fm[ridx].ap(dc * TF, (dc + 1) * TF),
                                 st=(dc == 0), sp=(dc == n_dc - 1):
                                 e.matmul(o, l, r, start=st, stop=sp),
                                 reads=[f'{fid}m1Tq{dc * nq // n_dc}',
                                        f'{fid}_k1fm{ridx}'],
                                 writes=[pbres], accum=(fid, 1, seq, hg, j),
                                 label=f'{fid}_mm1_{seq}_{hg}_{j}')
                    Bj, Bres = Bs[j % 2], f'{fid}_Bs{j % 2}'
                    if hg == 0:
                        P.record('v', lambda e, o=Bj.ap(0, 2): e.memset(o, 0.0),
                                 writes=[Bres])
                    else:
                        P.record('v', lambda e, o=Bj.ap(0, 2),
                                 i_=carry.ap(2 * j, 2 * j + 2):
                                 e.tensor_copy(o, i_), reads=[f'{fid}_carry'],
                                 writes=[Bres])
                    if hg == NHG - 1:
                        P.record('v', lambda e, o=Bj.ap(TF + 2, TF + 3):
                                 e.memset(o, 0.0), writes=[Bres])
                    P.record('v', lambda e, o=Bj.ap(2, TF + 2),
                             i0=pb.ap()[:, 0:TF], i1=g1bc[0].ap(0, TF):
                             e.tensor_tensor(o, i0, i1, ALU.mult),
                             reads=[pbres, f'{fid}_g1bc0'], writes=[Bres])
                    if hg < NHG - 1:
                        P.record('v', lambda e, o=carry.ap(2 * j, 2 * j + 2),
                                 i_=Bj.ap(TF, TF + 2): e.tensor_copy(o, i_),
                                 reads=[Bres], writes=[f'{fid}_carry'])
                    co = ws - t0 + 2
                    yj = ygrp.ap(j * (TF + 1), j * (TF + 1) + W)
                    yjres = f'{fid}_y{j}'
                    c0j = convc[0].ap(j, j + 1)
                    c1j = convc[1].ap(j, j + 1)
                    c2j = convc[2].ap(j, j + 1)
                    cbj = convb.ap(j, j + 1)
                    P.record('s', lambda e, o=yj, i_=Bj.ap(co - 1, co - 1 + W),
                             c=c0j, b=cbj: e.activation(o, i_, ACT.Identity,
                             bias=b, scale=c),
                             reads=[Bres, 'convc0', 'convb'], writes=[yjres])
                    P.record('v', lambda e, o=yj, i0=Bj.ap(co, co + W), c=c1j,
                             i1=yj: e.scalar_tensor_tensor(o, i0, c, i1,
                             ALU.mult, ALU.add),
                             reads=[Bres, 'convc1', yjres], writes=[yjres])
                    P.record('v', lambda e, o=yj, i0=Bj.ap(co + 1, co + 1 + W),
                             c=c2j, i1=yj: e.scalar_tensor_tensor(o, i0, c, i1,
                             ALU.mult, ALU.add),
                             reads=[Bres, 'convc2', yjres], writes=[yjres])
                    P.record('s', lambda e, o=yj: e.activation(o, o, ACT.Silu),
                             reads=[yjres], writes=[yjres])
                    ya, yares = yab[0], f'{fid}_yab0'
                    P.record('s', lambda e, o=ya.ap(0, W), i_=yj: e.activation(
                        o, i_, ACT.Abs), reads=[yjres], writes=[yares])
                    if j == 0:
                        P.record('v', lambda e, o=maxacc.ap(0, W),
                                 i_=ya.ap(0, W): e.tensor_copy(o, i_),
                                 reads=[yares], writes=[f'{fid}_mx'])
                    else:
                        P.record('v', lambda e, o=maxacc.ap(0, W),
                                 i_=ya.ap(0, W): e.tensor_tensor(o, o, i_,
                                 ALU.max), reads=[yares, f'{fid}_mx'],
                                 writes=[f'{fid}_mx'])
                for w in range(ceil_div(W, 128)):
                    wp = min(128, W - w * 128)
                    pst = psum_st.ap()[0:wp, 0:128]
                    P.record('t', lambda e, o=pst, i_=maxacc.ap(
                        w * 128, w * 128 + wp), idn=if_ap: e.transpose(
                        o, i_, idn), reads=[f'{fid}_mx', 'identf'],
                        writes=['ps_st'])
                    P.record('v', lambda e, o=stt_[0].ap(16, 17, p1=wp),
                             i_=pst: e.reduce_max(o, i_, AX.X),
                             reads=['ps_st'], writes=['am_tm'])
                    P.record('v', lambda e, o=stt_[0].ap(17, 18, p1=wp),
                             i_=stt_[0].ap(16, 17, p1=wp): e.tensor_scalar(
                             o, i_, 1e-5, None, ALU.max), reads=['am_tm'],
                             writes=['u2f'])
                    P.record('v', lambda e, o=stt_[0].ap(18, 19, p1=wp),
                             i_=stt_[0].ap(17, 18, p1=wp): e.reciprocal(o, i_),
                             reads=['u2f'], writes=['r2f'])
                    P.record('v', lambda e, o=stt_[0].ap(19, 20, p1=wp),
                             i_=stt_[0].ap(18, 19, p1=wp): e.tensor_scalar(
                             o, i_, 127.0, None, ALU.mult), reads=['r2f'],
                             writes=['s2tm'])
                    P.record('v', lambda e, o=stt_[0].ap(20, 21, p1=wp),
                             i_=stt_[0].ap(17, 18, p1=wp),
                             c=cdesc2.ap(0, 1, p1=wp): e.tensor_scalar(
                             o, i_, c, 0.5 / 127.0, ALU.mult, ALU.mult),
                             reads=['u2f', 'w2_cdesc'], writes=['ftm0'])
                    col2row(stt_[0].ap(19, 20, p1=wp), 's2tm',
                            s2row.ap(w * 128, w * 128 + wp, p0=0, p1=1),
                            f'{fid}_s2row', wp)
                    col2row(stt_[0].ap(20, 21, p1=wp), 'ftm0',
                            frowb.ap(0, wp, p0=0, p1=1), 'frow', wp)
                    for (gg, rr, ln) in _ring_pieces(ws + w * 128, wp, RK):
                        o0 = gg - ws - w * 128
                        P.record('v', lambda e, o=fring.ap(rr, rr + ln, p0=0,
                                 p1=1), i_=frowb.ap(o0, o0 + ln, p0=0, p1=1):
                                 e.tensor_copy(o, i_),
                                 reads=['frow'], writes=['fring'])
                bcast_row(s2row.ap(0, W, p0=0, p1=1), f'{fid}_s2row',
                          s2bc[0].ap(0, W), f'{fid}_s2bc0', W)
                for j in range(n_hc):
                    yj = ygrp.ap(j * (TF + 1), j * (TF + 1) + W)
                    ya, yares = yab[0], f'{fid}_yab0'
                    P.record('v', lambda e, o=ya.ap(0, W), i0=yj,
                             i1=s2bc[0].ap(0, W): e.tensor_tensor(
                             o, i0, i1, ALU.mult),
                             reads=[f'{fid}_y{j}', f'{fid}_s2bc0'],
                             writes=[yares])
                    for (gg, rr, ln) in _ring_pieces(ws, W, RK):
                        P.record('v', lambda e, o=k2r.ap(j * RK + rr,
                                 j * RK + rr + ln), i_=ya.ap(gg - ws,
                                 gg - ws + ln), m=magic_ap: e.tensor_scalar(
                                 o, i_, m, m, ALU.add, ALU.subtract),
                                 reads=[yares, 'magic'],
                                 writes=[f'{fid}_k2r{j}'])
                ready = S // 128 if hg == NHG - 1 else (we + 1 - 128) // 128
                for a in range(done_tiles, ready):
                    a_glob = seq * ntile_seq + a
                    ga = seq * ntile_seq + a
                    pbt = [psum_mm[2 + 2 * (a % 2)], psum_mm[3 + 2 * (a % 2)]]
                    pbr = [f'pb{2 + 2 * (a % 2)}', f'pb{3 + 2 * (a % 2)}']
                    xsi, xsres = xr[ga % 3], f'{fid}_xr{ga % 3}'
                    rc = (a * 128) % RK
                    nhalf = ceil_div(D, 512)
                    for j in range(n_hc):
                        for h2 in range(nhalf):
                            nn = min(512, D - h2 * 512)
                            P.record('t', lambda e, o=pbt[h2].ap()[:, 0:nn],
                                     l=k2r.ap(j * RK + rc, j * RK + rc + 128),
                                     r=m2T.ap(j * D + h2 * 512,
                                              j * D + h2 * 512 + nn),
                                     st=(j == 0), sp=(j == n_hc - 1):
                                     e.matmul(o, l, r, start=st, stop=sp),
                                     reads=[f'{fid}_k2r{j}',
                                            f'{fid}m2Tq{j * nq2 // n_hc}'],
                                     writes=[pbr[h2]],
                                     accum=(fid, 2, seq, a, h2),
                                     label=f'{fid}_mm2_{seq}_{a}_{j}_{h2}')
                    fi, fires = ftm[a % 2], f'{fid}_ftm{a % 2}'
                    row2col(fring.ap(rc, rc + 128, p0=0, p1=1), 'fring',
                            fi.ap(), fires, 128)
                    for h2 in range(nhalf):
                        nn = min(512, D - h2 * 512)
                        P.record('v', lambda e, o=xsi.ap(h2 * 512, h2 * 512 + nn),
                                 i0=pbt[h2].ap()[:, 0:nn], f=fi.ap():
                                 e.scalar_tensor_tensor(o, i0, f, o, ALU.mult,
                                 ALU.add), reads=[pbr[h2], fires, xsres],
                                 writes=[xsres])
                    P.dma('s', out_ap[a_glob * 128:(a_glob + 1) * 128, :],
                          xsi.ap(), reads=[xsres], writes=[out_res],
                          label=f'{fid}_xst{a_glob}')
                done_tiles = ready
                gctr += 1
        for b in (m1T, m2T, g1row, carry, ygrp, maxacc, s2row, fring, k2r,
                  frowb):
            b.free()
        for lst in (xr, sq, stt_, k1tm, k1fm, g1bc, Bs, yab, s2bc, ftm):
            for b in lst:
                b.free()

    n_kc = S // 128
    n_kg = ceil_div(S, 512)
    nhalf = ceil_div(D, 512)

    class Acc:
        def __init__(self, total):
            self.total = total
            self.i = 0

        def nxt(self):
            st = self.i == 0
            sp = self.i == self.total - 1
            self.i += 1
            return st, sp

    def emit_wprep(w):
        wth = mem.bf16(f'a_{w}th', n_dc * D)
        wtl = mem.bf16(f'a_{w}tl', n_dc * D)
        stg = [mem.f32(f'a_{w}stg{i}', D) for i in range(2)]
        hib = [mem.bf16(f'a_{w}hi{i}', D) for i in range(2)]
        lob = [mem.bf16(f'a_{w}lo{i}', D) for i in range(2)]
        for dt in range(n_dc):
            sg, rs = stg[dt % 2], f'a_{w}stg{dt % 2}'
            hi, rh = hib[dt % 2], f'a_{w}hi{dt % 2}'
            lo, rl = lob[dt % 2], f'a_{w}lo{dt % 2}'
            P.dma('y', sg.ap(), d_wx[w][dt * 128:(dt + 1) * 128, :],
                  reads=[f'd_w{w}'], writes=[rs], label=f'{w}ld{dt}')
            P.record('v', lambda e, o=hi.ap(), i=sg.ap(): e.tensor_copy(o, i),
                     reads=[rs], writes=[rh])
            P.record('v', lambda e, o=lo.ap(), i0=sg.ap(), i1=hi.ap():
                     e.tensor_tensor(o, i0, i1, ALU.subtract),
                     reads=[rs, rh], writes=[rl])
            for src, dst, res_s, res_d in ((hi, wth, rh, f'a_{w}th'),
                                           (lo, wtl, rl, f'a_{w}tl')):
                nb = min(8, n_dc)
                for cb in range(0, n_dc, nb):
                    blks = [src.ap(ec * 128, (ec + 1) * 128)
                            for ec in range(cb, cb + nb)]
                    dst3 = dst.ap3(cb * D + dt * 128, D, nb, 128)
                    tr_tile(blks, res_s, dst3, res_d, nb, (w, dt, cb, res_d))
        P.dma('s', d_wT[w + 'h'], wth.ap(), reads=[f'a_{w}th'],
              writes=[f'd_wT{w}h'], label=f'{w}spillh')
        P.dma('s', d_wT[w + 'l'], wtl.ap(), reads=[f'a_{w}tl'],
              writes=[f'd_wT{w}l'], label=f'{w}spilll')
        for b in stg + hib + lob + [wth, wtl]:
            b.free()

    def ld_wt(w, bufs):
        for hl, b in zip('hl', bufs):
            P.dma('y', b.ap(), d_wT[w + hl], reads=[f'd_wT{w}{hl}'],
                  writes=[f'aw_{w}{hl}'], label=f'ld_wT{w}{hl}')

    def bias_bcast(w, buf, row):
        P.dma('y', row.ap(0, D, p0=0, p1=1), d_bx[w], reads=[f'd_b{w}'],
              writes=['a_brow'], label=f'ld_b{w}')
        for h2 in range(nhalf):
            nn = min(512, D - h2 * 512)
            bcast_row(row.ap(h2 * 512, h2 * 512 + nn, p0=0, p1=1),
                      'a_brow', buf.ap(h2 * 512, h2 * 512 + nn),
                      'a_bb', nn)

    def emit_ln_split(xi, xires, st_i, rS, sq_i, rQ, lnf, rLF,
                      lnhi, rLH, lnlo, rLL):
        cm = lambda c: st_i.ap(c, c + 1)
        P.record('v', lambda e, o=cm(0), x=xi.ap(): e.reduce_sum(
            o, x, AX.X), reads=[xires], writes=[rS + 'a'])
        P.record('v', lambda e, o=cm(1), i_=cm(0): e.tensor_scalar(
            o, i_, -1.0 / D, None, ALU.mult), reads=[rS + 'a'],
            writes=[rS + 'm'])
        P.record('s', lambda e, o=sq_i.ap(), x=xi.ap(), b=cm(1), a=cm(2):
                 e.activation(o, x, ACT.Square, bias=b, accum_out=a),
                 reads=[xires, rS + 'm'], writes=[rQ, rS + 'v'])
        P.record('v', lambda e, o=cm(3), i_=cm(2): e.tensor_scalar(
            o, i_, 1.0 / D, None, ALU.mult), reads=[rS + 'v'],
            writes=[rS + 'v2'])
        P.record('s', lambda e, o=cm(4), i_=cm(3), b=eps_b.ap():
                 e.activation(o, i_, ACT.Sqrt, bias=b),
                 reads=[rS + 'v2', 'epsb'], writes=[rS + 'sd'])
        P.record('v', lambda e, o=cm(5), i_=cm(4): e.reciprocal(o, i_),
                 reads=[rS + 'sd'], writes=[rS + 'r'])
        P.record('v', lambda e, o=lnf.ap(), x=xi.ap(), m=cm(1), r=cm(5):
                 e.tensor_scalar(o, x, m, r, ALU.add, ALU.mult),
                 reads=[xires, rS + 'm', rS + 'r'], writes=[rLF])
        P.record('v', lambda e, o=lnhi.ap(), i_=lnf.ap(): e.tensor_copy(o, i_),
                 reads=[rLF], writes=[rLH])
        P.record('v', lambda e, o=lnlo.ap(), i0=lnf.ap(), i1=lnhi.ap():
                 e.tensor_tensor(o, i0, i1, ALU.subtract),
                 reads=[rLF, rLH], writes=[rLL])

    def emit_attn_all():
        for w in 'qkvo':
            emit_wprep(w)
            P.barrier()
        Khi = mem.bf16('a_Khi', n_dc * S)
        Klo = mem.bf16('a_Klo', n_dc * S)
        Vhi = mem.bf16('a_Vhi', n_kc * D)
        Vlo = mem.bf16('a_Vlo', n_kc * D)
        bbuf = mem.f32('a_bb', D)
        brow = mem.f32('a_brow', D)
        bb = {w: bbuf for w in 'qkvo'}
        stb = [mem.f32(f'a_stt{i}', 24) for i in range(2)]
        for seq in range(NSEQ):
            wk = [mem.bf16('a_wkh', n_dc * D), mem.bf16('a_wkl', n_dc * D)]
            ld_wt('k', wk)
            bias_bcast('k', bb['k'], brow)
            xr_ = [mem.f32(f'a_xr{i}', D) for i in range(2)]
            sq_ = [mem.f32(f'a_sq{i}', D) for i in range(1)]
            lnf_ = [mem.f32(f'a_lnf{i}', D) for i in range(1)]
            lnh_ = [mem.bf16(f'a_lnh{i}', D) for i in range(2)]
            lnl_ = [mem.bf16(f'a_lnl{i}', D) for i in range(2)]
            lnhF = [mem.bf16(f'a_lnhF{i}', n_dc * 128) for i in range(2)]
            lnlF = [mem.bf16(f'a_lnlF{i}', n_dc * 128) for i in range(2)]
            ktm = mem.f32('a_ktm', D)
            khi_t = mem.bf16('a_khit', D)
            klo_t = mem.bf16('a_klot', D)
            for i in range(n_kc):
                a_glob = seq * ntile_seq + i
                ri = i % 2
                xi, xires = xr_[ri], f'a_xr{ri}'
                P.dma('y', xi.ap(), d_x1[a_glob * 128:(a_glob + 1) * 128, :],
                      reads=['x1d'], writes=[xires], label=f'a1x{a_glob}')
                emit_ln_split(xi, xires, stb[ri], f'a_stt{ri}',
                              sq_[0], 'a_sq0', lnf_[0],
                              'a_lnf0', lnh_[ri], f'a_lnh{ri}',
                              lnl_[ri], f'a_lnl{ri}')
                for src, dstb, rs, rd in ((lnh_[ri], lnhF[ri], f'a_lnh{ri}',
                                           f'a_lnhF{ri}'),
                                          (lnl_[ri], lnlF[ri], f'a_lnl{ri}',
                                           f'a_lnlF{ri}')):
                    blks = [src.ap(dc * 128, (dc + 1) * 128)
                            for dc in range(n_dc)]
                    tr_tile(blks, rs, dstb.ap3(0, 128, n_dc, 128), rd, n_dc,
                            ('lnF', seq, i, rd))
                for dl, src, rs in ((d_lnh, lnhF[ri], f'a_lnhF{ri}'),
                                    (d_lnl, lnlF[ri], f'a_lnlF{ri}')):
                    P.dma('s', dl[a_glob].rearrange('dc p c -> p dc c'),
                          src.ap().rearrange('p (dc c) -> p dc c', dc=n_dc),
                          reads=[rs], writes=['lnd'], label=f'lnst{a_glob}')
                accs = [Acc(3 * n_dc) for _ in range(nhalf)]
                for ec in range(n_dc):
                    for sihi, sires in ((lnhF[ri], f'a_lnhF{ri}'),
                                        (lnlF[ri], f'a_lnlF{ri}')):
                        pass
                for ec in range(n_dc):
                    lh = lnhF[ri].ap(ec * 128, (ec + 1) * 128)
                    ll = lnlF[ri].ap(ec * 128, (ec + 1) * 128)
                    for h2 in range(nhalf):
                        nn = min(512, D - h2 * 512)
                        bk2 = 2 * (i % 2) + h2
                        pk, pkr = psum_mm[bk2], f'pb{bk2}'
                        for stat, statr, mov, movr in (
                                (lh, f'a_lnhF{ri}', wk[0], 'aw_kh'),
                                (lh, f'a_lnhF{ri}', wk[1], 'aw_kl'),
                                (ll, f'a_lnlF{ri}', wk[0], 'aw_kh')):
                            st, sp = accs[h2].nxt()
                            P.record('t', lambda e, o=pk.ap()[:, 0:nn], l=stat,
                                     r=mov.ap(ec * D + h2 * 512,
                                              ec * D + h2 * 512 + nn),
                                     st=st, sp=sp: e.matmul(o, l, r, start=st,
                                     stop=sp), reads=[statr, movr],
                                     writes=[pkr], accum=('K', seq, i, h2),
                                     label=f'K_{seq}_{i}_{ec}_{h2}')
                for h2 in range(nhalf):
                    nn = min(512, D - h2 * 512)
                    bk2 = 2 * (i % 2) + h2
                    P.record('v', lambda e, o=ktm.ap(h2 * 512, h2 * 512 + nn),
                             i0=psum_mm[bk2].ap()[:, 0:nn],
                             i1=bb['k'].ap(h2 * 512, h2 * 512 + nn):
                             e.tensor_tensor(o, i0, i1, ALU.add),
                             reads=[f'pb{bk2}', 'a_bb'], writes=['a_ktm'])
                P.record('v', lambda e, o=khi_t.ap(), i=ktm.ap():
                         e.tensor_copy(o, i), reads=['a_ktm'],
                         writes=['a_khit'])
                P.record('v', lambda e, o=klo_t.ap(), i0=ktm.ap(),
                         i1=khi_t.ap(): e.tensor_tensor(o, i0, i1,
                         ALU.subtract), reads=['a_ktm', 'a_khit'],
                         writes=['a_klot'])
                for src, dstb, rs, rd in ((khi_t, Khi, 'a_khit', 'a_Khi'),
                                          (klo_t, Klo, 'a_klot', 'a_Klo')):
                    blks = [src.ap(dc * 128, (dc + 1) * 128)
                            for dc in range(n_dc)]
                    tr_tile(blks, rs, dstb.ap3(i * 128, S, n_dc, 128), rd,
                            n_dc, ('KT', seq, i, rd))
            for b in wk + xr_ + sq_ + lnf_ + lnh_ + lnl_:
                b.free()
            ktm.free(); khi_t.free(); klo_t.free()
            P.barrier()
            wv = [mem.bf16('a_wvh', n_dc * D), mem.bf16('a_wvl', n_dc * D)]
            ld_wt('v', wv)
            bias_bcast('v', bb['v'], brow)
            vtm = mem.f32('a_vtm', D)
            for i in range(n_kc):
                a_glob = seq * ntile_seq + i
                ri = i % 2
                for dl, dstb, rd in ((d_lnh, lnhF[ri], f'a_lnhF{ri}'),
                                     (d_lnl, lnlF[ri], f'a_lnlF{ri}')):
                    P.dma('y', dstb.ap().rearrange('p (dc c) -> p dc c',
                          dc=n_dc), dl[a_glob].rearrange('dc p c -> p dc c'),
                          reads=['lnd'], writes=[rd], label=f'lnld{a_glob}')
                accs = [Acc(3 * n_dc) for _ in range(nhalf)]
                for ec in range(n_dc):
                    lh = lnhF[ri].ap(ec * 128, (ec + 1) * 128)
                    ll = lnlF[ri].ap(ec * 128, (ec + 1) * 128)
                    for h2 in range(nhalf):
                        nn = min(512, D - h2 * 512)
                        bk2 = 2 * (i % 2) + h2
                        pk, pkr = psum_mm[bk2], f'pb{bk2}'
                        for stat, statr, mov, movr in (
                                (lh, f'a_lnhF{ri}', wv[0], 'aw_vh'),
                                (lh, f'a_lnhF{ri}', wv[1], 'aw_vl'),
                                (ll, f'a_lnlF{ri}', wv[0], 'aw_vh')):
                            st, sp = accs[h2].nxt()
                            P.record('t', lambda e, o=pk.ap()[:, 0:nn], l=stat,
                                     r=mov.ap(ec * D + h2 * 512,
                                              ec * D + h2 * 512 + nn),
                                     st=st, sp=sp: e.matmul(o, l, r, start=st,
                                     stop=sp), reads=[statr, movr],
                                     writes=[pkr], accum=('V', seq, i, h2),
                                     label=f'V_{seq}_{i}_{ec}_{h2}')
                for h2 in range(nhalf):
                    nn = min(512, D - h2 * 512)
                    bk2 = 2 * (i % 2) + h2
                    P.record('v', lambda e, o=vtm.ap(h2 * 512, h2 * 512 + nn),
                             i0=psum_mm[bk2].ap()[:, 0:nn],
                             i1=bb['v'].ap(h2 * 512, h2 * 512 + nn):
                             e.tensor_tensor(o, i0, i1, ALU.add),
                             reads=[f'pb{bk2}', 'a_bb'], writes=['a_vtm'])
                P.record('v', lambda e, o=Vhi.ap(i * D, (i + 1) * D),
                         i=vtm.ap(): e.tensor_copy(o, i), reads=['a_vtm'],
                         writes=['a_Vhi'])
                P.record('v', lambda e, o=Vlo.ap(i * D, (i + 1) * D),
                         i0=vtm.ap(), i1=Vhi.ap(i * D, (i + 1) * D):
                         e.tensor_tensor(o, i0, i1, ALU.subtract),
                         reads=['a_vtm', 'a_Vhi'], writes=['a_Vlo'])
            for b in wv:
                b.free()
            vtm.free()
            P.barrier()
            wq = [mem.bf16('a_wqh', n_dc * D), mem.bf16('a_wql', n_dc * D)]
            ld_wt('q', wq)
            bias_bcast('q', bb['q'], brow)
            qtm = mem.f32('a_qtm', D)
            qhi_t = mem.bf16('a_qhit', D)
            qlo_t = mem.bf16('a_qlot', D)
            qF = [mem.bf16(f'a_qF{i}', n_dc * 128) for i in range(2)]
            for i in range(n_kc):
                a_glob = seq * ntile_seq + i
                ri = i % 2
                for dl, dstb, rd in ((d_lnh, lnhF[ri], f'a_lnhF{ri}'),
                                     (d_lnl, lnlF[ri], f'a_lnlF{ri}')):
                    P.dma('y', dstb.ap().rearrange('p (dc c) -> p dc c',
                          dc=n_dc), dl[a_glob].rearrange('dc p c -> p dc c'),
                          reads=['lnd'], writes=[rd], label=f'lnld2{a_glob}')
                accs = [Acc(3 * n_dc) for _ in range(nhalf)]
                for ec in range(n_dc):
                    lh = lnhF[ri].ap(ec * 128, (ec + 1) * 128)
                    ll = lnlF[ri].ap(ec * 128, (ec + 1) * 128)
                    for h2 in range(nhalf):
                        nn = min(512, D - h2 * 512)
                        bk2 = 2 * (i % 2) + h2
                        pk, pkr = psum_mm[bk2], f'pb{bk2}'
                        for stat, statr, mov, movr in (
                                (lh, f'a_lnhF{ri}', wq[0], 'aw_qh'),
                                (lh, f'a_lnhF{ri}', wq[1], 'aw_ql'),
                                (ll, f'a_lnlF{ri}', wq[0], 'aw_qh')):
                            st, sp = accs[h2].nxt()
                            P.record('t', lambda e, o=pk.ap()[:, 0:nn], l=stat,
                                     r=mov.ap(ec * D + h2 * 512,
                                              ec * D + h2 * 512 + nn),
                                     st=st, sp=sp: e.matmul(o, l, r, start=st,
                                     stop=sp), reads=[statr, movr],
                                     writes=[pkr], accum=('Q', seq, i, h2),
                                     label=f'Q_{seq}_{i}_{ec}_{h2}')
                for h2 in range(nhalf):
                    nn = min(512, D - h2 * 512)
                    bk2 = 2 * (i % 2) + h2
                    P.record('v', lambda e, o=qtm.ap(h2 * 512, h2 * 512 + nn),
                             i0=psum_mm[bk2].ap()[:, 0:nn],
                             i1=bb['q'].ap(h2 * 512, h2 * 512 + nn):
                             e.tensor_tensor(o, i0, i1, ALU.add),
                             reads=[f'pb{bk2}', 'a_bb'], writes=['a_qtm'])
                P.record('v', lambda e, o=qhi_t.ap(), i=qtm.ap():
                         e.tensor_copy(o, i), reads=['a_qtm'],
                         writes=['a_qhit'])
                P.record('v', lambda e, o=qlo_t.ap(), i0=qtm.ap(),
                         i1=qhi_t.ap(): e.tensor_tensor(o, i0, i1,
                         ALU.subtract), reads=['a_qtm', 'a_qhit'],
                         writes=['a_qlot'])
                for src, rs, dl in ((qhi_t, 'a_qhit', d_qh),
                                    (qlo_t, 'a_qlot', d_ql)):
                    qf, qfres = qF[ri], f'a_qF{ri}'
                    blks = [src.ap(dc * 128, (dc + 1) * 128)
                            for dc in range(n_dc)]
                    tr_tile(blks, rs, qf.ap3(0, 128, n_dc, 128), qfres, n_dc,
                            ('QT', seq, i, rs))
                    P.dma('s', dl[a_glob].rearrange('dc p c -> p dc c'),
                          qf.ap().rearrange('p (dc c) -> p dc c', dc=n_dc),
                          reads=[qfres], writes=['qd'], label=f'qst{a_glob}')
            for b in wq + qF + lnhF + lnlF:
                b.free()
            qtm.free(); qhi_t.free(); qlo_t.free()
            P.barrier()
            wo = [mem.bf16('a_woh', n_dc * D), mem.bf16('a_wol', n_dc * D)]
            ld_wt('o', wo)
            bias_bcast('o', bb['o'], brow)
            qh_ = [mem.bf16(f'a_q3h{i}', n_dc * 128) for i in range(1)]
            ql_ = [mem.bf16(f'a_q3l{i}', n_dc * 128) for i in range(1)]
            Pu = [mem.f32(f'a_Pu{i}', S) for i in range(2)]
            Phi = mem.bf16('a_Phi', S)
            Plo = mem.bf16('a_Plo', S)
            PhiT = mem.bf16('a_PhiT', n_kc * 128)
            PloT = mem.bf16('a_PloT', n_kc * 128)
            athi = mem.bf16('a_athi', D)
            atlo = mem.bf16('a_atlo', D)
            atFh = mem.bf16('a_atFh', n_dc * 128)
            atFl = mem.bf16('a_atFl', n_dc * 128)
            x1r = [mem.f32(f'a_x1r{i}', D) for i in range(2)]
            sc = 1.0 / math.sqrt(D)

            def st_scores(i):
                ri = i % 2
                a_glob = seq * ntile_seq + i
                for dl, dstb, rd in ((d_qh, qh_[0], 'a_q3h0'),
                                     (d_ql, ql_[0], 'a_q3l0')):
                    P.dma('y', dstb.ap().rearrange('p (dc c) -> p dc c',
                          dc=n_dc), dl[a_glob].rearrange('dc p c -> p dc c'),
                          reads=['qd'], writes=[rd], label=f'qld{a_glob}')
                P.dma('y', x1r[ri].ap(),
                      d_x1[a_glob * 128:(a_glob + 1) * 128, :],
                      reads=['x1d'], writes=[f'a_x1r{ri}'],
                      label=f'a3x{a_glob}')
                cmb = lambda c: stb[i % 2].ap(c, c + 1)
                rst = f'a_st{i % 2}'
                for g in range(n_kg):
                    nn = min(512, S - g * 512)
                    pq, pqr = psum_mm[g], f'pb{g}'
                    qacc = Acc(3 * n_dc)
                    for dc in range(n_dc):
                        qh = qh_[0].ap(dc * 128, (dc + 1) * 128)
                        ql = ql_[0].ap(dc * 128, (dc + 1) * 128)
                        for stat, statr, mov, movr in (
                                (qh, 'a_q3h0', Khi, 'a_Khi'),
                                (qh, 'a_q3h0', Klo, 'a_Klo'),
                                (ql, 'a_q3l0', Khi, 'a_Khi')):
                            st, sp = qacc.nxt()
                            P.record('t', lambda e, o=pq.ap()[:, 0:nn], l=stat,
                                     r=mov.ap(dc * S + g * 512,
                                              dc * S + g * 512 + nn),
                                     st=st, sp=sp: e.matmul(o, l, r, start=st,
                                     stop=sp), reads=[statr, movr],
                                     writes=[pqr], accum=('sc', seq, i, g),
                                     label=f'sc_{seq}_{i}_{dc}_{g}')
                    P.record('v', lambda e, o=cmb(g),
                             i_=pq.ap()[:, 0:nn]: e.reduce_max(
                             o, i_, AX.X), reads=[pqr], writes=[rst + f'm{g}'])
                    P.record('v', lambda e, o=cmb(4 + g), i_=cmb(g):
                             e.tensor_scalar(o, i_, -sc, None, ALU.mult),
                             reads=[rst + f'm{g}'], writes=[rst + f'b{g}'])
                    P.record('s', lambda e, o=Pu[i % 2].ap(g * 512,
                             g * 512 + nn), i_=pq.ap()[:, 0:nn], b=cmb(4 + g),
                             a=cmb(8 + g): e.activation(
                             o, i_, ACT.Exp, bias=b, scale=sc, accum_out=a),
                             reads=[pqr, rst + f'b{g}'],
                             writes=[f'a_Pu{i % 2}', rst + f's{g}'])

            def st_rest(i):
                ri = i % 2
                a_glob = seq * ntile_seq + i
                cmb = lambda c: stb[i % 2].ap(c, c + 1)
                rst = f'a_st{i % 2}'
                P.record('v', lambda e, o=cmb(12), i_=stb[i % 2].ap(0, n_kg):
                         e.reduce_max(o, i_, AX.X),
                         reads=[rst + f'm{g2}' for g2 in range(n_kg)],
                         writes=[rst + 'gm'])
                for g in range(n_kg):
                    P.record('v', lambda e, o=cmb(16 + g), i_=cmb(g),
                             m=cmb(12): e.tensor_scalar(o, i_, m, None,
                             ALU.subtract),
                             reads=[rst + f'm{g}', rst + 'gm'],
                             writes=[rst + f'd{g}'])
                P.record('s', lambda e, o=stb[i % 2].ap(16, 16 + n_kg),
                         i_=stb[i % 2].ap(16, 16 + n_kg): e.activation(
                         o, i_, ACT.Exp, scale=sc),
                         reads=[rst + f'd{g2}' for g2 in range(n_kg)],
                         writes=[rst + 'f'])
                P.record('v', lambda e, o=stb[i % 2].ap(8, 8 + n_kg),
                         i0=stb[i % 2].ap(8, 8 + n_kg),
                         i1=stb[i % 2].ap(16, 16 + n_kg): e.tensor_tensor(
                         o, i0, i1, ALU.mult),
                         reads=[rst + 'f'] + [rst + f's{g2}' for g2 in
                                              range(n_kg)],
                         writes=[rst + 'ws'])
                P.record('v', lambda e, o=cmb(13), i_=stb[i % 2].ap(8, 8 + n_kg):
                         e.reduce_sum(o, i_, AX.X), reads=[rst + 'ws'],
                         writes=[rst + 'tot'])
                P.record('v', lambda e, o=cmb(14), i_=cmb(13): e.reciprocal(
                    o, i_), reads=[rst + 'tot'], writes=[rst + 'r'])
                P.record('v', lambda e, o=stb[i % 2].ap(16, 16 + n_kg),
                         i0=stb[i % 2].ap(16, 16 + n_kg), s1=cmb(14):
                         e.tensor_scalar(o, i0, s1, None, ALU.mult),
                         reads=[rst + 'f', rst + 'r'], writes=[rst + 'rf'])
                for g in range(n_kg):
                    nn = min(512, S - g * 512)
                    P.record('v', lambda e, o=Pu[i % 2].ap(g * 512,
                             g * 512 + nn), s1=cmb(16 + g): e.tensor_scalar(
                             o, o, s1, None, ALU.mult),
                             reads=[f'a_Pu{i % 2}', rst + 'rf'],
                             writes=[f'a_Pu{i % 2}'])
                P.record('v', lambda e, o=Phi.ap(), i=Pu[i % 2].ap():
                         e.tensor_copy(o, i), reads=[f'a_Pu{i % 2}'],
                         writes=['a_Phi'])
                P.record('v', lambda e, o=Plo.ap(), i0=Pu[i % 2].ap(),
                         i1=Phi.ap(): e.tensor_tensor(o, i0, i1, ALU.subtract),
                         reads=[f'a_Pu{i % 2}', 'a_Phi'], writes=['a_Plo'])
                for src, rs, dstb, rd in ((Phi, 'a_Phi', PhiT, 'a_PhiT'),
                                          (Plo, 'a_Plo', PloT, 'a_PloT')):
                    nb = 8 if n_kc >= 8 else n_kc
                    for cb in range(0, n_kc, nb):
                        blks = [src.ap(kc * 128, (kc + 1) * 128)
                                for kc in range(cb, cb + nb)]
                        tr_tile(blks, rs, dstb.ap3(cb * 128, 128, nb, 128),
                                rd, nb, ('PT', seq, i, cb, rd))
                vaccs = [Acc(3 * n_kc) for _ in range(nhalf)]
                for kc in range(n_kc):
                    ph = PhiT.ap(kc * 128, (kc + 1) * 128)
                    pl = PloT.ap(kc * 128, (kc + 1) * 128)
                    for h2 in range(nhalf):
                        nn = min(512, D - h2 * 512)
                        pv = psum_mm[4 + h2] if nhalf > 1 else psum_mm[4]
                        pvr = f'pb{4 + h2}' if nhalf > 1 else 'pb4'
                        for stat, statr, mov, movr in (
                                (ph, 'a_PhiT', Vhi, 'a_Vhi'),
                                (ph, 'a_PhiT', Vlo, 'a_Vlo'),
                                (pl, 'a_PloT', Vhi, 'a_Vhi')):
                            st, sp = vaccs[h2].nxt()
                            P.record('t', lambda e, o=pv.ap()[:, 0:nn], l=stat,
                                     r=mov.ap(kc * D + h2 * 512,
                                              kc * D + h2 * 512 + nn),
                                     st=st, sp=sp: e.matmul(o, l, r, start=st,
                                     stop=sp), reads=[statr, movr],
                                     writes=[pvr], accum=('pv', seq, i, h2),
                                     label=f'pv_{seq}_{i}_{kc}_{h2}')
                for h2 in range(nhalf):
                    nn = min(512, D - h2 * 512)
                    pv = psum_mm[4 + h2] if nhalf > 1 else psum_mm[4]
                    pvr = f'pb{4 + h2}' if nhalf > 1 else 'pb4'
                    P.record('v', lambda e, o=athi.ap(h2 * 512, h2 * 512 + nn),
                             i=pv.ap()[:, 0:nn]: e.tensor_copy(o, i),
                             reads=[pvr], writes=['a_athi'])
                    P.record('v', lambda e, o=atlo.ap(h2 * 512, h2 * 512 + nn),
                             i0=pv.ap()[:, 0:nn],
                             i1=athi.ap(h2 * 512, h2 * 512 + nn):
                             e.tensor_tensor(o, i0, i1, ALU.subtract),
                             reads=[pvr, 'a_athi'], writes=['a_atlo'])
                for src, rs, dstb, rd in ((athi, 'a_athi', atFh, 'a_atFh'),
                                          (atlo, 'a_atlo', atFl, 'a_atFl')):
                    blks = [src.ap(dc * 128, (dc + 1) * 128)
                            for dc in range(n_dc)]
                    tr_tile(blks, rs, dstb.ap3(0, 128, n_dc, 128), rd, n_dc,
                            ('AT', seq, i, rd))
                oaccs = [Acc(3 * n_dc) for _ in range(nhalf)]
                for dv in range(n_dc):
                    ah = atFh.ap(dv * 128, (dv + 1) * 128)
                    al = atFl.ap(dv * 128, (dv + 1) * 128)
                    for h2 in range(nhalf):
                        nn = min(512, D - h2 * 512)
                        po = psum_mm[4 + h2] if nhalf > 1 else psum_mm[4]
                        por = f'pb{4 + h2}' if nhalf > 1 else 'pb4'
                        for stat, statr, mov, movr in (
                                (ah, 'a_atFh', wo[0], 'aw_oh'),
                                (ah, 'a_atFh', wo[1], 'aw_ol'),
                                (al, 'a_atFl', wo[0], 'aw_oh')):
                            st, sp = oaccs[h2].nxt()
                            P.record('t', lambda e, o=po.ap()[:, 0:nn], l=stat,
                                     r=mov.ap(dv * D + h2 * 512,
                                              dv * D + h2 * 512 + nn),
                                     st=st, sp=sp: e.matmul(o, l, r, start=st,
                                     stop=sp), reads=[statr, movr],
                                     writes=[por], accum=('O', seq, i, h2),
                                     label=f'O_{seq}_{i}_{dv}_{h2}')
                for h2 in range(nhalf):
                    nn = min(512, D - h2 * 512)
                    po = psum_mm[4 + h2] if nhalf > 1 else psum_mm[4]
                    por = f'pb{4 + h2}' if nhalf > 1 else 'pb4'
                    P.record('v', lambda e, o=x1r[ri].ap(h2 * 512,
                             h2 * 512 + nn), i0=po.ap()[:, 0:nn],
                             i1=x1r[ri].ap(h2 * 512, h2 * 512 + nn):
                             e.tensor_tensor(o, i0, i1, ALU.add),
                             reads=[por, f'a_x1r{ri}'],
                             writes=[f'a_x1r{ri}'])
                    P.record('v', lambda e, o=x1r[ri].ap(h2 * 512,
                             h2 * 512 + nn), i1=bb['o'].ap(h2 * 512,
                             h2 * 512 + nn): e.tensor_tensor(o, o, i1,
                             ALU.add), reads=['a_bb', f'a_x1r{ri}'],
                             writes=[f'a_x1r{ri}'])
                P.dma('s', d_x2[a_glob * 128:(a_glob + 1) * 128, :],
                      x1r[ri].ap(), reads=[f'a_x1r{ri}'], writes=['x2d'],
                      label=f'x2st{a_glob}')

            st_scores(0)
            for i in range(n_kc):
                if i + 1 < n_kc:
                    st_scores(i + 1)
                st_rest(i)
            for b in wo + qh_ + ql_ + x1r:
                b.free()
            for b in Pu + [Phi, Plo, PhiT, PloT, athi, atlo, atFh, atFl]:
                b.free()
            P.barrier()
        for b in [Khi, Klo, Vhi, Vlo, bbuf, brow] + stb:
            b.free()

    emit_wquant(d_w1, H, D, d_m1T, 'd_w1mT', cdesc1, 'w1_cdesc', 'w1')
    P.barrier()
    emit_wquant(d_w2, D, H, d_m2T, 'd_w2mT', cdesc2, 'w2_cdesc', 'w2')
    P.barrier()
    if PH == 'f1':
        emit_ffn(d_x, d_out, 'x_in', 'outd', 'F1')
    else:
        emit_ffn(d_x, d_x1, 'x_in', 'x1d', 'F1')
        P.barrier()
        emit_attn_all()
        P.barrier()
        emit_ffn(d_x2, d_out, 'x2d', 'outd', 'F2')
    P.record('y', lambda e: None, reads=['outd'], writes=[], label='final')

    from contextlib import ExitStack
    with ExitStack() as stack:
        P.materialize(stack)
    nstats = {e: len(P.ops[e]) for e in ENGINES}
    return nc, nstats


_NCORE = 8


def _mk_inputs_common(Wq, bq, Wk, bk, Wv, bv, Wo, bo, w1, conv_w, conv_b, w2):
    f32 = np.float32
    cw = np.ascontiguousarray(np.asarray(conv_w, dtype=f32)).reshape(-1, 3)
    com = {
        'w1': np.ascontiguousarray(np.asarray(w1, dtype=f32)),
        'w2': np.ascontiguousarray(np.asarray(w2, dtype=f32)),
        'cw0': np.ascontiguousarray(cw[:, 0].reshape(-1, 128).T),
        'cw1': np.ascontiguousarray(cw[:, 1].reshape(-1, 128).T),
        'cw2': np.ascontiguousarray(cw[:, 2].reshape(-1, 128).T),
        'cb': np.ascontiguousarray(
            np.asarray(conv_b, dtype=f32).reshape(-1, 128).T),
        'wq': np.ascontiguousarray(np.asarray(Wq, dtype=f32)),
        'wk': np.ascontiguousarray(np.asarray(Wk, dtype=f32)),
        'wv': np.ascontiguousarray(np.asarray(Wv, dtype=f32)),
        'wo': np.ascontiguousarray(np.asarray(Wo, dtype=f32)),
        'bq': np.ascontiguousarray(np.asarray(bq, dtype=f32).reshape(1, -1)),
        'bk': np.ascontiguousarray(np.asarray(bk, dtype=f32).reshape(1, -1)),
        'bv': np.ascontiguousarray(np.asarray(bv, dtype=f32).reshape(1, -1)),
        'bo': np.ascontiguousarray(np.asarray(bo, dtype=f32).reshape(1, -1)),
        'ident': np.eye(128, dtype=ml_dtypes.bfloat16),
    }
    return com


def kernel(L0_attn_x, L0_attn_Wq, L0_attn_bq, L0_attn_Wk, L0_attn_bk,
           L0_attn_Wv, L0_attn_bv, L0_attn_Wo, L0_attn_bo, L0_ffn_w1,
           L0_ffn_conv_w, L0_ffn_conv_b, L0_ffn_w2):
    x = np.ascontiguousarray(np.asarray(L0_attn_x, dtype=np.float32))
    B, S, D = x.shape
    nper = B // _NCORE
    cfg = dict(S=S, D=D, H=L0_ffn_w1.shape[0], NSEQ=nper, TF=256)
    nc, _ = build_core(cfg)
    com = _mk_inputs_common(L0_attn_Wq, L0_attn_bq, L0_attn_Wk, L0_attn_bk,
                            L0_attn_Wv, L0_attn_bv, L0_attn_Wo, L0_attn_bo,
                            L0_ffn_w1, L0_ffn_conv_w, L0_ffn_conv_b, L0_ffn_w2)
    in_maps = []
    for i in range(_NCORE):
        m = dict(com)
        m['x_in'] = np.ascontiguousarray(
            x[i * nper:(i + 1) * nper].reshape(nper * S, D))
        in_maps.append(m)
    res = run_bass_kernel_spmd(nc, in_maps, core_ids=list(range(_NCORE)))
    out = np.concatenate(
        [np.asarray(r['out']).reshape(nper, S, D) for r in res.results], axis=0)
    return out.astype(np.float32)


# version 06
# speedup vs baseline: 1.1161x; 1.1161x over previous
import math
import numpy as np
import ml_dtypes
import concourse.bass as bass
import concourse.mybir as mybir
from concourse.bass_utils import run_bass_kernel_spmd

F32 = mybir.dt.float32
BF16 = mybir.dt.bfloat16
ALU = mybir.AluOpType
ACT = mybir.ActivationFunctionType
AX = mybir.AxisListType

M1 = 12582912.0
M2 = 6755399441055744.0

ENGINES = ('t', 'v', 's', 'g', 'y')


class Op:
    __slots__ = ('eng', 'fn', 'reads', 'writes', 'idx', 'is_dma', 'tag',
                 'seq', 'signals', 'waits', 'drain_before', 'accum', 'label',
                 '_sigvals', 'clock')

    def __init__(self, eng, fn, reads, writes, is_dma, tag, seq, accum, label):
        self.eng = eng
        self.fn = fn
        self.reads = reads
        self.writes = writes
        self.is_dma = is_dma
        self.tag = tag
        self.seq = seq
        self.accum = accum
        self.label = label
        self.idx = -1
        self.signals = []
        self.waits = []
        self.drain_before = False
        self.clock = {}


class Prog:
    def __init__(self, nc):
        self.nc = nc
        self.ops = {e: [] for e in ENGINES}
        self.all_ops = []
        self.last_write = {}
        self.readers = {}
        self.dma_count = {}
        self.dma_written_res = {}
        self.barrier_count = 0
        self.pair_used = set()
        self.tag_used = set()
        self.gidx = {}
        self.pair_water = {}
        self.dma_water = {}
        self.drain_cover = {e: -1 for e in ENGINES}

    def _collect(self, targets, op):
        eprods = []
        dprods = []
        for t in targets:
            if t is None or t[0] == 'B':
                continue
            if t[0] == 'E':
                eprods.append(t[1])
            else:
                dprods.append(t)
        eprods.sort(key=lambda p: -self.gidx[id(p)])
        ck = op.clock
        for prod in eprods:
            if prod.eng == op.eng and op.eng == 't':
                continue
            if ck.get(prod.eng, -1) >= prod.idx:
                continue
            key = (prod.eng, op.eng)
            self.pair_used.add(key)
            prod.signals.append(key)
            op.waits.append(('P', key, prod))
            for k, v in prod.clock.items():
                if ck.get(k, -1) < v:
                    ck[k] = v
            if ck.get(prod.eng, -1) < prod.idx:
                ck[prod.eng] = prod.idx
        for (_, tag, seq, dop) in dprods:
            if ck.get(('D', tag), 0) >= seq:
                continue
            self.tag_used.add(tag)
            op.waits.append(('D', tag, 16 * seq))
            ck[('D', tag)] = seq
            if dop is not None:
                for k, v in dop.clock.items():
                    if ck.get(k, -1) < v:
                        ck[k] = v

    def record(self, eng, fn, reads=(), writes=(), is_dma=False, dma_res=None,
               accum=None, label=''):
        reads = list(reads)
        writes = list(writes)
        tag = seq = None
        if is_dma:
            assert dma_res is not None
            tag = dma_res
            seq = self.dma_count.get(tag, 0) + 1
            self.dma_count[tag] = seq
        op = Op(eng, fn, reads, writes, is_dma, tag, seq, accum, label)
        op.idx = len(self.ops[eng])
        self.gidx[id(op)] = len(self.all_ops)
        targets = []
        for r in reads:
            targets.append(self.last_write.get(r))
        for r in writes:
            lw = self.last_write.get(r)
            if lw is not None and not (accum is not None and lw[0] == 'E'
                                       and lw[1].accum == accum):
                targets.append(lw)
            for rd in self.readers.get(r, ()):
                targets.append(rd)
        self._collect(targets, op)
        self.ops[eng].append(op)
        self.all_ops.append(op)
        me = ('D', tag, seq, op) if is_dma else ('E', op)
        for r in reads:
            self.readers.setdefault(r, []).append(me)
        for r in writes:
            self.last_write[r] = me
            self.readers[r] = []
        return op

    def dma(self, queue_eng, out_ap, in_ap, reads, writes, label=''):
        def fn(eng, _o=out_ap, _i=in_ap):
            return eng.dma_start(out=_o, in_=_i)
        return self.record(queue_eng, fn, reads=reads, writes=writes,
                           is_dma=True, dma_res=writes[0], label=label)

    def barrier(self):
        self.barrier_count += 1
        k = self.barrier_count
        pending = [(tag, 16 * cnt) for tag, cnt in self.dma_count.items()]
        for tag, val in pending:
            self.tag_used.add(tag)

        def mk(eng_name):
            def fn(eng, _k=k, _pending=pending, _e=eng_name):
                if _e == 'y':
                    for tag, val in _pending:
                        if self.dma_water2.get(('y', tag), 0) < val:
                            eng.wait_ge(self._tag_sem[tag], val)
                            self.dma_water2[('y', tag)] = val
                eng.drain()
                eng.nop().then_inc(self._bar_sem, 1)
                eng.wait_ge(self._bar_sem, 5 * _k)
                return None
            return fn
        for e in ENGINES:
            op = Op(e, mk(e), [], [], False, None, None, None, f'barrier{k}')
            op.idx = len(self.ops[e])
            self.ops[e].append(op)
            self.all_ops.append(op)
        for r in self.last_write:
            self.last_write[r] = ('B',)
        self.readers = {}
        for e in ENGINES:
            self.drain_cover[e] = len(self.ops[e]) - 1

    def materialize(self, stack):
        nc = self.nc
        self._bar_sem = stack.enter_context(nc.semaphore('bar'))
        self._pair_sem = {}
        for key in sorted(self.pair_used):
            self._pair_sem[key] = stack.enter_context(
                nc.semaphore(f'p_{key[0]}{key[1]}'))
        self._tag_sem = {}
        for tag in sorted(self.tag_used):
            self._tag_sem[tag] = stack.enter_context(nc.semaphore(f'd_{tag}'))
        pair_counter = {k: 0 for k in self.pair_used}
        for e in ENGINES:
            for op in self.ops[e]:
                sig = sorted(set(op.signals))
                op.signals = sig
                op._sigvals = {}
                for key in sig:
                    pair_counter[key] += 1
                    op._sigvals[key] = pair_counter[key]
        blk = self.nc.Block()
        self._block = blk

        def emit_engine(eng_handle, ename):
            for op in self.ops[ename]:
                for w in op.waits:
                    if w[0] == 'P':
                        _, key, prod = w
                        val = prod._sigvals[key]
                        if self.pair_water2.get((ename, key), 0) < val:
                            eng_handle.wait_ge(self._pair_sem[key], val)
                            self.pair_water2[(ename, key)] = val
                    else:
                        _, tag, val = w
                        if self.dma_water2.get((ename, tag), 0) < val:
                            eng_handle.wait_ge(self._tag_sem[tag], val)
                            self.dma_water2[(ename, tag)] = val
                if op.drain_before:
                    eng_handle.drain()
                inst = op.fn(eng_handle)
                if op.is_dma:
                    assert inst is not None, f'dma op {op.label} returned None'
                    inst.then_inc(self._tag_sem[op.tag], 16)
                    if op.signals:
                        for key in op.signals:
                            eng_handle.nop().then_inc(self._pair_sem[key], 1)
                elif op.signals:
                    if inst is not None:
                        inst.then_inc(self._pair_sem[op.signals[0]], 1)
                        rest = op.signals[1:]
                    else:
                        rest = op.signals
                    if rest:
                        eng_handle.drain()
                        for key in rest:
                            eng_handle.nop().then_inc(self._pair_sem[key], 1)

        self.pair_water2 = {}
        self.dma_water2 = {}
        with blk as block:
            block.tensor(lambda e: emit_engine(e, 't'))
            block.vector(lambda e: emit_engine(e, 'v'))
            block.scalar(lambda e: emit_engine(e, 's'))
            block.gpsimd(lambda e: emit_engine(e, 'g'))
            block.sync(lambda e: emit_engine(e, 'y'))


class Buf:
    __slots__ = ('mem', 'byte_start', 'cols', 'name', 'dtype', 'freed', 't')

    def __init__(self, mem, byte_start, cols, name, dtype):
        self.mem = mem
        self.byte_start = byte_start
        self.cols = cols
        self.name = name
        self.dtype = dtype
        self.freed = False
        mem.uid += 1
        self.t = mem.nc.alloc_sbuf_tensor_at(
            f'c{mem.uid}_{name}', [128, cols], dtype,
            offset=mem.addr + byte_start)

    def ap(self, c0=0, c1=None, p0=0, p1=128):
        assert not self.freed, self.name
        if c1 is None:
            c1 = self.cols
        assert 0 <= c0 <= c1 <= self.cols, (self.name, c0, c1, self.cols)
        return self.t.ap()[p0:p1, c0:c1]

    def ap3(self, start, stride, nblk, width, p0=0, p1=128):
        assert not self.freed, self.name
        assert start + (nblk - 1) * stride + width <= self.cols
        assert p0 == 0 and p1 == 128
        import dataclasses
        full = self.t.ap()
        row_stride = full.ap[0][0]
        return dataclasses.replace(
            full, offset=full.offset + start,
            ap=type(full.ap)([[row_stride, 128], [stride, nblk], [1, width]]))

    def free(self):
        assert not self.freed
        self.mem._free(self.byte_start, self.cols * (4 if self.dtype == F32 else 2))
        self.freed = True


class Mem:
    def __init__(self, nc, nbytes):
        self.nc = nc
        self.nbytes = nbytes
        self.uid = 0
        base = nc.alloc_sbuf_tensor('arena8', [128, nbytes], mybir.dt.uint8)
        self.addr = nc.lookup_mloc(base).addr
        assert self.addr % 64 == 0, self.addr
        self.free_list = [(0, nbytes)]

    def _alloc(self, n, align=64):
        for i, (st, sz) in enumerate(self.free_list):
            a = (st + align - 1) // align * align
            if a + n <= st + sz:
                repl = [p for p in ((st, a - st), (a + n, st + sz - a - n))
                        if p[1] > 0]
                self.free_list[i:i + 1] = repl
                return a
        used = self.nbytes - sum(sz for _, sz in self.free_list)
        raise MemoryError(f'arena: cannot alloc {n}B (used={used}B, '
                          f'free={self.free_list[:6]})')

    def _free(self, start, n):
        self.free_list.append((start, n))
        self.free_list.sort()
        merged = []
        for st, sz in self.free_list:
            if merged and merged[-1][0] + merged[-1][1] == st:
                merged[-1] = (merged[-1][0], merged[-1][1] + sz)
            else:
                merged.append((st, sz))
        self.free_list = merged

    def f32(self, name, cols):
        return Buf(self, self._alloc(cols * 4, 64 if cols > 8 else 32),
                   cols, name, F32)

    def bf16(self, name, cols):
        return Buf(self, self._alloc(cols * 2, 64 if cols > 16 else 32),
                   cols, name, BF16)


def ceil_div(a, b):
    return (a + b - 1) // b




def ceil_div(a, b):
    return (a + b - 1) // b


def _ring_pieces(g0, length, RK):
    out = []
    g = g0
    rem = length
    while rem > 0:
        r = g % RK
        ln = min(rem, RK - r)
        out.append((g, r, ln))
        g += ln
        rem -= ln
    return out


def _chunk_rows(d_ap, nchunk):
    return d_ap.rearrange('o (j p) -> (o p) j', p=128)


def build_core(cfg):
    S = cfg['S']
    D = cfg['D']
    H = cfg['H']
    NSEQ = cfg['NSEQ']
    TF = cfg['TF']
    NT = S * NSEQ
    n_dc = D // 128
    n_hc = H // 128
    TPG = TF // 128
    NHG = S // TF
    ntile_seq = S // 128
    ntiles = NT // 128
    RK = TF + 128
    PH = cfg.get('PHASES', 'full')

    nc = bass.Bass('TRN2', target_bir_lowering=False,
                   dynamic_dma_scratch_size=512)

    d_x = nc.dram_tensor('x_in', [NT, D], F32, kind='ExternalInput').ap()
    d_w1 = nc.dram_tensor('w1', [H, D], F32, kind='ExternalInput').ap()
    d_w2 = nc.dram_tensor('w2', [D, H], F32, kind='ExternalInput').ap()
    d_cw = [nc.dram_tensor(f'cw{i}', [128, H // 128], F32,
                           kind='ExternalInput').ap() for i in range(3)]
    d_cb = nc.dram_tensor('cb', [128, H // 128], F32,
                          kind='ExternalInput').ap()
    d_wx = {w: nc.dram_tensor('w' + w, [D, D], F32, kind='ExternalInput').ap()
            for w in 'qkvo'}
    d_bx = {w: nc.dram_tensor('b' + w, [1, D], F32, kind='ExternalInput').ap()
            for w in 'qkvo'}
    d_ident = nc.dram_tensor('ident', [128, 128], BF16,
                             kind='ExternalInput').ap()
    d_out = nc.dram_tensor('out', [NT, D], F32, kind='ExternalOutput').ap()

    d_x1 = nc.dram_tensor('x1d', [NT, D], F32).ap()
    d_x2 = nc.dram_tensor('x2d', [NT, D], F32).ap()
    d_m1T = nc.dram_tensor('m1Td', [128, n_dc * H], BF16).ap()
    d_m2T = nc.dram_tensor('m2Td', [128, n_hc * D], BF16).ap()
    d_lnh = nc.dram_tensor('lnhd', [ntiles, n_dc, 128, 128], BF16).ap()
    d_lnl = nc.dram_tensor('lnld', [ntiles, n_dc, 128, 128], BF16).ap()
    d_qh = nc.dram_tensor('qhd', [ntiles, n_dc, 128, 128], BF16).ap()
    d_ql = nc.dram_tensor('qld', [ntiles, n_dc, 128, 128], BF16).ap()
    d_wT = {}
    for w in 'qkvo':
        for hl in 'hl':
            d_wT[w + hl] = nc.dram_tensor(f'w{w}T{hl}d', [128, n_dc * D],
                                          BF16).ap()

    psum_mm = [nc.alloc_psum_tensor(f'pmm{i}', [128, 512], F32)
               for i in range(6)]
    psum_tb = nc.alloc_psum_tensor('ptb', [128, 1024], BF16)
    psum_st = nc.alloc_psum_tensor('pst', [128, 512], F32)

    mem = Mem(nc, (nc.sbuf_bytes_remaining - 128) // 64 * 64)
    P = Prog(nc)

    ident = mem.bf16('ident', 128)
    onesr = mem.f32('onesr', 128)
    onesc = mem.f32('onesc', 1)
    magic = mem.f32('magic', 1)
    kapb = mem.f32('kap', 1)
    cdesc1 = mem.f32('cdesc1', 1)
    cdesc2 = mem.f32('cdesc2', 1)
    convc = [mem.f32(f'convc{i}', n_hc) for i in range(3)]
    convb = mem.f32('convb', n_hc)
    eps_b = mem.f32('epsb', 1)

    identf = mem.f32('identf', 128)
    i_ap = ident.ap()
    if_ap = identf.ap()
    ones_row = onesr.ap(p0=0, p1=1)

    P.dma('y', i_ap, d_ident, reads=['d_ident'], writes=['ident'],
          label='ld_ident')
    P.record('v', lambda e, o=if_ap, i=i_ap: e.tensor_copy(o, i),
             reads=['ident'], writes=['identf'])
    for i in range(3):
        P.dma('y', convc[i].ap(), d_cw[i], reads=[f'd_cw{i}'],
              writes=[f'convc{i}'], label=f'ld_cw{i}')
    P.dma('y', convb.ap(), d_cb, reads=['d_cb'], writes=['convb'],
          label='ld_cb')
    P.record('v', lambda e, o=onesr.ap(p0=0, p1=1): e.memset(o, 1.0),
             writes=['ones1'])
    P.record('v', lambda e, o=onesc.ap(): e.memset(o, 1.0), writes=['onesc'])
    P.record('v', lambda e, o=eps_b.ap(): e.memset(o, 1e-5), writes=['epsb'])
    kap_ap = kapb.ap()
    magic_ap = magic.ap()
    P.record('v', lambda e, o=kap_ap: e.memset(o, 1.5), writes=['kap'])
    P.record('v', lambda e, o=kap_ap: e.tensor_scalar(o, o, M2, M2, ALU.add,
             ALU.subtract), reads=['kap'], writes=['kap'])
    P.record('v', lambda e, o=magic_ap, i=kap_ap: e.tensor_scalar(
        o, i, (M2 - M1) / 2.0, M1, ALU.mult, ALU.add),
        reads=['kap'], writes=['magic'])

    def bcast_row(row_ap, row_res, out_ap, out_res, n):
        ps = psum_st.ap()[:, 0:n]
        P.record('t', lambda e, o=ps, l=ones_row, r=row_ap: e.matmul(
            o, l, r, start=True, stop=True), reads=['ones1', row_res],
            writes=['ps_st'], label=f'bc_{out_res}')
        P.record('v', lambda e, o=out_ap, i=ps: e.tensor_copy(o, i),
                 reads=['ps_st'], writes=[out_res])

    def col2row(col_ap, col_res, row_ap, row_res, wp=128):
        ps = psum_st.ap()[0:1, 0:wp]
        P.record('t', lambda e, o=ps, i=col_ap, idn=if_ap[0:wp, 0:wp]:
                 e.transpose(o, i, idn),
                 reads=[col_res, 'identf'], writes=['ps_st'])
        P.record('v', lambda e, o=row_ap, i=ps: e.tensor_copy(o, i),
                 reads=['ps_st'], writes=[row_res])

    def row2col(row_ap, row_res, col_ap, col_res, wp=128):
        ps = psum_st.ap()[0:wp, 0:1]
        P.record('t', lambda e, o=ps, i=row_ap, idn=if_ap[0:1, 0:1]:
                 e.transpose(o, i, idn),
                 reads=[row_res, 'identf'], writes=['ps_st'])
        P.record('v', lambda e, o=col_ap, i=ps: e.tensor_copy(o, i),
                 reads=['ps_st'], writes=[col_res])

    def tr_tile(src_aps, src_res, dst3_ap, dst_res, nblk, key, evac='v'):
        for c in range(nblk):
            pt = psum_tb.ap()[:, c * 128:(c + 1) * 128]
            P.record('t', lambda e, o=pt, i=src_aps[c], idn=i_ap:
                     e.transpose(o, i, idn), reads=[src_res, 'ident'],
                     writes=['ps_tb'], accum=key)
        src = psum_tb.ap()[:, 0:nblk * 128].rearrange('p (c n) -> p c n',
                                                      c=nblk)
        P.record(evac, lambda e, o=dst3_ap, i=src: e.tensor_copy(o, i),
                 reads=['ps_tb'], writes=[dst_res])

    def emit_wquant(d_w, rows, cols, d_mT, dmT_res, cdesc_buf, cdesc_res,
                    wname):
        n_rt = rows // 128
        n_cc = cols // 128
        stage = mem.f32(f'{wname}_stage', n_rt * cols)
        acc = mem.f32(f'{wname}_acc', 1)
        red = mem.f32(f'{wname}_red', 1)
        scl = mem.f32(f'{wname}_scl', 1)
        mq = mem.bf16(f'{wname}_mq', cols)
        mT_sb = mem.bf16(f'{wname}_mT', n_cc * rows)
        sres = f'{wname}_stage'
        for t in range(n_rt):
            P.dma('y', stage.ap(t * cols, (t + 1) * cols),
                  d_w[t * 128:(t + 1) * 128, :], reads=[f'd_{wname}'],
                  writes=[sres], label=f'{wname}_ld{t}')
        for t in range(n_rt):
            sap = stage.ap(t * cols, (t + 1) * cols)
            P.record('v', lambda e, o=red.ap(), i=sap: e.reduce_sum(
                o, i, AX.X, apply_absolute_value=True),
                reads=[sres], writes=[f'{wname}_red'])
            if t == 0:
                P.record('v', lambda e, o=acc.ap(), i=red.ap():
                         e.tensor_copy(o, i), reads=[f'{wname}_red'],
                         writes=[f'{wname}_acc'])
            else:
                P.record('v', lambda e, o=acc.ap(), i=red.ap():
                         e.tensor_add(o, o, i),
                         reads=[f'{wname}_red', f'{wname}_acc'],
                         writes=[f'{wname}_acc'])
        ps = psum_st.ap()[0:1, 0:1]
        P.record('t', lambda e, o=ps, l=acc.ap(), r=onesc.ap(): e.matmul(
            o, l, r, start=True, stop=True),
            reads=[f'{wname}_acc', 'onesc'], writes=['ps_st'])
        sc1 = scl.ap(p0=0, p1=1)[:, 0:1]
        P.record('v', lambda e, o=sc1, i=ps: e.tensor_scalar(
            o, i, 1.0 / (rows * cols), 1e-5, ALU.mult, ALU.max),
            reads=['ps_st'], writes=[f'{wname}_c'])
        P.record('v', lambda e, o=sc1: e.reciprocal(o, o),
                 reads=[f'{wname}_c'], writes=[f'{wname}_c'])
        bcast_row(sc1, f'{wname}_c', scl.ap(), f'{wname}_s128', 1)
        P.record('v', lambda e, o=cdesc_buf.ap(), i=scl.ap():
                 e.reciprocal(o, i), reads=[f'{wname}_s128'],
                 writes=[cdesc_res])
        for t in range(n_rt):
            sap = stage.ap(t * cols, (t + 1) * cols)
            P.record('v', lambda e, o=sap, s=scl.ap(), m=magic_ap:
                     e.tensor_scalar(o, o, s, m, ALU.mult, ALU.add),
                     reads=[sres, f'{wname}_s128', 'magic'], writes=[sres])
            P.record('v', lambda e, o=sap, m=magic_ap: e.tensor_scalar(
                o, o, m, None, ALU.subtract), reads=[sres, 'magic'],
                writes=[sres])
            P.record('v', lambda e, o=mq.ap(), i=sap: e.tensor_scalar(
                o, i, -1.0, 1.0, ALU.max, ALU.min),
                reads=[sres], writes=[f'{wname}_mq'])
            nb = 8 if n_cc >= 8 else n_cc
            for cb in range(0, n_cc, nb):
                blks = [mq.ap(c * 128, (c + 1) * 128)
                        for c in range(cb, cb + nb)]
                dst3 = mT_sb.ap3(cb * rows + t * 128, rows, nb, 128)
                tr_tile(blks, f'{wname}_mq', dst3, f'{wname}_mT', nb,
                        (wname, t, cb))
        P.dma('s', d_mT, mT_sb.ap(), reads=[f'{wname}_mT'], writes=[dmT_res],
              label=f'{wname}_spill')
        stage.free(); acc.free(); red.free(); scl.free(); mq.free()
        mT_sb.free()

    def emit_ffn(in_ap, out_ap, in_res, out_res, fid):
        m1T = mem.bf16(f'{fid}_m1T', n_dc * H)
        m2T = mem.bf16(f'{fid}_m2T', n_hc * D)
        nq = min(4, n_dc)
        for q in range(nq):
            lo, hi = q * (n_dc // nq) * H, (q + 1) * (n_dc // nq) * H
            P.dma('y', m1T.ap(lo, hi), d_m1T[:, lo:hi], reads=['d_w1mT'],
                  writes=[f'{fid}m1Tq{q}'], label=f'{fid}_m1Tld{q}')
        nq2 = min(4, n_hc)
        for q in range(nq2):
            lo, hi = q * (n_hc // nq2) * D, (q + 1) * (n_hc // nq2) * D
            P.dma('y', m2T.ap(lo, hi), d_m2T[:, lo:hi], reads=['d_w2mT'],
                  writes=[f'{fid}m2Tq{q}'], label=f'{fid}_m2Tld{q}')

        xr = [mem.f32(f'{fid}_xr{i}', D) for i in range(3)]
        sq = [mem.f32(f'{fid}_sq{i}', D) for i in range(1)]
        stt_ = [mem.f32(f'{fid}_stt{i}', 24) for i in range(2)]
        k1tm = [mem.bf16(f'{fid}_k1tm{i}', D) for i in range(1)]
        k1fm = [mem.bf16(f'{fid}_k1fm{i}', n_dc * TF) for i in range(2)]
        g1row = mem.f32(f'{fid}_g1row', TF)
        g1bc = [mem.f32(f'{fid}_g1bc{i}', TF) for i in range(1)]
        Bs = [mem.f32(f'{fid}_Bs{i}', TF + 3) for i in range(2)]
        carry = mem.f32(f'{fid}_carry', 2 * n_hc)
        ygrp = mem.f32(f'{fid}_ygrp', n_hc * (TF + 1))
        yab = [mem.f32(f'{fid}_yab{i}', TF + 2) for i in range(1)]
        maxacc = mem.f32(f'{fid}_maxacc', TF + 2)
        s2row = mem.f32(f'{fid}_s2row', TF + 2)
        fring = mem.f32(f'{fid}_fring', RK)
        s2bc = [mem.f32(f'{fid}_s2bc{i}', TF + 2) for i in range(1)]
        frowb = mem.f32(f'{fid}_frow', 128)
        k2r = mem.bf16(f'{fid}_k2r', n_hc * RK)
        ftm = [mem.f32(f'{fid}_ftm{i}', 1) for i in range(2)]

        gctr = 0
        for seq in range(NSEQ):
            done_tiles = 0
            for hg in range(NHG):
                t0 = hg * TF
                ws = max(t0 - 1, 0)
                we = S if hg == NHG - 1 else t0 + TF - 1
                W = we - ws
                ridx = gctr % 2
                for i in range(TPG):
                    a_glob = (seq * S + t0) // 128 + i
                    xi = xr[(gctr * TPG + i) % 3]
                    xires = f'{fid}_xr{(gctr * TPG + i) % 3}'
                    P.dma('y', xi.ap(), in_ap[a_glob * 128:(a_glob + 1) * 128, :],
                          reads=[in_res], writes=[xires],
                          label=f'{fid}_xld{a_glob}')
                    st_i = stt_[i % 2]
                    rS = f'{fid}_stt{i % 2}'
                    sq_i, rQ = sq[0], f'{fid}_sq0'
                    k1_i, rK = k1tm[0], f'{fid}_k1tm0'
                    cm = lambda c: st_i.ap(c, c + 1)
                    P.record('v', lambda e, o=cm(0), x=xi.ap(): e.reduce_sum(
                        o, x, AX.X), reads=[xires], writes=[rS + 'a'])
                    P.record('v', lambda e, o=cm(1), i_=cm(0): e.tensor_scalar(
                        o, i_, -1.0 / D, None, ALU.mult), reads=[rS + 'a'],
                        writes=[rS + 'm'])
                    P.record('s', lambda e, o=sq_i.ap(), x=xi.ap(), b=cm(1),
                             a=cm(2): e.activation(o, x, ACT.Square, bias=b,
                             accum_out=a),
                             reads=[xires, rS + 'm'], writes=[rQ, rS + 'v'])
                    P.record('v', lambda e, o=cm(3), i_=cm(2): e.tensor_scalar(
                        o, i_, 1.0 / D, None, ALU.mult), reads=[rS + 'v'],
                        writes=[rS + 'v2'])
                    P.record('s', lambda e, o=cm(4), i_=cm(3), b=eps_b.ap():
                             e.activation(o, i_, ACT.Sqrt, bias=b),
                             reads=[rS + 'v2', 'epsb'], writes=[rS + 'sd'])
                    P.record('v', lambda e, o=cm(5), i_=cm(4): e.reciprocal(o, i_),
                             reads=[rS + 'sd'], writes=[rS + 'r'])
                    P.record('v', lambda e, o=cm(6), i_=xi.ap(): e.reduce_max(
                        o, i_, AX.X), reads=[xires], writes=[rS + 'x'])
                    P.record('v', lambda e, o=cm(11), i_=xi.ap():
                             e.tensor_reduce(o, i_, AX.X, ALU.min),
                             reads=[xires], writes=[rS + 'n'])
                    P.record('v', lambda e, o=cm(6), m=cm(1):
                             e.tensor_scalar(o, o, m, None, ALU.add),
                             reads=[rS + 'x', rS + 'm'], writes=[rS + 'x'])
                    P.record('v', lambda e, o=cm(11), m=cm(1):
                             e.tensor_scalar(o, o, m, -1.0, ALU.add, ALU.mult),
                             reads=[rS + 'n', rS + 'm'], writes=[rS + 'n'])
                    P.record('v', lambda e, o=cm(6), i1=cm(11):
                             e.tensor_tensor(o, o, i1, ALU.max),
                             reads=[rS + 'x', rS + 'n'], writes=[rS + 'x'])
                    P.record('v', lambda e, o=cm(7), i_=cm(6), r=cm(5):
                             e.tensor_scalar(o, i_, r, 1e-5, ALU.mult, ALU.max),
                             reads=[rS + 'x', rS + 'r'], writes=[rS + 'u'])
                    P.record('v', lambda e, o=cm(8), i_=cm(7): e.reciprocal(o, i_),
                             reads=[rS + 'u'], writes=[rS + 'u2'])
                    P.record('v', lambda e, o=cm(9), i_=cm(8), r=cm(5):
                             e.tensor_scalar(o, i_, 127.0, r, ALU.mult, ALU.mult),
                             reads=[rS + 'u2', rS + 'r'], writes=[rS + 's1'])
                    P.record('v', lambda e, o=sq_i.ap(), x=xi.ap(), m=cm(1),
                             s=cm(9): e.tensor_scalar(o, x, m, s, ALU.add,
                             ALU.mult), reads=[xires, rS + 'm', rS + 's1'],
                             writes=[rQ])
                    P.record('v', lambda e, o=k1_i.ap(), i_=sq_i.ap(),
                             m=magic_ap: e.tensor_scalar(o, i_, m, m, ALU.add,
                             ALU.subtract), reads=[rQ, 'magic'], writes=[rK])
                    P.record('v', lambda e, o=cm(10), i_=cm(7), c=cdesc1.ap():
                             e.tensor_scalar(o, i_, c, 1.0 / 127.0, ALU.mult,
                             ALU.mult), reads=[rS + 'u', 'w1_cdesc'],
                             writes=[rS + 'g'])
                    col2row(cm(10), rS + 'g',
                            g1row.ap(i * 128, (i + 1) * 128, p0=0, p1=1),
                            f'{fid}_g1row', 128)
                    blks = [k1_i.ap(dc * 128, (dc + 1) * 128)
                            for dc in range(n_dc)]
                    dst3 = k1fm[ridx].ap3(i * 128, TF, n_dc, 128)
                    tr_tile(blks, rK, dst3, f'{fid}_k1fm{ridx}', n_dc,
                            (fid, 'k1t', gctr, i))
                bcast_row(g1row.ap(0, TF, p0=0, p1=1), f'{fid}_g1row',
                          g1bc[0].ap(0, TF), f'{fid}_g1bc0', TF)
                co = ws - t0 + 2

                def stA(j):
                    pb = psum_mm[j % 2]
                    pbres = f'pb{j % 2}'
                    for dc in range(n_dc):
                        P.record('t', lambda e, o=pb.ap()[:, 0:TF],
                                 l=m1T.ap(dc * H + j * 128,
                                          dc * H + (j + 1) * 128),
                                 r=k1fm[ridx].ap(dc * TF, (dc + 1) * TF),
                                 st=(dc == 0), sp=(dc == n_dc - 1):
                                 e.matmul(o, l, r, start=st, stop=sp),
                                 reads=[f'{fid}m1Tq{dc * nq // n_dc}',
                                        f'{fid}_k1fm{ridx}'],
                                 writes=[pbres], accum=(fid, 1, seq, hg, j),
                                 label=f'{fid}_mm1_{seq}_{hg}_{j}')

                def stB(j):
                    pb = psum_mm[j % 2]
                    pbres = f'pb{j % 2}'
                    Bj, Bres = Bs[j % 2], f'{fid}_Bs{j % 2}'
                    if hg == 0:
                        P.record('v', lambda e, o=Bj.ap(0, 2): e.memset(o, 0.0),
                                 writes=[Bres])
                    else:
                        P.record('v', lambda e, o=Bj.ap(0, 2),
                                 i_=carry.ap(2 * j, 2 * j + 2):
                                 e.tensor_copy(o, i_), reads=[f'{fid}_carry'],
                                 writes=[Bres])
                    if hg == NHG - 1:
                        P.record('v', lambda e, o=Bj.ap(TF + 2, TF + 3):
                                 e.memset(o, 0.0), writes=[Bres])
                    P.record('v', lambda e, o=Bj.ap(2, TF + 2),
                             i0=pb.ap()[:, 0:TF], i1=g1bc[0].ap(0, TF):
                             e.tensor_tensor(o, i0, i1, ALU.mult),
                             reads=[pbres, f'{fid}_g1bc0'], writes=[Bres])
                    if hg < NHG - 1:
                        P.record('v', lambda e, o=carry.ap(2 * j, 2 * j + 2),
                                 i_=Bj.ap(TF, TF + 2): e.tensor_copy(o, i_),
                                 reads=[Bres], writes=[f'{fid}_carry'])

                def stCD(j):
                    Bj, Bres = Bs[j % 2], f'{fid}_Bs{j % 2}'
                    yj = ygrp.ap(j * (TF + 1), j * (TF + 1) + W)
                    yjres = f'{fid}_y{j}'
                    P.record('s', lambda e, o=yj, i_=Bj.ap(co - 1, co - 1 + W),
                             c=convc[0].ap(j, j + 1), b=convb.ap(j, j + 1):
                             e.activation(o, i_, ACT.Identity, bias=b, scale=c),
                             reads=[Bres, 'convc0', 'convb'], writes=[yjres])
                    P.record('v', lambda e, o=yj, i0=Bj.ap(co, co + W),
                             c=convc[1].ap(j, j + 1), i1=yj:
                             e.scalar_tensor_tensor(o, i0, c, i1, ALU.mult,
                             ALU.add), reads=[Bres, 'convc1', yjres],
                             writes=[yjres])
                    P.record('v', lambda e, o=yj, i0=Bj.ap(co + 1, co + 1 + W),
                             c=convc[2].ap(j, j + 1), i1=yj:
                             e.scalar_tensor_tensor(o, i0, c, i1, ALU.mult,
                             ALU.add), reads=[Bres, 'convc2', yjres],
                             writes=[yjres])

                def stEF(j):
                    yj = ygrp.ap(j * (TF + 1), j * (TF + 1) + W)
                    yjres = f'{fid}_y{j}'
                    ya, yares = yab[0], f'{fid}_yab0'
                    P.record('s', lambda e, o=yj: e.activation(o, o, ACT.Silu),
                             reads=[yjres], writes=[yjres])
                    P.record('s', lambda e, o=ya.ap(0, W), i_=yj: e.activation(
                        o, i_, ACT.Abs), reads=[yjres], writes=[yares])
                    if j == 0:
                        P.record('v', lambda e, o=maxacc.ap(0, W),
                                 i_=ya.ap(0, W): e.tensor_copy(o, i_),
                                 reads=[yares], writes=[f'{fid}_mx'])
                    else:
                        P.record('v', lambda e, o=maxacc.ap(0, W),
                                 i_=ya.ap(0, W): e.tensor_tensor(o, o, i_,
                                 ALU.max), reads=[yares, f'{fid}_mx'],
                                 writes=[f'{fid}_mx'])

                for jj in range(n_hc + 3):
                    if jj < n_hc:
                        stA(jj)
                    if 0 <= jj - 1 < n_hc:
                        stB(jj - 1)
                    if 0 <= jj - 2 < n_hc:
                        stCD(jj - 2)
                    if 0 <= jj - 3 < n_hc:
                        stEF(jj - 3)
                for w in range(ceil_div(W, 128)):
                    wp = min(128, W - w * 128)
                    pst = psum_st.ap()[0:wp, 0:128]
                    P.record('t', lambda e, o=pst, i_=maxacc.ap(
                        w * 128, w * 128 + wp), idn=if_ap: e.transpose(
                        o, i_, idn), reads=[f'{fid}_mx', 'identf'],
                        writes=['ps_st'])
                    P.record('v', lambda e, o=stt_[0].ap(16, 17, p1=wp),
                             i_=pst: e.reduce_max(o, i_, AX.X),
                             reads=['ps_st'], writes=['am_tm'])
                    P.record('v', lambda e, o=stt_[0].ap(17, 18, p1=wp),
                             i_=stt_[0].ap(16, 17, p1=wp): e.tensor_scalar(
                             o, i_, 1e-5, None, ALU.max), reads=['am_tm'],
                             writes=['u2f'])
                    P.record('v', lambda e, o=stt_[0].ap(18, 19, p1=wp),
                             i_=stt_[0].ap(17, 18, p1=wp): e.reciprocal(o, i_),
                             reads=['u2f'], writes=['r2f'])
                    P.record('v', lambda e, o=stt_[0].ap(19, 20, p1=wp),
                             i_=stt_[0].ap(18, 19, p1=wp): e.tensor_scalar(
                             o, i_, 127.0, None, ALU.mult), reads=['r2f'],
                             writes=['s2tm'])
                    P.record('v', lambda e, o=stt_[0].ap(20, 21, p1=wp),
                             i_=stt_[0].ap(17, 18, p1=wp),
                             c=cdesc2.ap(0, 1, p1=wp): e.tensor_scalar(
                             o, i_, c, 0.5 / 127.0, ALU.mult, ALU.mult),
                             reads=['u2f', 'w2_cdesc'], writes=['ftm0'])
                    col2row(stt_[0].ap(19, 20, p1=wp), 's2tm',
                            s2row.ap(w * 128, w * 128 + wp, p0=0, p1=1),
                            f'{fid}_s2row', wp)
                    col2row(stt_[0].ap(20, 21, p1=wp), 'ftm0',
                            frowb.ap(0, wp, p0=0, p1=1), 'frow', wp)
                    for (gg, rr, ln) in _ring_pieces(ws + w * 128, wp, RK):
                        o0 = gg - ws - w * 128
                        P.record('v', lambda e, o=fring.ap(rr, rr + ln, p0=0,
                                 p1=1), i_=frowb.ap(o0, o0 + ln, p0=0, p1=1):
                                 e.tensor_copy(o, i_),
                                 reads=['frow'], writes=['fring'])
                bcast_row(s2row.ap(0, W, p0=0, p1=1), f'{fid}_s2row',
                          s2bc[0].ap(0, W), f'{fid}_s2bc0', W)
                for j in range(n_hc):
                    yj = ygrp.ap(j * (TF + 1), j * (TF + 1) + W)
                    ya, yares = yab[0], f'{fid}_yab0'
                    P.record('v', lambda e, o=ya.ap(0, W), i0=yj,
                             i1=s2bc[0].ap(0, W): e.tensor_tensor(
                             o, i0, i1, ALU.mult),
                             reads=[f'{fid}_y{j}', f'{fid}_s2bc0'],
                             writes=[yares])
                    for (gg, rr, ln) in _ring_pieces(ws, W, RK):
                        P.record('v', lambda e, o=k2r.ap(j * RK + rr,
                                 j * RK + rr + ln), i_=ya.ap(gg - ws,
                                 gg - ws + ln), m=magic_ap: e.tensor_scalar(
                                 o, i_, m, m, ALU.add, ALU.subtract),
                                 reads=[yares, 'magic'],
                                 writes=[f'{fid}_k2r{j}'])
                ready = S // 128 if hg == NHG - 1 else (we + 1 - 128) // 128
                for a in range(done_tiles, ready):
                    a_glob = seq * ntile_seq + a
                    ga = seq * ntile_seq + a
                    pbt = [psum_mm[2 + 2 * (a % 2)], psum_mm[3 + 2 * (a % 2)]]
                    pbr = [f'pb{2 + 2 * (a % 2)}', f'pb{3 + 2 * (a % 2)}']
                    xsi, xsres = xr[ga % 3], f'{fid}_xr{ga % 3}'
                    rc = (a * 128) % RK
                    nhalf = ceil_div(D, 512)
                    for j in range(n_hc):
                        for h2 in range(nhalf):
                            nn = min(512, D - h2 * 512)
                            P.record('t', lambda e, o=pbt[h2].ap()[:, 0:nn],
                                     l=k2r.ap(j * RK + rc, j * RK + rc + 128),
                                     r=m2T.ap(j * D + h2 * 512,
                                              j * D + h2 * 512 + nn),
                                     st=(j == 0), sp=(j == n_hc - 1):
                                     e.matmul(o, l, r, start=st, stop=sp),
                                     reads=[f'{fid}_k2r{j}',
                                            f'{fid}m2Tq{j * nq2 // n_hc}'],
                                     writes=[pbr[h2]],
                                     accum=(fid, 2, seq, a, h2),
                                     label=f'{fid}_mm2_{seq}_{a}_{j}_{h2}')
                    fi, fires = ftm[a % 2], f'{fid}_ftm{a % 2}'
                    row2col(fring.ap(rc, rc + 128, p0=0, p1=1), 'fring',
                            fi.ap(), fires, 128)
                    for h2 in range(nhalf):
                        nn = min(512, D - h2 * 512)
                        P.record('v', lambda e, o=xsi.ap(h2 * 512, h2 * 512 + nn),
                                 i0=pbt[h2].ap()[:, 0:nn], f=fi.ap():
                                 e.scalar_tensor_tensor(o, i0, f, o, ALU.mult,
                                 ALU.add), reads=[pbr[h2], fires, xsres],
                                 writes=[xsres])
                    P.dma('s', out_ap[a_glob * 128:(a_glob + 1) * 128, :],
                          xsi.ap(), reads=[xsres], writes=[out_res],
                          label=f'{fid}_xst{a_glob}')
                done_tiles = ready
                gctr += 1
        for b in (m1T, m2T, g1row, carry, ygrp, maxacc, s2row, fring, k2r,
                  frowb):
            b.free()
        for lst in (xr, sq, stt_, k1tm, k1fm, g1bc, Bs, yab, s2bc, ftm):
            for b in lst:
                b.free()

    n_kc = S // 128
    n_kg = ceil_div(S, 512)
    nhalf = ceil_div(D, 512)

    class Acc:
        def __init__(self, total):
            self.total = total
            self.i = 0

        def nxt(self):
            st = self.i == 0
            sp = self.i == self.total - 1
            self.i += 1
            return st, sp

    def emit_wprep(w):
        wth = mem.bf16(f'a_{w}th', n_dc * D)
        wtl = mem.bf16(f'a_{w}tl', n_dc * D)
        stg = [mem.f32(f'a_{w}stg{i}', D) for i in range(2)]
        hib = [mem.bf16(f'a_{w}hi{i}', D) for i in range(2)]
        lob = [mem.bf16(f'a_{w}lo{i}', D) for i in range(2)]
        for dt in range(n_dc):
            sg, rs = stg[dt % 2], f'a_{w}stg{dt % 2}'
            hi, rh = hib[dt % 2], f'a_{w}hi{dt % 2}'
            lo, rl = lob[dt % 2], f'a_{w}lo{dt % 2}'
            P.dma('y', sg.ap(), d_wx[w][dt * 128:(dt + 1) * 128, :],
                  reads=[f'd_w{w}'], writes=[rs], label=f'{w}ld{dt}')
            P.record('v', lambda e, o=hi.ap(), i=sg.ap(): e.tensor_copy(o, i),
                     reads=[rs], writes=[rh])
            P.record('v', lambda e, o=lo.ap(), i0=sg.ap(), i1=hi.ap():
                     e.tensor_tensor(o, i0, i1, ALU.subtract),
                     reads=[rs, rh], writes=[rl])
            for src, dst, res_s, res_d in ((hi, wth, rh, f'a_{w}th'),
                                           (lo, wtl, rl, f'a_{w}tl')):
                nb = min(8, n_dc)
                for cb in range(0, n_dc, nb):
                    blks = [src.ap(ec * 128, (ec + 1) * 128)
                            for ec in range(cb, cb + nb)]
                    dst3 = dst.ap3(cb * D + dt * 128, D, nb, 128)
                    tr_tile(blks, res_s, dst3, res_d, nb, (w, dt, cb, res_d))
        P.dma('s', d_wT[w + 'h'], wth.ap(), reads=[f'a_{w}th'],
              writes=[f'd_wT{w}h'], label=f'{w}spillh')
        P.dma('s', d_wT[w + 'l'], wtl.ap(), reads=[f'a_{w}tl'],
              writes=[f'd_wT{w}l'], label=f'{w}spilll')
        for b in stg + hib + lob + [wth, wtl]:
            b.free()

    def ld_wt(w, bufs):
        for hl, b in zip('hl', bufs):
            P.dma('y', b.ap(), d_wT[w + hl], reads=[f'd_wT{w}{hl}'],
                  writes=[f'aw_{w}{hl}'], label=f'ld_wT{w}{hl}')

    def bias_bcast(w, buf, row):
        P.dma('y', row.ap(0, D, p0=0, p1=1), d_bx[w], reads=[f'd_b{w}'],
              writes=['a_brow'], label=f'ld_b{w}')
        for h2 in range(nhalf):
            nn = min(512, D - h2 * 512)
            bcast_row(row.ap(h2 * 512, h2 * 512 + nn, p0=0, p1=1),
                      'a_brow', buf.ap(h2 * 512, h2 * 512 + nn),
                      'a_bb', nn)

    def emit_ln_split(xi, xires, st_i, rS, sq_i, rQ, lnf, rLF,
                      lnhi, rLH, lnlo, rLL):
        cm = lambda c: st_i.ap(c, c + 1)
        P.record('v', lambda e, o=cm(0), x=xi.ap(): e.reduce_sum(
            o, x, AX.X), reads=[xires], writes=[rS + 'a'])
        P.record('v', lambda e, o=cm(1), i_=cm(0): e.tensor_scalar(
            o, i_, -1.0 / D, None, ALU.mult), reads=[rS + 'a'],
            writes=[rS + 'm'])
        P.record('s', lambda e, o=sq_i.ap(), x=xi.ap(), b=cm(1), a=cm(2):
                 e.activation(o, x, ACT.Square, bias=b, accum_out=a),
                 reads=[xires, rS + 'm'], writes=[rQ, rS + 'v'])
        P.record('v', lambda e, o=cm(3), i_=cm(2): e.tensor_scalar(
            o, i_, 1.0 / D, None, ALU.mult), reads=[rS + 'v'],
            writes=[rS + 'v2'])
        P.record('s', lambda e, o=cm(4), i_=cm(3), b=eps_b.ap():
                 e.activation(o, i_, ACT.Sqrt, bias=b),
                 reads=[rS + 'v2', 'epsb'], writes=[rS + 'sd'])
        P.record('v', lambda e, o=cm(5), i_=cm(4): e.reciprocal(o, i_),
                 reads=[rS + 'sd'], writes=[rS + 'r'])
        P.record('v', lambda e, o=lnf.ap(), x=xi.ap(), m=cm(1), r=cm(5):
                 e.tensor_scalar(o, x, m, r, ALU.add, ALU.mult),
                 reads=[xires, rS + 'm', rS + 'r'], writes=[rLF])
        P.record('v', lambda e, o=lnhi.ap(), i_=lnf.ap(): e.tensor_copy(o, i_),
                 reads=[rLF], writes=[rLH])
        P.record('v', lambda e, o=lnlo.ap(), i0=lnf.ap(), i1=lnhi.ap():
                 e.tensor_tensor(o, i0, i1, ALU.subtract),
                 reads=[rLF, rLH], writes=[rLL])

    def emit_attn_all():
        for w in 'qkvo':
            emit_wprep(w)
            P.barrier()
        Khi = mem.bf16('a_Khi', n_dc * S)
        Klo = mem.bf16('a_Klo', n_dc * S)
        Vhi = mem.bf16('a_Vhi', n_kc * D)
        Vlo = mem.bf16('a_Vlo', n_kc * D)
        bbuf = mem.f32('a_bb', D)
        brow = mem.f32('a_brow', D)
        bb = {w: bbuf for w in 'qkvo'}
        stb = [mem.f32(f'a_stt{i}', 24) for i in range(2)]
        for seq in range(NSEQ):
            wk = [mem.bf16('a_wkh', n_dc * D), mem.bf16('a_wkl', n_dc * D)]
            ld_wt('k', wk)
            bias_bcast('k', bb['k'], brow)
            xr_ = [mem.f32(f'a_xr{i}', D) for i in range(2)]
            sq_ = [mem.f32(f'a_sq{i}', D) for i in range(1)]
            lnf_ = [mem.f32(f'a_lnf{i}', D) for i in range(1)]
            lnh_ = [mem.bf16(f'a_lnh{i}', D) for i in range(2)]
            lnl_ = [mem.bf16(f'a_lnl{i}', D) for i in range(2)]
            lnhF = [mem.bf16(f'a_lnhF{i}', n_dc * 128) for i in range(2)]
            lnlF = [mem.bf16(f'a_lnlF{i}', n_dc * 128) for i in range(2)]
            ktm = mem.f32('a_ktm', D)
            khi_t = mem.bf16('a_khit', D)
            klo_t = mem.bf16('a_klot', D)
            for i in range(n_kc):
                a_glob = seq * ntile_seq + i
                ri = i % 2
                xi, xires = xr_[ri], f'a_xr{ri}'
                P.dma('y', xi.ap(), d_x1[a_glob * 128:(a_glob + 1) * 128, :],
                      reads=['x1d'], writes=[xires], label=f'a1x{a_glob}')
                emit_ln_split(xi, xires, stb[ri], f'a_stt{ri}',
                              sq_[0], 'a_sq0', lnf_[0],
                              'a_lnf0', lnh_[ri], f'a_lnh{ri}',
                              lnl_[ri], f'a_lnl{ri}')
                for src, dstb, rs, rd in ((lnh_[ri], lnhF[ri], f'a_lnh{ri}',
                                           f'a_lnhF{ri}'),
                                          (lnl_[ri], lnlF[ri], f'a_lnl{ri}',
                                           f'a_lnlF{ri}')):
                    blks = [src.ap(dc * 128, (dc + 1) * 128)
                            for dc in range(n_dc)]
                    tr_tile(blks, rs, dstb.ap3(0, 128, n_dc, 128), rd, n_dc,
                            ('lnF', seq, i, rd))
                for dl, src, rs in ((d_lnh, lnhF[ri], f'a_lnhF{ri}'),
                                    (d_lnl, lnlF[ri], f'a_lnlF{ri}')):
                    P.dma('s', dl[a_glob].rearrange('dc p c -> p dc c'),
                          src.ap().rearrange('p (dc c) -> p dc c', dc=n_dc),
                          reads=[rs], writes=['lnd'], label=f'lnst{a_glob}')
                accs = [Acc(3 * n_dc) for _ in range(nhalf)]
                for ec in range(n_dc):
                    for sihi, sires in ((lnhF[ri], f'a_lnhF{ri}'),
                                        (lnlF[ri], f'a_lnlF{ri}')):
                        pass
                for ec in range(n_dc):
                    lh = lnhF[ri].ap(ec * 128, (ec + 1) * 128)
                    ll = lnlF[ri].ap(ec * 128, (ec + 1) * 128)
                    for h2 in range(nhalf):
                        nn = min(512, D - h2 * 512)
                        bk2 = 2 * (i % 2) + h2
                        pk, pkr = psum_mm[bk2], f'pb{bk2}'
                        for stat, statr, mov, movr in (
                                (lh, f'a_lnhF{ri}', wk[0], 'aw_kh'),
                                (lh, f'a_lnhF{ri}', wk[1], 'aw_kl'),
                                (ll, f'a_lnlF{ri}', wk[0], 'aw_kh')):
                            st, sp = accs[h2].nxt()
                            P.record('t', lambda e, o=pk.ap()[:, 0:nn], l=stat,
                                     r=mov.ap(ec * D + h2 * 512,
                                              ec * D + h2 * 512 + nn),
                                     st=st, sp=sp: e.matmul(o, l, r, start=st,
                                     stop=sp), reads=[statr, movr],
                                     writes=[pkr], accum=('K', seq, i, h2),
                                     label=f'K_{seq}_{i}_{ec}_{h2}')
                for h2 in range(nhalf):
                    nn = min(512, D - h2 * 512)
                    bk2 = 2 * (i % 2) + h2
                    P.record('v', lambda e, o=ktm.ap(h2 * 512, h2 * 512 + nn),
                             i0=psum_mm[bk2].ap()[:, 0:nn],
                             i1=bb['k'].ap(h2 * 512, h2 * 512 + nn):
                             e.tensor_tensor(o, i0, i1, ALU.add),
                             reads=[f'pb{bk2}', 'a_bb'], writes=['a_ktm'])
                P.record('v', lambda e, o=khi_t.ap(), i=ktm.ap():
                         e.tensor_copy(o, i), reads=['a_ktm'],
                         writes=['a_khit'])
                P.record('v', lambda e, o=klo_t.ap(), i0=ktm.ap(),
                         i1=khi_t.ap(): e.tensor_tensor(o, i0, i1,
                         ALU.subtract), reads=['a_ktm', 'a_khit'],
                         writes=['a_klot'])
                for src, dstb, rs, rd in ((khi_t, Khi, 'a_khit', 'a_Khi'),
                                          (klo_t, Klo, 'a_klot', 'a_Klo')):
                    blks = [src.ap(dc * 128, (dc + 1) * 128)
                            for dc in range(n_dc)]
                    tr_tile(blks, rs, dstb.ap3(i * 128, S, n_dc, 128), rd,
                            n_dc, ('KT', seq, i, rd))
            for b in wk + xr_ + sq_ + lnf_ + lnh_ + lnl_:
                b.free()
            ktm.free(); khi_t.free(); klo_t.free()
            P.barrier()
            wv = [mem.bf16('a_wvh', n_dc * D), mem.bf16('a_wvl', n_dc * D)]
            ld_wt('v', wv)
            bias_bcast('v', bb['v'], brow)
            vtm = mem.f32('a_vtm', D)
            for i in range(n_kc):
                a_glob = seq * ntile_seq + i
                ri = i % 2
                for dl, dstb, rd in ((d_lnh, lnhF[ri], f'a_lnhF{ri}'),
                                     (d_lnl, lnlF[ri], f'a_lnlF{ri}')):
                    P.dma('y', dstb.ap().rearrange('p (dc c) -> p dc c',
                          dc=n_dc), dl[a_glob].rearrange('dc p c -> p dc c'),
                          reads=['lnd'], writes=[rd], label=f'lnld{a_glob}')
                accs = [Acc(3 * n_dc) for _ in range(nhalf)]
                for ec in range(n_dc):
                    lh = lnhF[ri].ap(ec * 128, (ec + 1) * 128)
                    ll = lnlF[ri].ap(ec * 128, (ec + 1) * 128)
                    for h2 in range(nhalf):
                        nn = min(512, D - h2 * 512)
                        bk2 = 2 * (i % 2) + h2
                        pk, pkr = psum_mm[bk2], f'pb{bk2}'
                        for stat, statr, mov, movr in (
                                (lh, f'a_lnhF{ri}', wv[0], 'aw_vh'),
                                (lh, f'a_lnhF{ri}', wv[1], 'aw_vl'),
                                (ll, f'a_lnlF{ri}', wv[0], 'aw_vh')):
                            st, sp = accs[h2].nxt()
                            P.record('t', lambda e, o=pk.ap()[:, 0:nn], l=stat,
                                     r=mov.ap(ec * D + h2 * 512,
                                              ec * D + h2 * 512 + nn),
                                     st=st, sp=sp: e.matmul(o, l, r, start=st,
                                     stop=sp), reads=[statr, movr],
                                     writes=[pkr], accum=('V', seq, i, h2),
                                     label=f'V_{seq}_{i}_{ec}_{h2}')
                for h2 in range(nhalf):
                    nn = min(512, D - h2 * 512)
                    bk2 = 2 * (i % 2) + h2
                    P.record('v', lambda e, o=vtm.ap(h2 * 512, h2 * 512 + nn),
                             i0=psum_mm[bk2].ap()[:, 0:nn],
                             i1=bb['v'].ap(h2 * 512, h2 * 512 + nn):
                             e.tensor_tensor(o, i0, i1, ALU.add),
                             reads=[f'pb{bk2}', 'a_bb'], writes=['a_vtm'])
                P.record('v', lambda e, o=Vhi.ap(i * D, (i + 1) * D),
                         i=vtm.ap(): e.tensor_copy(o, i), reads=['a_vtm'],
                         writes=['a_Vhi'])
                P.record('v', lambda e, o=Vlo.ap(i * D, (i + 1) * D),
                         i0=vtm.ap(), i1=Vhi.ap(i * D, (i + 1) * D):
                         e.tensor_tensor(o, i0, i1, ALU.subtract),
                         reads=['a_vtm', 'a_Vhi'], writes=['a_Vlo'])
            for b in wv:
                b.free()
            vtm.free()
            P.barrier()
            wq = [mem.bf16('a_wqh', n_dc * D), mem.bf16('a_wql', n_dc * D)]
            ld_wt('q', wq)
            bias_bcast('q', bb['q'], brow)
            qtm = mem.f32('a_qtm', D)
            qhi_t = mem.bf16('a_qhit', D)
            qlo_t = mem.bf16('a_qlot', D)
            qF = [mem.bf16(f'a_qF{i}', n_dc * 128) for i in range(2)]
            for i in range(n_kc):
                a_glob = seq * ntile_seq + i
                ri = i % 2
                for dl, dstb, rd in ((d_lnh, lnhF[ri], f'a_lnhF{ri}'),
                                     (d_lnl, lnlF[ri], f'a_lnlF{ri}')):
                    P.dma('y', dstb.ap().rearrange('p (dc c) -> p dc c',
                          dc=n_dc), dl[a_glob].rearrange('dc p c -> p dc c'),
                          reads=['lnd'], writes=[rd], label=f'lnld2{a_glob}')
                accs = [Acc(3 * n_dc) for _ in range(nhalf)]
                for ec in range(n_dc):
                    lh = lnhF[ri].ap(ec * 128, (ec + 1) * 128)
                    ll = lnlF[ri].ap(ec * 128, (ec + 1) * 128)
                    for h2 in range(nhalf):
                        nn = min(512, D - h2 * 512)
                        bk2 = 2 * (i % 2) + h2
                        pk, pkr = psum_mm[bk2], f'pb{bk2}'
                        for stat, statr, mov, movr in (
                                (lh, f'a_lnhF{ri}', wq[0], 'aw_qh'),
                                (lh, f'a_lnhF{ri}', wq[1], 'aw_ql'),
                                (ll, f'a_lnlF{ri}', wq[0], 'aw_qh')):
                            st, sp = accs[h2].nxt()
                            P.record('t', lambda e, o=pk.ap()[:, 0:nn], l=stat,
                                     r=mov.ap(ec * D + h2 * 512,
                                              ec * D + h2 * 512 + nn),
                                     st=st, sp=sp: e.matmul(o, l, r, start=st,
                                     stop=sp), reads=[statr, movr],
                                     writes=[pkr], accum=('Q', seq, i, h2),
                                     label=f'Q_{seq}_{i}_{ec}_{h2}')
                for h2 in range(nhalf):
                    nn = min(512, D - h2 * 512)
                    bk2 = 2 * (i % 2) + h2
                    P.record('v', lambda e, o=qtm.ap(h2 * 512, h2 * 512 + nn),
                             i0=psum_mm[bk2].ap()[:, 0:nn],
                             i1=bb['q'].ap(h2 * 512, h2 * 512 + nn):
                             e.tensor_tensor(o, i0, i1, ALU.add),
                             reads=[f'pb{bk2}', 'a_bb'], writes=['a_qtm'])
                P.record('v', lambda e, o=qhi_t.ap(), i=qtm.ap():
                         e.tensor_copy(o, i), reads=['a_qtm'],
                         writes=['a_qhit'])
                P.record('v', lambda e, o=qlo_t.ap(), i0=qtm.ap(),
                         i1=qhi_t.ap(): e.tensor_tensor(o, i0, i1,
                         ALU.subtract), reads=['a_qtm', 'a_qhit'],
                         writes=['a_qlot'])
                for src, rs, dl in ((qhi_t, 'a_qhit', d_qh),
                                    (qlo_t, 'a_qlot', d_ql)):
                    qf, qfres = qF[ri], f'a_qF{ri}'
                    blks = [src.ap(dc * 128, (dc + 1) * 128)
                            for dc in range(n_dc)]
                    tr_tile(blks, rs, qf.ap3(0, 128, n_dc, 128), qfres, n_dc,
                            ('QT', seq, i, rs))
                    P.dma('s', dl[a_glob].rearrange('dc p c -> p dc c'),
                          qf.ap().rearrange('p (dc c) -> p dc c', dc=n_dc),
                          reads=[qfres], writes=['qd'], label=f'qst{a_glob}')
            for b in wq + qF + lnhF + lnlF:
                b.free()
            qtm.free(); qhi_t.free(); qlo_t.free()
            P.barrier()
            wo = [mem.bf16('a_woh', n_dc * D), mem.bf16('a_wol', n_dc * D)]
            ld_wt('o', wo)
            bias_bcast('o', bb['o'], brow)
            qh_ = [mem.bf16(f'a_q3h{i}', n_dc * 128) for i in range(1)]
            ql_ = [mem.bf16(f'a_q3l{i}', n_dc * 128) for i in range(1)]
            Pu = [mem.f32(f'a_Pu{i}', S) for i in range(2)]
            Phi = mem.bf16('a_Phi', S)
            Plo = mem.bf16('a_Plo', S)
            PhiT = mem.bf16('a_PhiT', n_kc * 128)
            PloT = mem.bf16('a_PloT', n_kc * 128)
            athi = mem.bf16('a_athi', D)
            atlo = mem.bf16('a_atlo', D)
            atFh = mem.bf16('a_atFh', n_dc * 128)
            atFl = mem.bf16('a_atFl', n_dc * 128)
            x1r = [mem.f32(f'a_x1r{i}', D) for i in range(2)]
            sc = 1.0 / math.sqrt(D)

            def st_scores(i):
                ri = i % 2
                a_glob = seq * ntile_seq + i
                for dl, dstb, rd in ((d_qh, qh_[0], 'a_q3h0'),
                                     (d_ql, ql_[0], 'a_q3l0')):
                    P.dma('y', dstb.ap().rearrange('p (dc c) -> p dc c',
                          dc=n_dc), dl[a_glob].rearrange('dc p c -> p dc c'),
                          reads=['qd'], writes=[rd], label=f'qld{a_glob}')
                P.dma('y', x1r[ri].ap(),
                      d_x1[a_glob * 128:(a_glob + 1) * 128, :],
                      reads=['x1d'], writes=[f'a_x1r{ri}'],
                      label=f'a3x{a_glob}')
                cmb = lambda c: stb[i % 2].ap(c, c + 1)
                rst = f'a_st{i % 2}'
                for g in range(n_kg):
                    nn = min(512, S - g * 512)
                    pq, pqr = psum_mm[g], f'pb{g}'
                    qacc = Acc(3 * n_dc)
                    for dc in range(n_dc):
                        qh = qh_[0].ap(dc * 128, (dc + 1) * 128)
                        ql = ql_[0].ap(dc * 128, (dc + 1) * 128)
                        for stat, statr, mov, movr in (
                                (qh, 'a_q3h0', Khi, 'a_Khi'),
                                (qh, 'a_q3h0', Klo, 'a_Klo'),
                                (ql, 'a_q3l0', Khi, 'a_Khi')):
                            st, sp = qacc.nxt()
                            P.record('t', lambda e, o=pq.ap()[:, 0:nn], l=stat,
                                     r=mov.ap(dc * S + g * 512,
                                              dc * S + g * 512 + nn),
                                     st=st, sp=sp: e.matmul(o, l, r, start=st,
                                     stop=sp), reads=[statr, movr],
                                     writes=[pqr], accum=('sc', seq, i, g),
                                     label=f'sc_{seq}_{i}_{dc}_{g}')
                    P.record('v', lambda e, o=cmb(g),
                             i_=pq.ap()[:, 0:nn]: e.reduce_max(
                             o, i_, AX.X), reads=[pqr], writes=[rst + f'm{g}'])
                    P.record('v', lambda e, o=cmb(4 + g), i_=cmb(g):
                             e.tensor_scalar(o, i_, -sc, None, ALU.mult),
                             reads=[rst + f'm{g}'], writes=[rst + f'b{g}'])
                    P.record('s', lambda e, o=Pu[i % 2].ap(g * 512,
                             g * 512 + nn), i_=pq.ap()[:, 0:nn], b=cmb(4 + g),
                             a=cmb(8 + g): e.activation(
                             o, i_, ACT.Exp, bias=b, scale=sc, accum_out=a),
                             reads=[pqr, rst + f'b{g}'],
                             writes=[f'a_Pu{i % 2}', rst + f's{g}'])

            def st_rest(i):
                ri = i % 2
                a_glob = seq * ntile_seq + i
                cmb = lambda c: stb[i % 2].ap(c, c + 1)
                rst = f'a_st{i % 2}'
                P.record('v', lambda e, o=cmb(12), i_=stb[i % 2].ap(0, n_kg):
                         e.reduce_max(o, i_, AX.X),
                         reads=[rst + f'm{g2}' for g2 in range(n_kg)],
                         writes=[rst + 'gm'])
                for g in range(n_kg):
                    P.record('v', lambda e, o=cmb(16 + g), i_=cmb(g),
                             m=cmb(12): e.tensor_scalar(o, i_, m, None,
                             ALU.subtract),
                             reads=[rst + f'm{g}', rst + 'gm'],
                             writes=[rst + f'd{g}'])
                P.record('s', lambda e, o=stb[i % 2].ap(16, 16 + n_kg),
                         i_=stb[i % 2].ap(16, 16 + n_kg): e.activation(
                         o, i_, ACT.Exp, scale=sc),
                         reads=[rst + f'd{g2}' for g2 in range(n_kg)],
                         writes=[rst + 'f'])
                P.record('v', lambda e, o=stb[i % 2].ap(8, 8 + n_kg),
                         i0=stb[i % 2].ap(8, 8 + n_kg),
                         i1=stb[i % 2].ap(16, 16 + n_kg): e.tensor_tensor(
                         o, i0, i1, ALU.mult),
                         reads=[rst + 'f'] + [rst + f's{g2}' for g2 in
                                              range(n_kg)],
                         writes=[rst + 'ws'])
                P.record('v', lambda e, o=cmb(13), i_=stb[i % 2].ap(8, 8 + n_kg):
                         e.reduce_sum(o, i_, AX.X), reads=[rst + 'ws'],
                         writes=[rst + 'tot'])
                P.record('v', lambda e, o=cmb(14), i_=cmb(13): e.reciprocal(
                    o, i_), reads=[rst + 'tot'], writes=[rst + 'r'])
                P.record('v', lambda e, o=stb[i % 2].ap(16, 16 + n_kg),
                         i0=stb[i % 2].ap(16, 16 + n_kg), s1=cmb(14):
                         e.tensor_scalar(o, i0, s1, None, ALU.mult),
                         reads=[rst + 'f', rst + 'r'], writes=[rst + 'rf'])
                for g in range(n_kg):
                    nn = min(512, S - g * 512)
                    P.record('v', lambda e, o=Pu[i % 2].ap(g * 512,
                             g * 512 + nn), s1=cmb(16 + g): e.tensor_scalar(
                             o, o, s1, None, ALU.mult),
                             reads=[f'a_Pu{i % 2}', rst + 'rf'],
                             writes=[f'a_Pu{i % 2}'])
                P.record('v', lambda e, o=Phi.ap(), i=Pu[i % 2].ap():
                         e.tensor_copy(o, i), reads=[f'a_Pu{i % 2}'],
                         writes=['a_Phi'])
                P.record('v', lambda e, o=Plo.ap(), i0=Pu[i % 2].ap(),
                         i1=Phi.ap(): e.tensor_tensor(o, i0, i1, ALU.subtract),
                         reads=[f'a_Pu{i % 2}', 'a_Phi'], writes=['a_Plo'])
                for src, rs, dstb, rd in ((Phi, 'a_Phi', PhiT, 'a_PhiT'),
                                          (Plo, 'a_Plo', PloT, 'a_PloT')):
                    nb = 8 if n_kc >= 8 else n_kc
                    for cb in range(0, n_kc, nb):
                        blks = [src.ap(kc * 128, (kc + 1) * 128)
                                for kc in range(cb, cb + nb)]
                        tr_tile(blks, rs, dstb.ap3(cb * 128, 128, nb, 128),
                                rd, nb, ('PT', seq, i, cb, rd))
                vaccs = [Acc(3 * n_kc) for _ in range(nhalf)]
                for kc in range(n_kc):
                    ph = PhiT.ap(kc * 128, (kc + 1) * 128)
                    pl = PloT.ap(kc * 128, (kc + 1) * 128)
                    for h2 in range(nhalf):
                        nn = min(512, D - h2 * 512)
                        pv = psum_mm[4 + h2] if nhalf > 1 else psum_mm[4]
                        pvr = f'pb{4 + h2}' if nhalf > 1 else 'pb4'
                        for stat, statr, mov, movr in (
                                (ph, 'a_PhiT', Vhi, 'a_Vhi'),
                                (ph, 'a_PhiT', Vlo, 'a_Vlo'),
                                (pl, 'a_PloT', Vhi, 'a_Vhi')):
                            st, sp = vaccs[h2].nxt()
                            P.record('t', lambda e, o=pv.ap()[:, 0:nn], l=stat,
                                     r=mov.ap(kc * D + h2 * 512,
                                              kc * D + h2 * 512 + nn),
                                     st=st, sp=sp: e.matmul(o, l, r, start=st,
                                     stop=sp), reads=[statr, movr],
                                     writes=[pvr], accum=('pv', seq, i, h2),
                                     label=f'pv_{seq}_{i}_{kc}_{h2}')
                for h2 in range(nhalf):
                    nn = min(512, D - h2 * 512)
                    pv = psum_mm[4 + h2] if nhalf > 1 else psum_mm[4]
                    pvr = f'pb{4 + h2}' if nhalf > 1 else 'pb4'
                    P.record('v', lambda e, o=athi.ap(h2 * 512, h2 * 512 + nn),
                             i=pv.ap()[:, 0:nn]: e.tensor_copy(o, i),
                             reads=[pvr], writes=['a_athi'])
                    P.record('v', lambda e, o=atlo.ap(h2 * 512, h2 * 512 + nn),
                             i0=pv.ap()[:, 0:nn],
                             i1=athi.ap(h2 * 512, h2 * 512 + nn):
                             e.tensor_tensor(o, i0, i1, ALU.subtract),
                             reads=[pvr, 'a_athi'], writes=['a_atlo'])
                for src, rs, dstb, rd in ((athi, 'a_athi', atFh, 'a_atFh'),
                                          (atlo, 'a_atlo', atFl, 'a_atFl')):
                    blks = [src.ap(dc * 128, (dc + 1) * 128)
                            for dc in range(n_dc)]
                    tr_tile(blks, rs, dstb.ap3(0, 128, n_dc, 128), rd, n_dc,
                            ('AT', seq, i, rd))
                oaccs = [Acc(3 * n_dc) for _ in range(nhalf)]
                for dv in range(n_dc):
                    ah = atFh.ap(dv * 128, (dv + 1) * 128)
                    al = atFl.ap(dv * 128, (dv + 1) * 128)
                    for h2 in range(nhalf):
                        nn = min(512, D - h2 * 512)
                        po = psum_mm[4 + h2] if nhalf > 1 else psum_mm[4]
                        por = f'pb{4 + h2}' if nhalf > 1 else 'pb4'
                        for stat, statr, mov, movr in (
                                (ah, 'a_atFh', wo[0], 'aw_oh'),
                                (ah, 'a_atFh', wo[1], 'aw_ol'),
                                (al, 'a_atFl', wo[0], 'aw_oh')):
                            st, sp = oaccs[h2].nxt()
                            P.record('t', lambda e, o=po.ap()[:, 0:nn], l=stat,
                                     r=mov.ap(dv * D + h2 * 512,
                                              dv * D + h2 * 512 + nn),
                                     st=st, sp=sp: e.matmul(o, l, r, start=st,
                                     stop=sp), reads=[statr, movr],
                                     writes=[por], accum=('O', seq, i, h2),
                                     label=f'O_{seq}_{i}_{dv}_{h2}')
                for h2 in range(nhalf):
                    nn = min(512, D - h2 * 512)
                    po = psum_mm[4 + h2] if nhalf > 1 else psum_mm[4]
                    por = f'pb{4 + h2}' if nhalf > 1 else 'pb4'
                    P.record('v', lambda e, o=x1r[ri].ap(h2 * 512,
                             h2 * 512 + nn), i0=po.ap()[:, 0:nn],
                             i1=x1r[ri].ap(h2 * 512, h2 * 512 + nn):
                             e.tensor_tensor(o, i0, i1, ALU.add),
                             reads=[por, f'a_x1r{ri}'],
                             writes=[f'a_x1r{ri}'])
                    P.record('v', lambda e, o=x1r[ri].ap(h2 * 512,
                             h2 * 512 + nn), i1=bb['o'].ap(h2 * 512,
                             h2 * 512 + nn): e.tensor_tensor(o, o, i1,
                             ALU.add), reads=['a_bb', f'a_x1r{ri}'],
                             writes=[f'a_x1r{ri}'])
                P.dma('s', d_x2[a_glob * 128:(a_glob + 1) * 128, :],
                      x1r[ri].ap(), reads=[f'a_x1r{ri}'], writes=['x2d'],
                      label=f'x2st{a_glob}')

            st_scores(0)
            for i in range(n_kc):
                if i + 1 < n_kc:
                    st_scores(i + 1)
                st_rest(i)
            for b in wo + qh_ + ql_ + x1r:
                b.free()
            for b in Pu + [Phi, Plo, PhiT, PloT, athi, atlo, atFh, atFl]:
                b.free()
            P.barrier()
        for b in [Khi, Klo, Vhi, Vlo, bbuf, brow] + stb:
            b.free()

    emit_wquant(d_w1, H, D, d_m1T, 'd_w1mT', cdesc1, 'w1_cdesc', 'w1')
    P.barrier()
    emit_wquant(d_w2, D, H, d_m2T, 'd_w2mT', cdesc2, 'w2_cdesc', 'w2')
    P.barrier()
    if PH == 'f1':
        emit_ffn(d_x, d_out, 'x_in', 'outd', 'F1')
    else:
        emit_ffn(d_x, d_x1, 'x_in', 'x1d', 'F1')
        P.barrier()
        emit_attn_all()
        P.barrier()
        emit_ffn(d_x2, d_out, 'x2d', 'outd', 'F2')
    P.record('y', lambda e: None, reads=['outd'], writes=[], label='final')

    from contextlib import ExitStack
    with ExitStack() as stack:
        P.materialize(stack)
    nstats = {e: len(P.ops[e]) for e in ENGINES}
    return nc, nstats


_NCORE = 8


def _mk_inputs_common(Wq, bq, Wk, bk, Wv, bv, Wo, bo, w1, conv_w, conv_b, w2):
    f32 = np.float32
    cw = np.ascontiguousarray(np.asarray(conv_w, dtype=f32)).reshape(-1, 3)
    com = {
        'w1': np.ascontiguousarray(np.asarray(w1, dtype=f32)),
        'w2': np.ascontiguousarray(np.asarray(w2, dtype=f32)),
        'cw0': np.ascontiguousarray(cw[:, 0].reshape(-1, 128).T),
        'cw1': np.ascontiguousarray(cw[:, 1].reshape(-1, 128).T),
        'cw2': np.ascontiguousarray(cw[:, 2].reshape(-1, 128).T),
        'cb': np.ascontiguousarray(
            np.asarray(conv_b, dtype=f32).reshape(-1, 128).T),
        'wq': np.ascontiguousarray(np.asarray(Wq, dtype=f32)),
        'wk': np.ascontiguousarray(np.asarray(Wk, dtype=f32)),
        'wv': np.ascontiguousarray(np.asarray(Wv, dtype=f32)),
        'wo': np.ascontiguousarray(np.asarray(Wo, dtype=f32)),
        'bq': np.ascontiguousarray(np.asarray(bq, dtype=f32).reshape(1, -1)),
        'bk': np.ascontiguousarray(np.asarray(bk, dtype=f32).reshape(1, -1)),
        'bv': np.ascontiguousarray(np.asarray(bv, dtype=f32).reshape(1, -1)),
        'bo': np.ascontiguousarray(np.asarray(bo, dtype=f32).reshape(1, -1)),
        'ident': np.eye(128, dtype=ml_dtypes.bfloat16),
    }
    return com


def kernel(L0_attn_x, L0_attn_Wq, L0_attn_bq, L0_attn_Wk, L0_attn_bk,
           L0_attn_Wv, L0_attn_bv, L0_attn_Wo, L0_attn_bo, L0_ffn_w1,
           L0_ffn_conv_w, L0_ffn_conv_b, L0_ffn_w2):
    x = np.ascontiguousarray(np.asarray(L0_attn_x, dtype=np.float32))
    B, S, D = x.shape
    nper = B // _NCORE
    cfg = dict(S=S, D=D, H=L0_ffn_w1.shape[0], NSEQ=nper, TF=256)
    nc, _ = build_core(cfg)
    com = _mk_inputs_common(L0_attn_Wq, L0_attn_bq, L0_attn_Wk, L0_attn_bk,
                            L0_attn_Wv, L0_attn_bv, L0_attn_Wo, L0_attn_bo,
                            L0_ffn_w1, L0_ffn_conv_w, L0_ffn_conv_b, L0_ffn_w2)
    in_maps = []
    for i in range(_NCORE):
        m = dict(com)
        m['x_in'] = np.ascontiguousarray(
            x[i * nper:(i + 1) * nper].reshape(nper * S, D))
        in_maps.append(m)
    res = run_bass_kernel_spmd(nc, in_maps, core_ids=list(range(_NCORE)))
    out = np.concatenate(
        [np.asarray(r['out']).reshape(nper, S, D) for r in res.results], axis=0)
    return out.astype(np.float32)


# version 08
# speedup vs baseline: 1.1181x; 1.0018x over previous
import math
import numpy as np
import ml_dtypes
import concourse.bass as bass
import concourse.mybir as mybir
from concourse.bass_utils import run_bass_kernel_spmd

F32 = mybir.dt.float32
BF16 = mybir.dt.bfloat16
ALU = mybir.AluOpType
ACT = mybir.ActivationFunctionType
AX = mybir.AxisListType

M1 = 12582912.0
M2 = 6755399441055744.0

ENGINES = ('t', 'v', 's', 'g', 'y')


class Op:
    __slots__ = ('eng', 'fn', 'reads', 'writes', 'idx', 'is_dma', 'tag',
                 'seq', 'signals', 'waits', 'drain_before', 'accum', 'label',
                 '_sigvals', 'clock')

    def __init__(self, eng, fn, reads, writes, is_dma, tag, seq, accum, label):
        self.eng = eng
        self.fn = fn
        self.reads = reads
        self.writes = writes
        self.is_dma = is_dma
        self.tag = tag
        self.seq = seq
        self.accum = accum
        self.label = label
        self.idx = -1
        self.signals = []
        self.waits = []
        self.drain_before = False
        self.clock = {}


class Prog:
    def __init__(self, nc):
        self.nc = nc
        self.ops = {e: [] for e in ENGINES}
        self.all_ops = []
        self.last_write = {}
        self.readers = {}
        self.dma_count = {}
        self.dma_written_res = {}
        self.barrier_count = 0
        self.pair_used = set()
        self.tag_used = set()
        self.gidx = {}
        self.pair_water = {}
        self.dma_water = {}
        self.drain_cover = {e: -1 for e in ENGINES}

    def _collect(self, targets, op):
        eprods = []
        dprods = []
        for t in targets:
            if t is None or t[0] == 'B':
                continue
            if t[0] == 'E':
                eprods.append(t[1])
            else:
                dprods.append(t)
        eprods.sort(key=lambda p: -self.gidx[id(p)])
        ck = op.clock
        for prod in eprods:
            if prod.eng == op.eng and op.eng == 't':
                continue
            if ck.get(prod.eng, -1) >= prod.idx:
                continue
            key = (prod.eng, op.eng)
            self.pair_used.add(key)
            prod.signals.append(key)
            op.waits.append(('P', key, prod))
            for k, v in prod.clock.items():
                if ck.get(k, -1) < v:
                    ck[k] = v
            if ck.get(prod.eng, -1) < prod.idx:
                ck[prod.eng] = prod.idx
        for (_, tag, seq, dop) in dprods:
            if ck.get(('D', tag), 0) >= seq:
                continue
            self.tag_used.add(tag)
            op.waits.append(('D', tag, 16 * seq))
            ck[('D', tag)] = seq
            if dop is not None:
                for k, v in dop.clock.items():
                    if ck.get(k, -1) < v:
                        ck[k] = v

    def record(self, eng, fn, reads=(), writes=(), is_dma=False, dma_res=None,
               accum=None, label=''):
        reads = list(reads)
        writes = list(writes)
        tag = seq = None
        if is_dma:
            assert dma_res is not None
            tag = dma_res
            seq = self.dma_count.get(tag, 0) + 1
            self.dma_count[tag] = seq
        op = Op(eng, fn, reads, writes, is_dma, tag, seq, accum, label)
        op.idx = len(self.ops[eng])
        self.gidx[id(op)] = len(self.all_ops)
        targets = []
        for r in reads:
            targets.append(self.last_write.get(r))
        for r in writes:
            lw = self.last_write.get(r)
            if lw is not None and not (accum is not None and lw[0] == 'E'
                                       and lw[1].accum == accum):
                targets.append(lw)
            for rd in self.readers.get(r, ()):
                targets.append(rd)
        self._collect(targets, op)
        self.ops[eng].append(op)
        self.all_ops.append(op)
        me = ('D', tag, seq, op) if is_dma else ('E', op)
        for r in reads:
            self.readers.setdefault(r, []).append(me)
        for r in writes:
            self.last_write[r] = me
            self.readers[r] = []
        return op

    def dma(self, queue_eng, out_ap, in_ap, reads, writes, label=''):
        def fn(eng, _o=out_ap, _i=in_ap):
            return eng.dma_start(out=_o, in_=_i)
        return self.record(queue_eng, fn, reads=reads, writes=writes,
                           is_dma=True, dma_res=writes[0], label=label)

    def barrier(self):
        self.barrier_count += 1
        k = self.barrier_count
        pending = [(tag, 16 * cnt) for tag, cnt in self.dma_count.items()]
        for tag, val in pending:
            self.tag_used.add(tag)

        def mk(eng_name):
            def fn(eng, _k=k, _pending=pending, _e=eng_name):
                if _e == 'y':
                    for tag, val in _pending:
                        if self.dma_water2.get(('y', tag), 0) < val:
                            eng.wait_ge(self._tag_sem[tag], val)
                            self.dma_water2[('y', tag)] = val
                eng.drain()
                eng.nop().then_inc(self._bar_sem, 1)
                eng.wait_ge(self._bar_sem, 5 * _k)
                return None
            return fn
        for e in ENGINES:
            op = Op(e, mk(e), [], [], False, None, None, None, f'barrier{k}')
            op.idx = len(self.ops[e])
            self.ops[e].append(op)
            self.all_ops.append(op)
        for r in self.last_write:
            self.last_write[r] = ('B',)
        self.readers = {}
        for e in ENGINES:
            self.drain_cover[e] = len(self.ops[e]) - 1

    def materialize(self, stack):
        nc = self.nc
        self._bar_sem = stack.enter_context(nc.semaphore('bar'))
        self._pair_sem = {}
        for key in sorted(self.pair_used):
            self._pair_sem[key] = stack.enter_context(
                nc.semaphore(f'p_{key[0]}{key[1]}'))
        self._tag_sem = {}
        for tag in sorted(self.tag_used):
            self._tag_sem[tag] = stack.enter_context(nc.semaphore(f'd_{tag}'))
        pair_counter = {k: 0 for k in self.pair_used}
        for e in ENGINES:
            for op in self.ops[e]:
                sig = sorted(set(op.signals))
                op.signals = sig
                op._sigvals = {}
                for key in sig:
                    pair_counter[key] += 1
                    op._sigvals[key] = pair_counter[key]
        blk = self.nc.Block()
        self._block = blk

        def emit_engine(eng_handle, ename):
            for op in self.ops[ename]:
                for w in op.waits:
                    if w[0] == 'P':
                        _, key, prod = w
                        val = prod._sigvals[key]
                        if self.pair_water2.get((ename, key), 0) < val:
                            eng_handle.wait_ge(self._pair_sem[key], val)
                            self.pair_water2[(ename, key)] = val
                    else:
                        _, tag, val = w
                        if self.dma_water2.get((ename, tag), 0) < val:
                            eng_handle.wait_ge(self._tag_sem[tag], val)
                            self.dma_water2[(ename, tag)] = val
                if op.drain_before:
                    eng_handle.drain()
                inst = op.fn(eng_handle)
                if op.is_dma:
                    assert inst is not None, f'dma op {op.label} returned None'
                    inst.then_inc(self._tag_sem[op.tag], 16)
                    if op.signals:
                        for key in op.signals:
                            eng_handle.nop().then_inc(self._pair_sem[key], 1)
                elif op.signals:
                    if inst is not None:
                        inst.then_inc(self._pair_sem[op.signals[0]], 1)
                        rest = op.signals[1:]
                    else:
                        rest = op.signals
                    if rest:
                        eng_handle.drain()
                        for key in rest:
                            eng_handle.nop().then_inc(self._pair_sem[key], 1)

        self.pair_water2 = {}
        self.dma_water2 = {}
        with blk as block:
            block.tensor(lambda e: emit_engine(e, 't'))
            block.vector(lambda e: emit_engine(e, 'v'))
            block.scalar(lambda e: emit_engine(e, 's'))
            block.gpsimd(lambda e: emit_engine(e, 'g'))
            block.sync(lambda e: emit_engine(e, 'y'))


class Buf:
    __slots__ = ('mem', 'byte_start', 'cols', 'name', 'dtype', 'freed', 't',
                 'nalloc')

    def __init__(self, mem, byte_start, cols, name, dtype):
        self.mem = mem
        self.byte_start = byte_start
        self.cols = cols
        self.name = name
        self.dtype = dtype
        self.freed = False
        mem.uid += 1
        self.t = mem.nc.alloc_sbuf_tensor_at(
            f'c{mem.uid}_{name}', [128, cols], dtype,
            offset=mem.addr + byte_start)

    def ap(self, c0=0, c1=None, p0=0, p1=128):
        assert not self.freed, self.name
        if c1 is None:
            c1 = self.cols
        assert 0 <= c0 <= c1 <= self.cols, (self.name, c0, c1, self.cols)
        return self.t.ap()[p0:p1, c0:c1]

    def ap3(self, start, stride, nblk, width, p0=0, p1=128):
        assert not self.freed, self.name
        assert start + (nblk - 1) * stride + width <= self.cols
        assert p0 == 0 and p1 == 128
        import dataclasses
        full = self.t.ap()
        row_stride = full.ap[0][0]
        return dataclasses.replace(
            full, offset=full.offset + start,
            ap=type(full.ap)([[row_stride, 128], [stride, nblk], [1, width]]))

    def free(self):
        assert not self.freed
        self.mem._free(self.byte_start, self.nalloc)
        self.freed = True


class Mem:
    def __init__(self, nc, nbytes):
        self.nc = nc
        self.nbytes = nbytes
        self.uid = 0
        base = nc.alloc_sbuf_tensor('arena8', [128, nbytes], mybir.dt.uint8)
        self.addr = nc.lookup_mloc(base).addr
        assert self.addr % 64 == 0, self.addr
        self.free_list = [(0, nbytes)]

    def _alloc(self, n, align=64):
        if n <= 2048:
            for i in range(len(self.free_list) - 1, -1, -1):
                st, sz = self.free_list[i]
                a = (st + sz - n) // align * align
                if a >= st:
                    repl = [p for p in ((st, a - st), (a + n, st + sz - a - n))
                            if p[1] > 0]
                    self.free_list[i:i + 1] = repl
                    return a
        for i, (st, sz) in enumerate(self.free_list):
            a = (st + align - 1) // align * align
            if a + n <= st + sz:
                repl = [p for p in ((st, a - st), (a + n, st + sz - a - n))
                        if p[1] > 0]
                self.free_list[i:i + 1] = repl
                return a
        used = self.nbytes - sum(sz for _, sz in self.free_list)
        raise MemoryError(f'arena: cannot alloc {n}B (used={used}B, '
                          f'free={self.free_list[:6]})')

    def _free(self, start, n):
        self.free_list.append((start, n))
        self.free_list.sort()
        merged = []
        for st, sz in self.free_list:
            if merged and merged[-1][0] + merged[-1][1] == st:
                merged[-1] = (merged[-1][0], merged[-1][1] + sz)
            else:
                merged.append((st, sz))
        self.free_list = merged

    def f32(self, name, cols):
        al = 64 if cols > 8 else 32
        n = (cols * 4 + al - 1) // al * al
        b = Buf(self, self._alloc(n, al), cols, name, F32)
        b.nalloc = n
        return b

    def bf16(self, name, cols):
        al = 64 if cols > 16 else 32
        n = (cols * 2 + al - 1) // al * al
        b = Buf(self, self._alloc(n, al), cols, name, BF16)
        b.nalloc = n
        return b


def ceil_div(a, b):
    return (a + b - 1) // b




def ceil_div(a, b):
    return (a + b - 1) // b


def _ring_pieces(g0, length, RK):
    out = []
    g = g0
    rem = length
    while rem > 0:
        r = g % RK
        ln = min(rem, RK - r)
        out.append((g, r, ln))
        g += ln
        rem -= ln
    return out


def _chunk_rows(d_ap, nchunk):
    return d_ap.rearrange('o (j p) -> (o p) j', p=128)


def build_core(cfg):
    S = cfg['S']
    D = cfg['D']
    H = cfg['H']
    NSEQ = cfg['NSEQ']
    TF = cfg['TF']
    NT = S * NSEQ
    n_dc = D // 128
    n_hc = H // 128
    TPG = TF // 128
    NHG = S // TF
    ntile_seq = S // 128
    ntiles = NT // 128
    RK = TF + 128
    PH = cfg.get('PHASES', 'full')

    nc = bass.Bass('TRN2', target_bir_lowering=False,
                   dynamic_dma_scratch_size=64)

    d_x = nc.dram_tensor('x_in', [NT, D], F32, kind='ExternalInput').ap()
    d_w1 = nc.dram_tensor('w1', [H, D], F32, kind='ExternalInput').ap()
    d_w2 = nc.dram_tensor('w2', [D, H], F32, kind='ExternalInput').ap()
    d_cw = [nc.dram_tensor(f'cw{i}', [128, H // 128], F32,
                           kind='ExternalInput').ap() for i in range(3)]
    d_cb = nc.dram_tensor('cb', [128, H // 128], F32,
                          kind='ExternalInput').ap()
    d_wx = {w: nc.dram_tensor('w' + w, [D, D], F32, kind='ExternalInput').ap()
            for w in 'qkvo'}
    d_bx = {w: nc.dram_tensor('b' + w, [1, D], F32, kind='ExternalInput').ap()
            for w in 'qkvo'}
    d_ident = nc.dram_tensor('ident', [128, 128], BF16,
                             kind='ExternalInput').ap()
    d_out = nc.dram_tensor('out', [NT, D], F32, kind='ExternalOutput').ap()

    d_x1 = nc.dram_tensor('x1d', [NT, D], F32).ap()
    d_x2 = nc.dram_tensor('x2d', [NT, D], F32).ap()
    d_m1T = nc.dram_tensor('m1Td', [128, n_dc * H], BF16).ap()
    d_m2T = nc.dram_tensor('m2Td', [128, n_hc * D], BF16).ap()
    d_lnh = nc.dram_tensor('lnhd', [ntiles, n_dc, 128, 128], BF16).ap()
    d_lnl = nc.dram_tensor('lnld', [ntiles, n_dc, 128, 128], BF16).ap()
    d_qh = nc.dram_tensor('qhd', [ntiles, n_dc, 128, 128], BF16).ap()
    d_ql = nc.dram_tensor('qld', [ntiles, n_dc, 128, 128], BF16).ap()
    d_wT = {}
    for w in 'qkvo':
        for hl in 'hl':
            d_wT[w + hl] = nc.dram_tensor(f'w{w}T{hl}d', [128, n_dc * D],
                                          BF16).ap()

    psum_mm = [nc.alloc_psum_tensor(f'pmm{i}', [128, 512], F32)
               for i in range(6)]
    psum_tb = nc.alloc_psum_tensor('ptb', [128, 1024], BF16)
    psum_st = nc.alloc_psum_tensor('pst', [128, 512], F32)

    mem = Mem(nc, nc.sbuf_bytes_remaining // 64 * 64)
    P = Prog(nc)

    ident = mem.bf16('ident', 128)
    onesr = mem.f32('onesr', 128)
    onesc = mem.f32('onesc', 1)
    magic = mem.f32('magic', 1)
    kapb = mem.f32('kap', 1)
    cdesc1 = mem.f32('cdesc1', 1)
    cdesc2 = mem.f32('cdesc2', 1)
    convc = [mem.f32(f'convc{i}', n_hc) for i in range(3)]
    convb = mem.f32('convb', n_hc)
    eps_b = mem.f32('epsb', 1)

    identf = mem.f32('identf', 128)
    i_ap = ident.ap()
    if_ap = identf.ap()
    ones_row = onesr.ap(p0=0, p1=1)

    P.dma('y', i_ap, d_ident, reads=['d_ident'], writes=['ident'],
          label='ld_ident')
    P.record('v', lambda e, o=if_ap, i=i_ap: e.tensor_copy(o, i),
             reads=['ident'], writes=['identf'])
    for i in range(3):
        P.dma('y', convc[i].ap(), d_cw[i], reads=[f'd_cw{i}'],
              writes=[f'convc{i}'], label=f'ld_cw{i}')
    P.dma('y', convb.ap(), d_cb, reads=['d_cb'], writes=['convb'],
          label='ld_cb')
    P.record('v', lambda e, o=onesr.ap(p0=0, p1=1): e.memset(o, 1.0),
             writes=['ones1'])
    P.record('v', lambda e, o=onesc.ap(): e.memset(o, 1.0), writes=['onesc'])
    P.record('v', lambda e, o=eps_b.ap(): e.memset(o, 1e-5), writes=['epsb'])
    kap_ap = kapb.ap()
    magic_ap = magic.ap()
    P.record('v', lambda e, o=kap_ap: e.memset(o, 1.5), writes=['kap'])
    P.record('v', lambda e, o=kap_ap: e.tensor_scalar(o, o, M2, M2, ALU.add,
             ALU.subtract), reads=['kap'], writes=['kap'])
    P.record('v', lambda e, o=magic_ap, i=kap_ap: e.tensor_scalar(
        o, i, (M2 - M1) / 2.0, M1, ALU.mult, ALU.add),
        reads=['kap'], writes=['magic'])

    def bcast_row(row_ap, row_res, out_ap, out_res, n):
        ps = psum_st.ap()[:, 0:n]
        P.record('t', lambda e, o=ps, l=ones_row, r=row_ap: e.matmul(
            o, l, r, start=True, stop=True), reads=['ones1', row_res],
            writes=['ps_st'], label=f'bc_{out_res}')
        P.record('v', lambda e, o=out_ap, i=ps: e.tensor_copy(o, i),
                 reads=['ps_st'], writes=[out_res])

    def col2row(col_ap, col_res, row_ap, row_res, wp=128):
        ps = psum_st.ap()[0:1, 0:wp]
        P.record('t', lambda e, o=ps, i=col_ap, idn=if_ap[0:wp, 0:wp]:
                 e.transpose(o, i, idn),
                 reads=[col_res, 'identf'], writes=['ps_st'])
        P.record('v', lambda e, o=row_ap, i=ps: e.tensor_copy(o, i),
                 reads=['ps_st'], writes=[row_res])

    def row2col(row_ap, row_res, col_ap, col_res, wp=128):
        ps = psum_st.ap()[0:wp, 0:1]
        P.record('t', lambda e, o=ps, i=row_ap, idn=if_ap[0:1, 0:1]:
                 e.transpose(o, i, idn),
                 reads=[row_res, 'identf'], writes=['ps_st'])
        P.record('v', lambda e, o=col_ap, i=ps: e.tensor_copy(o, i),
                 reads=['ps_st'], writes=[col_res])

    def tr_tile(src_aps, src_res, dst3_ap, dst_res, nblk, key, evac='v'):
        for c in range(nblk):
            pt = psum_tb.ap()[:, c * 128:(c + 1) * 128]
            P.record('t', lambda e, o=pt, i=src_aps[c], idn=i_ap:
                     e.transpose(o, i, idn), reads=[src_res, 'ident'],
                     writes=['ps_tb'], accum=key)
        src = psum_tb.ap()[:, 0:nblk * 128].rearrange('p (c n) -> p c n',
                                                      c=nblk)
        P.record(evac, lambda e, o=dst3_ap, i=src: e.tensor_copy(o, i),
                 reads=['ps_tb'], writes=[dst_res])

    def emit_wquant(d_w, rows, cols, d_mT, dmT_res, cdesc_buf, cdesc_res,
                    wname):
        n_rt = rows // 128
        n_cc = cols // 128
        stage = mem.f32(f'{wname}_stage', n_rt * cols)
        acc = mem.f32(f'{wname}_acc', 1)
        red = mem.f32(f'{wname}_red', 1)
        scl = mem.f32(f'{wname}_scl', 1)
        mq = mem.bf16(f'{wname}_mq', cols)
        mT_sb = mem.bf16(f'{wname}_mT', n_cc * rows)
        sres = f'{wname}_stage'
        for t in range(n_rt):
            P.dma('y', stage.ap(t * cols, (t + 1) * cols),
                  d_w[t * 128:(t + 1) * 128, :], reads=[f'd_{wname}'],
                  writes=[sres], label=f'{wname}_ld{t}')
        for t in range(n_rt):
            sap = stage.ap(t * cols, (t + 1) * cols)
            P.record('v', lambda e, o=red.ap(), i=sap: e.reduce_sum(
                o, i, AX.X, apply_absolute_value=True),
                reads=[sres], writes=[f'{wname}_red'])
            if t == 0:
                P.record('v', lambda e, o=acc.ap(), i=red.ap():
                         e.tensor_copy(o, i), reads=[f'{wname}_red'],
                         writes=[f'{wname}_acc'])
            else:
                P.record('v', lambda e, o=acc.ap(), i=red.ap():
                         e.tensor_add(o, o, i),
                         reads=[f'{wname}_red', f'{wname}_acc'],
                         writes=[f'{wname}_acc'])
        ps = psum_st.ap()[0:1, 0:1]
        P.record('t', lambda e, o=ps, l=acc.ap(), r=onesc.ap(): e.matmul(
            o, l, r, start=True, stop=True),
            reads=[f'{wname}_acc', 'onesc'], writes=['ps_st'])
        sc1 = scl.ap(p0=0, p1=1)[:, 0:1]
        P.record('v', lambda e, o=sc1, i=ps: e.tensor_scalar(
            o, i, 1.0 / (rows * cols), 1e-5, ALU.mult, ALU.max),
            reads=['ps_st'], writes=[f'{wname}_c'])
        P.record('v', lambda e, o=sc1: e.reciprocal(o, o),
                 reads=[f'{wname}_c'], writes=[f'{wname}_c'])
        bcast_row(sc1, f'{wname}_c', scl.ap(), f'{wname}_s128', 1)
        P.record('v', lambda e, o=cdesc_buf.ap(), i=scl.ap():
                 e.reciprocal(o, i), reads=[f'{wname}_s128'],
                 writes=[cdesc_res])
        for t in range(n_rt):
            sap = stage.ap(t * cols, (t + 1) * cols)
            P.record('v', lambda e, o=sap, s=scl.ap(), m=magic_ap:
                     e.tensor_scalar(o, o, s, m, ALU.mult, ALU.add),
                     reads=[sres, f'{wname}_s128', 'magic'], writes=[sres])
            P.record('v', lambda e, o=sap, m=magic_ap: e.tensor_scalar(
                o, o, m, None, ALU.subtract), reads=[sres, 'magic'],
                writes=[sres])
            P.record('v', lambda e, o=mq.ap(), i=sap: e.tensor_scalar(
                o, i, -1.0, 1.0, ALU.max, ALU.min),
                reads=[sres], writes=[f'{wname}_mq'])
            nb = 8 if n_cc >= 8 else n_cc
            for cb in range(0, n_cc, nb):
                blks = [mq.ap(c * 128, (c + 1) * 128)
                        for c in range(cb, cb + nb)]
                dst3 = mT_sb.ap3(cb * rows + t * 128, rows, nb, 128)
                tr_tile(blks, f'{wname}_mq', dst3, f'{wname}_mT', nb,
                        (wname, t, cb))
        P.dma('s', d_mT, mT_sb.ap(), reads=[f'{wname}_mT'], writes=[dmT_res],
              label=f'{wname}_spill')
        stage.free(); acc.free(); red.free(); scl.free(); mq.free()
        mT_sb.free()

    def emit_ffn(in_ap, out_ap, in_res, out_res, fid):
        m1T = mem.bf16(f'{fid}_m1T', n_dc * H)
        m2T = mem.bf16(f'{fid}_m2T', n_hc * D)
        nq = min(4, n_dc)
        for q in range(nq):
            lo, hi = q * (n_dc // nq) * H, (q + 1) * (n_dc // nq) * H
            P.dma('y', m1T.ap(lo, hi), d_m1T[:, lo:hi], reads=['d_w1mT'],
                  writes=[f'{fid}m1Tq{q}'], label=f'{fid}_m1Tld{q}')
        nq2 = min(4, n_hc)
        for q in range(nq2):
            lo, hi = q * (n_hc // nq2) * D, (q + 1) * (n_hc // nq2) * D
            P.dma('y', m2T.ap(lo, hi), d_m2T[:, lo:hi], reads=['d_w2mT'],
                  writes=[f'{fid}m2Tq{q}'], label=f'{fid}_m2Tld{q}')

        ygrp = mem.f32(f'{fid}_ygrp', n_hc * (TF + 1))
        k2r = mem.bf16(f'{fid}_k2r', n_hc * RK)
        xr = [mem.f32(f'{fid}_xr{i}', D) for i in range(3)]
        sq = [mem.f32(f'{fid}_sq{i}', D) for i in range(1)]
        k1fm = [mem.bf16(f'{fid}_k1fm{i}', n_dc * TF) for i in range(2)]
        k1tm = [mem.bf16(f'{fid}_k1tm{i}', D) for i in range(1)]
        Bs = [mem.f32(f'{fid}_Bs{i}', TF + 3) for i in range(3)]
        g1row = mem.f32(f'{fid}_g1row', TF)
        g1bc = [mem.f32(f'{fid}_g1bc{i}', TF) for i in range(1)]
        carry = mem.f32(f'{fid}_carry', 2 * n_hc)
        yab = [mem.f32(f'{fid}_yab{i}', TF + 1) for i in range(1)]
        maxacc = mem.f32(f'{fid}_maxacc', TF + 1)
        s2row = mem.f32(f'{fid}_s2row', TF + 1)
        fring = mem.f32(f'{fid}_fring', RK)
        s2bc = [mem.f32(f'{fid}_s2bc{i}', TF + 1) for i in range(1)]
        stt_ = [mem.f32(f'{fid}_stt{i}', 24) for i in range(2)]

        gctr = 0
        for seq in range(NSEQ):
            done_tiles = 0
            for hg in range(NHG):
                t0 = hg * TF
                ws = max(t0 - 1, 0)
                we = S if hg == NHG - 1 else t0 + TF - 1
                W = we - ws
                ridx = gctr % 2
                for i in range(TPG):
                    a_glob = (seq * S + t0) // 128 + i
                    xi = xr[(gctr * TPG + i) % 3]
                    xires = f'{fid}_xr{(gctr * TPG + i) % 3}'
                    P.dma('y', xi.ap(), in_ap[a_glob * 128:(a_glob + 1) * 128, :],
                          reads=[in_res], writes=[xires],
                          label=f'{fid}_xld{a_glob}')
                    st_i = stt_[i % 2]
                    rS = f'{fid}_stt{i % 2}'
                    sq_i, rQ = sq[0], f'{fid}_sq0'
                    k1_i, rK = k1tm[0], f'{fid}_k1tm0'
                    cm = lambda c: st_i.ap(c, c + 1)
                    P.record('v', lambda e, o=cm(0), x=xi.ap(): e.reduce_sum(
                        o, x, AX.X), reads=[xires], writes=[rS + 'a'])
                    P.record('v', lambda e, o=cm(1), i_=cm(0): e.tensor_scalar(
                        o, i_, -1.0 / D, None, ALU.mult), reads=[rS + 'a'],
                        writes=[rS + 'm'])
                    P.record('s', lambda e, o=sq_i.ap(), x=xi.ap(), b=cm(1),
                             a=cm(2): e.activation(o, x, ACT.Square, bias=b,
                             accum_out=a),
                             reads=[xires, rS + 'm'], writes=[rQ, rS + 'v'])
                    P.record('v', lambda e, o=cm(3), i_=cm(2): e.tensor_scalar(
                        o, i_, 1.0 / D, None, ALU.mult), reads=[rS + 'v'],
                        writes=[rS + 'v2'])
                    P.record('s', lambda e, o=cm(4), i_=cm(3), b=eps_b.ap():
                             e.activation(o, i_, ACT.Sqrt, bias=b),
                             reads=[rS + 'v2', 'epsb'], writes=[rS + 'sd'])
                    P.record('v', lambda e, o=cm(5), i_=cm(4): e.reciprocal(o, i_),
                             reads=[rS + 'sd'], writes=[rS + 'r'])
                    P.record('v', lambda e, o=cm(6), i_=xi.ap(): e.reduce_max(
                        o, i_, AX.X), reads=[xires], writes=[rS + 'x'])
                    P.record('v', lambda e, o=cm(11), i_=xi.ap():
                             e.tensor_reduce(o, i_, AX.X, ALU.min),
                             reads=[xires], writes=[rS + 'n'])
                    P.record('v', lambda e, o=cm(6), m=cm(1):
                             e.tensor_scalar(o, o, m, None, ALU.add),
                             reads=[rS + 'x', rS + 'm'], writes=[rS + 'x'])
                    P.record('v', lambda e, o=cm(11), m=cm(1):
                             e.tensor_scalar(o, o, m, -1.0, ALU.add, ALU.mult),
                             reads=[rS + 'n', rS + 'm'], writes=[rS + 'n'])
                    P.record('v', lambda e, o=cm(6), i1=cm(11):
                             e.tensor_tensor(o, o, i1, ALU.max),
                             reads=[rS + 'x', rS + 'n'], writes=[rS + 'x'])
                    P.record('v', lambda e, o=cm(7), i_=cm(6), r=cm(5):
                             e.tensor_scalar(o, i_, r, 1e-5, ALU.mult, ALU.max),
                             reads=[rS + 'x', rS + 'r'], writes=[rS + 'u'])
                    P.record('v', lambda e, o=cm(8), i_=cm(7): e.reciprocal(o, i_),
                             reads=[rS + 'u'], writes=[rS + 'u2'])
                    P.record('v', lambda e, o=cm(9), i_=cm(8), r=cm(5):
                             e.tensor_scalar(o, i_, 127.0, r, ALU.mult, ALU.mult),
                             reads=[rS + 'u2', rS + 'r'], writes=[rS + 's1'])
                    P.record('v', lambda e, o=sq_i.ap(), x=xi.ap(), m=cm(1),
                             s=cm(9): e.tensor_scalar(o, x, m, s, ALU.add,
                             ALU.mult), reads=[xires, rS + 'm', rS + 's1'],
                             writes=[rQ])
                    P.record('v', lambda e, o=k1_i.ap(), i_=sq_i.ap(),
                             m=magic_ap: e.tensor_scalar(o, i_, m, m, ALU.add,
                             ALU.subtract), reads=[rQ, 'magic'], writes=[rK])
                    P.record('v', lambda e, o=cm(10), i_=cm(7), c=cdesc1.ap():
                             e.tensor_scalar(o, i_, c, 1.0 / 127.0, ALU.mult,
                             ALU.mult), reads=[rS + 'u', 'w1_cdesc'],
                             writes=[rS + 'g'])
                    col2row(cm(10), rS + 'g',
                            g1row.ap(i * 128, (i + 1) * 128, p0=0, p1=1),
                            f'{fid}_g1row', 128)
                    blks = [k1_i.ap(dc * 128, (dc + 1) * 128)
                            for dc in range(n_dc)]
                    dst3 = k1fm[ridx].ap3(i * 128, TF, n_dc, 128)
                    tr_tile(blks, rK, dst3, f'{fid}_k1fm{ridx}', n_dc,
                            (fid, 'k1t', gctr, i))
                bcast_row(g1row.ap(0, TF, p0=0, p1=1), f'{fid}_g1row',
                          g1bc[0].ap(0, TF), f'{fid}_g1bc0', TF)
                co = ws - t0 + 2

                def stA(j):
                    pb = psum_mm[j % 2]
                    pbres = f'pb{j % 2}'
                    for dc in range(n_dc):
                        P.record('t', lambda e, o=pb.ap()[:, 0:TF],
                                 l=m1T.ap(dc * H + j * 128,
                                          dc * H + (j + 1) * 128),
                                 r=k1fm[ridx].ap(dc * TF, (dc + 1) * TF),
                                 st=(dc == 0), sp=(dc == n_dc - 1):
                                 e.matmul(o, l, r, start=st, stop=sp),
                                 reads=[f'{fid}m1Tq{dc * nq // n_dc}',
                                        f'{fid}_k1fm{ridx}'],
                                 writes=[pbres], accum=(fid, 1, seq, hg, j),
                                 label=f'{fid}_mm1_{seq}_{hg}_{j}')

                def stB(j):
                    pb = psum_mm[j % 2]
                    pbres = f'pb{j % 2}'
                    Bj, Bres = Bs[j % 3], f'{fid}_Bs{j % 3}'
                    if hg == 0:
                        P.record('v', lambda e, o=Bj.ap(0, 2): e.memset(o, 0.0),
                                 writes=[Bres])
                    else:
                        P.record('v', lambda e, o=Bj.ap(0, 2),
                                 i_=carry.ap(2 * j, 2 * j + 2):
                                 e.tensor_copy(o, i_), reads=[f'{fid}_carry'],
                                 writes=[Bres])
                    if hg == NHG - 1:
                        P.record('v', lambda e, o=Bj.ap(TF + 2, TF + 3):
                                 e.memset(o, 0.0), writes=[Bres])
                    P.record('v', lambda e, o=Bj.ap(2, TF + 2),
                             i0=pb.ap()[:, 0:TF], i1=g1bc[0].ap(0, TF):
                             e.tensor_tensor(o, i0, i1, ALU.mult),
                             reads=[pbres, f'{fid}_g1bc0'], writes=[Bres])
                    if hg < NHG - 1:
                        P.record('v', lambda e, o=carry.ap(2 * j, 2 * j + 2),
                                 i_=Bj.ap(TF, TF + 2): e.tensor_copy(o, i_),
                                 reads=[Bres], writes=[f'{fid}_carry'])

                def stC(j):
                    Bj, Bres = Bs[j % 3], f'{fid}_Bs{j % 3}'
                    yj = ygrp.ap(j * (TF + 1), j * (TF + 1) + W)
                    yjres = f'{fid}_y{j}'
                    P.record('s', lambda e, o=yj, i_=Bj.ap(co - 1, co - 1 + W),
                             c=convc[0].ap(j, j + 1), b=convb.ap(j, j + 1):
                             e.activation(o, i_, ACT.Identity, bias=b, scale=c),
                             reads=[Bres, 'convc0', 'convb'], writes=[yjres])

                def stD(j):
                    Bj, Bres = Bs[j % 3], f'{fid}_Bs{j % 3}'
                    yj = ygrp.ap(j * (TF + 1), j * (TF + 1) + W)
                    yjres = f'{fid}_y{j}'
                    P.record('v', lambda e, o=yj, i0=Bj.ap(co, co + W),
                             c=convc[1].ap(j, j + 1), i1=yj:
                             e.scalar_tensor_tensor(o, i0, c, i1, ALU.mult,
                             ALU.add), reads=[Bres, 'convc1', yjres],
                             writes=[yjres])
                    P.record('v', lambda e, o=yj, i0=Bj.ap(co + 1, co + 1 + W),
                             c=convc[2].ap(j, j + 1), i1=yj:
                             e.scalar_tensor_tensor(o, i0, c, i1, ALU.mult,
                             ALU.add), reads=[Bres, 'convc2', yjres],
                             writes=[yjres])

                def stE(j):
                    yj = ygrp.ap(j * (TF + 1), j * (TF + 1) + W)
                    yjres = f'{fid}_y{j}'
                    ya, yares = yab[0], f'{fid}_yab0'
                    P.record('s', lambda e, o=yj: e.activation(o, o, ACT.Silu),
                             reads=[yjres], writes=[yjres])
                    P.record('s', lambda e, o=ya.ap(0, W), i_=yj: e.activation(
                        o, i_, ACT.Abs), reads=[yjres], writes=[yares])
                    if j == 0:
                        P.record('v', lambda e, o=maxacc.ap(0, W),
                                 i_=ya.ap(0, W): e.tensor_copy(o, i_),
                                 reads=[yares], writes=[f'{fid}_mx'])
                    else:
                        P.record('v', lambda e, o=maxacc.ap(0, W),
                                 i_=ya.ap(0, W): e.tensor_tensor(o, o, i_,
                                 ALU.max), reads=[yares, f'{fid}_mx'],
                                 writes=[f'{fid}_mx'])

                for jj in range(n_hc + 4):
                    if jj < n_hc:
                        stA(jj)
                    if 0 <= jj - 1 < n_hc:
                        stB(jj - 1)
                    if 0 <= jj - 2 < n_hc:
                        stC(jj - 2)
                    if 0 <= jj - 3 < n_hc:
                        stD(jj - 3)
                    if 0 <= jj - 4 < n_hc:
                        stE(jj - 4)
                for w in range(ceil_div(W, 128)):
                    wp = min(128, W - w * 128)
                    pst = psum_st.ap()[0:wp, 0:128]
                    P.record('t', lambda e, o=pst, i_=maxacc.ap(
                        w * 128, w * 128 + wp), idn=if_ap: e.transpose(
                        o, i_, idn), reads=[f'{fid}_mx', 'identf'],
                        writes=['ps_st'])
                    P.record('v', lambda e, o=stt_[0].ap(16, 17, p1=wp),
                             i_=pst: e.reduce_max(o, i_, AX.X),
                             reads=['ps_st'], writes=['am_tm'])
                    P.record('v', lambda e, o=stt_[0].ap(17, 18, p1=wp),
                             i_=stt_[0].ap(16, 17, p1=wp): e.tensor_scalar(
                             o, i_, 1e-5, None, ALU.max), reads=['am_tm'],
                             writes=['u2f'])
                    P.record('v', lambda e, o=stt_[0].ap(18, 19, p1=wp),
                             i_=stt_[0].ap(17, 18, p1=wp): e.reciprocal(o, i_),
                             reads=['u2f'], writes=['r2f'])
                    P.record('v', lambda e, o=stt_[0].ap(19, 20, p1=wp),
                             i_=stt_[0].ap(18, 19, p1=wp): e.tensor_scalar(
                             o, i_, 127.0, None, ALU.mult), reads=['r2f'],
                             writes=['s2tm'])
                    P.record('v', lambda e, o=stt_[0].ap(20, 21, p1=wp),
                             i_=stt_[0].ap(17, 18, p1=wp),
                             c=cdesc2.ap(0, 1, p1=wp): e.tensor_scalar(
                             o, i_, c, 0.5 / 127.0, ALU.mult, ALU.mult),
                             reads=['u2f', 'w2_cdesc'], writes=['ftm0'])
                    col2row(stt_[0].ap(19, 20, p1=wp), 's2tm',
                            s2row.ap(w * 128, w * 128 + wp, p0=0, p1=1),
                            f'{fid}_s2row', wp)
                    col2row(stt_[0].ap(20, 21, p1=wp), 'ftm0',
                            yab[0].ap(0, wp, p0=0, p1=1), f'{fid}_yab0', wp)
                    for (gg, rr, ln) in _ring_pieces(ws + w * 128, wp, RK):
                        o0 = gg - ws - w * 128
                        P.record('v', lambda e, o=fring.ap(rr, rr + ln, p0=0,
                                 p1=1), i_=yab[0].ap(o0, o0 + ln, p0=0, p1=1):
                                 e.tensor_copy(o, i_),
                                 reads=[f'{fid}_yab0'], writes=['fring'])
                bcast_row(s2row.ap(0, W, p0=0, p1=1), f'{fid}_s2row',
                          s2bc[0].ap(0, W), f'{fid}_s2bc0', W)
                for j in range(n_hc):
                    yj = ygrp.ap(j * (TF + 1), j * (TF + 1) + W)
                    ya, yares = yab[0], f'{fid}_yab0'
                    P.record('v', lambda e, o=ya.ap(0, W), i0=yj,
                             i1=s2bc[0].ap(0, W): e.tensor_tensor(
                             o, i0, i1, ALU.mult),
                             reads=[f'{fid}_y{j}', f'{fid}_s2bc0'],
                             writes=[yares])
                    for (gg, rr, ln) in _ring_pieces(ws, W, RK):
                        P.record('v', lambda e, o=k2r.ap(j * RK + rr,
                                 j * RK + rr + ln), i_=ya.ap(gg - ws,
                                 gg - ws + ln), m=magic_ap: e.tensor_scalar(
                                 o, i_, m, m, ALU.add, ALU.subtract),
                                 reads=[yares, 'magic'],
                                 writes=[f'{fid}_k2r{j}'])
                ready = S // 128 if hg == NHG - 1 else (we + 1 - 128) // 128
                for a in range(done_tiles, ready):
                    a_glob = seq * ntile_seq + a
                    ga = seq * ntile_seq + a
                    pbt = [psum_mm[2 + 2 * (a % 2)], psum_mm[3 + 2 * (a % 2)]]
                    pbr = [f'pb{2 + 2 * (a % 2)}', f'pb{3 + 2 * (a % 2)}']
                    xsi, xsres = xr[ga % 3], f'{fid}_xr{ga % 3}'
                    rc = (a * 128) % RK
                    nhalf = ceil_div(D, 512)
                    for j in range(n_hc):
                        for h2 in range(nhalf):
                            nn = min(512, D - h2 * 512)
                            P.record('t', lambda e, o=pbt[h2].ap()[:, 0:nn],
                                     l=k2r.ap(j * RK + rc, j * RK + rc + 128),
                                     r=m2T.ap(j * D + h2 * 512,
                                              j * D + h2 * 512 + nn),
                                     st=(j == 0), sp=(j == n_hc - 1):
                                     e.matmul(o, l, r, start=st, stop=sp),
                                     reads=[f'{fid}_k2r{j}',
                                            f'{fid}m2Tq{j * nq2 // n_hc}'],
                                     writes=[pbr[h2]],
                                     accum=(fid, 2, seq, a, h2),
                                     label=f'{fid}_mm2_{seq}_{a}_{j}_{h2}')
                    fi, fires = None, f'{fid}_ftm{a % 2}'
                    row2col(fring.ap(rc, rc + 128, p0=0, p1=1), 'fring',
                            stt_[a % 2].ap(22, 23), fires, 128)
                    for h2 in range(nhalf):
                        nn = min(512, D - h2 * 512)
                        P.record('v', lambda e, o=xsi.ap(h2 * 512, h2 * 512 + nn),
                                 i0=pbt[h2].ap()[:, 0:nn],
                                 f=stt_[a % 2].ap(22, 23):
                                 e.scalar_tensor_tensor(o, i0, f, o, ALU.mult,
                                 ALU.add), reads=[pbr[h2], fires, xsres],
                                 writes=[xsres])
                    P.dma('s', out_ap[a_glob * 128:(a_glob + 1) * 128, :],
                          xsi.ap(), reads=[xsres], writes=[out_res],
                          label=f'{fid}_xst{a_glob}')
                done_tiles = ready
                gctr += 1
        for b in (m1T, m2T, g1row, carry, ygrp, maxacc, s2row, fring, k2r):
            b.free()
        for lst in (xr, sq, stt_, k1tm, k1fm, g1bc, Bs, yab, s2bc):
            for b in lst:
                b.free()

    n_kc = S // 128
    n_kg = ceil_div(S, 512)
    nhalf = ceil_div(D, 512)

    class Acc:
        def __init__(self, total):
            self.total = total
            self.i = 0

        def nxt(self):
            st = self.i == 0
            sp = self.i == self.total - 1
            self.i += 1
            return st, sp

    def emit_wprep(w):
        wth = mem.bf16(f'a_{w}th', n_dc * D)
        wtl = mem.bf16(f'a_{w}tl', n_dc * D)
        stg = [mem.f32(f'a_{w}stg{i}', D) for i in range(2)]
        hib = [mem.bf16(f'a_{w}hi{i}', D) for i in range(2)]
        lob = [mem.bf16(f'a_{w}lo{i}', D) for i in range(2)]
        for dt in range(n_dc):
            sg, rs = stg[dt % 2], f'a_{w}stg{dt % 2}'
            hi, rh = hib[dt % 2], f'a_{w}hi{dt % 2}'
            lo, rl = lob[dt % 2], f'a_{w}lo{dt % 2}'
            P.dma('y', sg.ap(), d_wx[w][dt * 128:(dt + 1) * 128, :],
                  reads=[f'd_w{w}'], writes=[rs], label=f'{w}ld{dt}')
            P.record('v', lambda e, o=hi.ap(), i=sg.ap(): e.tensor_copy(o, i),
                     reads=[rs], writes=[rh])
            P.record('v', lambda e, o=lo.ap(), i0=sg.ap(), i1=hi.ap():
                     e.tensor_tensor(o, i0, i1, ALU.subtract),
                     reads=[rs, rh], writes=[rl])
            for src, dst, res_s, res_d in ((hi, wth, rh, f'a_{w}th'),
                                           (lo, wtl, rl, f'a_{w}tl')):
                nb = min(8, n_dc)
                for cb in range(0, n_dc, nb):
                    blks = [src.ap(ec * 128, (ec + 1) * 128)
                            for ec in range(cb, cb + nb)]
                    dst3 = dst.ap3(cb * D + dt * 128, D, nb, 128)
                    tr_tile(blks, res_s, dst3, res_d, nb, (w, dt, cb, res_d))
        P.dma('s', d_wT[w + 'h'], wth.ap(), reads=[f'a_{w}th'],
              writes=[f'd_wT{w}h'], label=f'{w}spillh')
        P.dma('s', d_wT[w + 'l'], wtl.ap(), reads=[f'a_{w}tl'],
              writes=[f'd_wT{w}l'], label=f'{w}spilll')
        for b in stg + hib + lob + [wth, wtl]:
            b.free()

    def ld_wt(w, bufs):
        for hl, b in zip('hl', bufs):
            P.dma('y', b.ap(), d_wT[w + hl], reads=[f'd_wT{w}{hl}'],
                  writes=[f'aw_{w}{hl}'], label=f'ld_wT{w}{hl}')

    def bias_bcast(w, buf, row=None):
        P.dma('y', buf.ap(0, D, p0=0, p1=1), d_bx[w], reads=[f'd_b{w}'],
              writes=['a_bb'], label=f'ld_b{w}')
        for h2 in range(nhalf):
            nn = min(512, D - h2 * 512)
            bcast_row(buf.ap(h2 * 512, h2 * 512 + nn, p0=0, p1=1),
                      'a_bb', buf.ap(h2 * 512, h2 * 512 + nn),
                      'a_bb', nn)

    def emit_ln_split(xi, xires, st_i, rS, sq_i, rQ, lnf, rLF,
                      lnhi, rLH, lnlo, rLL):
        cm = lambda c: st_i.ap(c, c + 1)
        P.record('v', lambda e, o=cm(0), x=xi.ap(): e.reduce_sum(
            o, x, AX.X), reads=[xires], writes=[rS + 'a'])
        P.record('v', lambda e, o=cm(1), i_=cm(0): e.tensor_scalar(
            o, i_, -1.0 / D, None, ALU.mult), reads=[rS + 'a'],
            writes=[rS + 'm'])
        P.record('s', lambda e, o=sq_i.ap(), x=xi.ap(), b=cm(1), a=cm(2):
                 e.activation(o, x, ACT.Square, bias=b, accum_out=a),
                 reads=[xires, rS + 'm'], writes=[rQ, rS + 'v'])
        P.record('v', lambda e, o=cm(3), i_=cm(2): e.tensor_scalar(
            o, i_, 1.0 / D, None, ALU.mult), reads=[rS + 'v'],
            writes=[rS + 'v2'])
        P.record('s', lambda e, o=cm(4), i_=cm(3), b=eps_b.ap():
                 e.activation(o, i_, ACT.Sqrt, bias=b),
                 reads=[rS + 'v2', 'epsb'], writes=[rS + 'sd'])
        P.record('v', lambda e, o=cm(5), i_=cm(4): e.reciprocal(o, i_),
                 reads=[rS + 'sd'], writes=[rS + 'r'])
        P.record('v', lambda e, o=lnf.ap(), x=xi.ap(), m=cm(1), r=cm(5):
                 e.tensor_scalar(o, x, m, r, ALU.add, ALU.mult),
                 reads=[xires, rS + 'm', rS + 'r'], writes=[rLF])
        P.record('v', lambda e, o=lnhi.ap(), i_=lnf.ap(): e.tensor_copy(o, i_),
                 reads=[rLF], writes=[rLH])
        P.record('v', lambda e, o=lnlo.ap(), i0=lnf.ap(), i1=lnhi.ap():
                 e.tensor_tensor(o, i0, i1, ALU.subtract),
                 reads=[rLF, rLH], writes=[rLL])

    def emit_attn_all():
        for w in 'qkvo':
            emit_wprep(w)
            P.barrier()
        Khi = mem.bf16('a_Khi', n_dc * S)
        Klo = mem.bf16('a_Klo', n_dc * S)
        Vhi = mem.bf16('a_Vhi', n_kc * D)
        Vlo = mem.bf16('a_Vlo', n_kc * D)
        bbuf = mem.f32('a_bb', D)
        brow = None
        bb = {w: bbuf for w in 'qkvo'}
        stb = [mem.f32(f'a_stt{i}', 24) for i in range(2)]
        for seq in range(NSEQ):
            wk = [mem.bf16('a_wkh', n_dc * D), mem.bf16('a_wkl', n_dc * D)]
            ld_wt('k', wk)
            bias_bcast('k', bb['k'], brow)
            xr_ = [mem.f32(f'a_xr{i}', D) for i in range(2)]
            sq_ = [mem.f32(f'a_sq{i}', D) for i in range(1)]
            lnf_ = [mem.f32(f'a_lnf{i}', D) for i in range(1)]
            lnh_ = [mem.bf16(f'a_lnh{i}', D) for i in range(2)]
            lnl_ = [mem.bf16(f'a_lnl{i}', D) for i in range(2)]
            lnhF = [mem.bf16(f'a_lnhF{i}', n_dc * 128) for i in range(2)]
            lnlF = [mem.bf16(f'a_lnlF{i}', n_dc * 128) for i in range(2)]
            ktm = mem.f32('a_ktm', D)
            khi_t = mem.bf16('a_khit', D)
            klo_t = mem.bf16('a_klot', D)
            for i in range(n_kc):
                a_glob = seq * ntile_seq + i
                ri = i % 2
                xi, xires = xr_[ri], f'a_xr{ri}'
                P.dma('y', xi.ap(), d_x1[a_glob * 128:(a_glob + 1) * 128, :],
                      reads=['x1d'], writes=[xires], label=f'a1x{a_glob}')
                emit_ln_split(xi, xires, stb[ri], f'a_stt{ri}',
                              sq_[0], 'a_sq0', lnf_[0],
                              'a_lnf0', lnh_[ri], f'a_lnh{ri}',
                              lnl_[ri], f'a_lnl{ri}')
                for src, dstb, rs, rd in ((lnh_[ri], lnhF[ri], f'a_lnh{ri}',
                                           f'a_lnhF{ri}'),
                                          (lnl_[ri], lnlF[ri], f'a_lnl{ri}',
                                           f'a_lnlF{ri}')):
                    blks = [src.ap(dc * 128, (dc + 1) * 128)
                            for dc in range(n_dc)]
                    tr_tile(blks, rs, dstb.ap3(0, 128, n_dc, 128), rd, n_dc,
                            ('lnF', seq, i, rd))
                for dl, src, rs in ((d_lnh, lnhF[ri], f'a_lnhF{ri}'),
                                    (d_lnl, lnlF[ri], f'a_lnlF{ri}')):
                    P.dma('s', dl[a_glob].rearrange('dc p c -> p dc c'),
                          src.ap().rearrange('p (dc c) -> p dc c', dc=n_dc),
                          reads=[rs], writes=['lnd'], label=f'lnst{a_glob}')
                accs = [Acc(3 * n_dc) for _ in range(nhalf)]
                for ec in range(n_dc):
                    for sihi, sires in ((lnhF[ri], f'a_lnhF{ri}'),
                                        (lnlF[ri], f'a_lnlF{ri}')):
                        pass
                for ec in range(n_dc):
                    lh = lnhF[ri].ap(ec * 128, (ec + 1) * 128)
                    ll = lnlF[ri].ap(ec * 128, (ec + 1) * 128)
                    for h2 in range(nhalf):
                        nn = min(512, D - h2 * 512)
                        bk2 = 2 * (i % 2) + h2
                        pk, pkr = psum_mm[bk2], f'pb{bk2}'
                        for stat, statr, mov, movr in (
                                (lh, f'a_lnhF{ri}', wk[0], 'aw_kh'),
                                (lh, f'a_lnhF{ri}', wk[1], 'aw_kl'),
                                (ll, f'a_lnlF{ri}', wk[0], 'aw_kh')):
                            st, sp = accs[h2].nxt()
                            P.record('t', lambda e, o=pk.ap()[:, 0:nn], l=stat,
                                     r=mov.ap(ec * D + h2 * 512,
                                              ec * D + h2 * 512 + nn),
                                     st=st, sp=sp: e.matmul(o, l, r, start=st,
                                     stop=sp), reads=[statr, movr],
                                     writes=[pkr], accum=('K', seq, i, h2),
                                     label=f'K_{seq}_{i}_{ec}_{h2}')
                for h2 in range(nhalf):
                    nn = min(512, D - h2 * 512)
                    bk2 = 2 * (i % 2) + h2
                    P.record('v', lambda e, o=ktm.ap(h2 * 512, h2 * 512 + nn),
                             i0=psum_mm[bk2].ap()[:, 0:nn],
                             i1=bb['k'].ap(h2 * 512, h2 * 512 + nn):
                             e.tensor_tensor(o, i0, i1, ALU.add),
                             reads=[f'pb{bk2}', 'a_bb'], writes=['a_ktm'])
                P.record('v', lambda e, o=khi_t.ap(), i=ktm.ap():
                         e.tensor_copy(o, i), reads=['a_ktm'],
                         writes=['a_khit'])
                P.record('v', lambda e, o=klo_t.ap(), i0=ktm.ap(),
                         i1=khi_t.ap(): e.tensor_tensor(o, i0, i1,
                         ALU.subtract), reads=['a_ktm', 'a_khit'],
                         writes=['a_klot'])
                for src, dstb, rs, rd in ((khi_t, Khi, 'a_khit', 'a_Khi'),
                                          (klo_t, Klo, 'a_klot', 'a_Klo')):
                    blks = [src.ap(dc * 128, (dc + 1) * 128)
                            for dc in range(n_dc)]
                    tr_tile(blks, rs, dstb.ap3(i * 128, S, n_dc, 128), rd,
                            n_dc, ('KT', seq, i, rd))
            for b in wk + xr_ + sq_ + lnf_ + lnh_ + lnl_:
                b.free()
            ktm.free(); khi_t.free(); klo_t.free()
            P.barrier()
            wv = [mem.bf16('a_wvh', n_dc * D), mem.bf16('a_wvl', n_dc * D)]
            ld_wt('v', wv)
            bias_bcast('v', bb['v'], brow)
            vtm = mem.f32('a_vtm', D)
            for i in range(n_kc):
                a_glob = seq * ntile_seq + i
                ri = i % 2
                for dl, dstb, rd in ((d_lnh, lnhF[ri], f'a_lnhF{ri}'),
                                     (d_lnl, lnlF[ri], f'a_lnlF{ri}')):
                    P.dma('y', dstb.ap().rearrange('p (dc c) -> p dc c',
                          dc=n_dc), dl[a_glob].rearrange('dc p c -> p dc c'),
                          reads=['lnd'], writes=[rd], label=f'lnld{a_glob}')
                accs = [Acc(3 * n_dc) for _ in range(nhalf)]
                for ec in range(n_dc):
                    lh = lnhF[ri].ap(ec * 128, (ec + 1) * 128)
                    ll = lnlF[ri].ap(ec * 128, (ec + 1) * 128)
                    for h2 in range(nhalf):
                        nn = min(512, D - h2 * 512)
                        bk2 = 2 * (i % 2) + h2
                        pk, pkr = psum_mm[bk2], f'pb{bk2}'
                        for stat, statr, mov, movr in (
                                (lh, f'a_lnhF{ri}', wv[0], 'aw_vh'),
                                (lh, f'a_lnhF{ri}', wv[1], 'aw_vl'),
                                (ll, f'a_lnlF{ri}', wv[0], 'aw_vh')):
                            st, sp = accs[h2].nxt()
                            P.record('t', lambda e, o=pk.ap()[:, 0:nn], l=stat,
                                     r=mov.ap(ec * D + h2 * 512,
                                              ec * D + h2 * 512 + nn),
                                     st=st, sp=sp: e.matmul(o, l, r, start=st,
                                     stop=sp), reads=[statr, movr],
                                     writes=[pkr], accum=('V', seq, i, h2),
                                     label=f'V_{seq}_{i}_{ec}_{h2}')
                for h2 in range(nhalf):
                    nn = min(512, D - h2 * 512)
                    bk2 = 2 * (i % 2) + h2
                    P.record('v', lambda e, o=vtm.ap(h2 * 512, h2 * 512 + nn),
                             i0=psum_mm[bk2].ap()[:, 0:nn],
                             i1=bb['v'].ap(h2 * 512, h2 * 512 + nn):
                             e.tensor_tensor(o, i0, i1, ALU.add),
                             reads=[f'pb{bk2}', 'a_bb'], writes=['a_vtm'])
                P.record('v', lambda e, o=Vhi.ap(i * D, (i + 1) * D),
                         i=vtm.ap(): e.tensor_copy(o, i), reads=['a_vtm'],
                         writes=['a_Vhi'])
                P.record('v', lambda e, o=Vlo.ap(i * D, (i + 1) * D),
                         i0=vtm.ap(), i1=Vhi.ap(i * D, (i + 1) * D):
                         e.tensor_tensor(o, i0, i1, ALU.subtract),
                         reads=['a_vtm', 'a_Vhi'], writes=['a_Vlo'])
            for b in wv:
                b.free()
            vtm.free()
            P.barrier()
            wq = [mem.bf16('a_wqh', n_dc * D), mem.bf16('a_wql', n_dc * D)]
            ld_wt('q', wq)
            bias_bcast('q', bb['q'], brow)
            qtm = mem.f32('a_qtm', D)
            qhi_t = mem.bf16('a_qhit', D)
            qlo_t = mem.bf16('a_qlot', D)
            qF = [mem.bf16(f'a_qF{i}', n_dc * 128) for i in range(2)]
            for i in range(n_kc):
                a_glob = seq * ntile_seq + i
                ri = i % 2
                for dl, dstb, rd in ((d_lnh, lnhF[ri], f'a_lnhF{ri}'),
                                     (d_lnl, lnlF[ri], f'a_lnlF{ri}')):
                    P.dma('y', dstb.ap().rearrange('p (dc c) -> p dc c',
                          dc=n_dc), dl[a_glob].rearrange('dc p c -> p dc c'),
                          reads=['lnd'], writes=[rd], label=f'lnld2{a_glob}')
                accs = [Acc(3 * n_dc) for _ in range(nhalf)]
                for ec in range(n_dc):
                    lh = lnhF[ri].ap(ec * 128, (ec + 1) * 128)
                    ll = lnlF[ri].ap(ec * 128, (ec + 1) * 128)
                    for h2 in range(nhalf):
                        nn = min(512, D - h2 * 512)
                        bk2 = 2 * (i % 2) + h2
                        pk, pkr = psum_mm[bk2], f'pb{bk2}'
                        for stat, statr, mov, movr in (
                                (lh, f'a_lnhF{ri}', wq[0], 'aw_qh'),
                                (lh, f'a_lnhF{ri}', wq[1], 'aw_ql'),
                                (ll, f'a_lnlF{ri}', wq[0], 'aw_qh')):
                            st, sp = accs[h2].nxt()
                            P.record('t', lambda e, o=pk.ap()[:, 0:nn], l=stat,
                                     r=mov.ap(ec * D + h2 * 512,
                                              ec * D + h2 * 512 + nn),
                                     st=st, sp=sp: e.matmul(o, l, r, start=st,
                                     stop=sp), reads=[statr, movr],
                                     writes=[pkr], accum=('Q', seq, i, h2),
                                     label=f'Q_{seq}_{i}_{ec}_{h2}')
                for h2 in range(nhalf):
                    nn = min(512, D - h2 * 512)
                    bk2 = 2 * (i % 2) + h2
                    P.record('v', lambda e, o=qtm.ap(h2 * 512, h2 * 512 + nn),
                             i0=psum_mm[bk2].ap()[:, 0:nn],
                             i1=bb['q'].ap(h2 * 512, h2 * 512 + nn):
                             e.tensor_tensor(o, i0, i1, ALU.add),
                             reads=[f'pb{bk2}', 'a_bb'], writes=['a_qtm'])
                P.record('v', lambda e, o=qhi_t.ap(), i=qtm.ap():
                         e.tensor_copy(o, i), reads=['a_qtm'],
                         writes=['a_qhit'])
                P.record('v', lambda e, o=qlo_t.ap(), i0=qtm.ap(),
                         i1=qhi_t.ap(): e.tensor_tensor(o, i0, i1,
                         ALU.subtract), reads=['a_qtm', 'a_qhit'],
                         writes=['a_qlot'])
                for src, rs, dl in ((qhi_t, 'a_qhit', d_qh),
                                    (qlo_t, 'a_qlot', d_ql)):
                    qf, qfres = qF[ri], f'a_qF{ri}'
                    blks = [src.ap(dc * 128, (dc + 1) * 128)
                            for dc in range(n_dc)]
                    tr_tile(blks, rs, qf.ap3(0, 128, n_dc, 128), qfres, n_dc,
                            ('QT', seq, i, rs))
                    P.dma('s', dl[a_glob].rearrange('dc p c -> p dc c'),
                          qf.ap().rearrange('p (dc c) -> p dc c', dc=n_dc),
                          reads=[qfres], writes=['qd'], label=f'qst{a_glob}')
            for b in wq + qF + lnhF + lnlF:
                b.free()
            qtm.free(); qhi_t.free(); qlo_t.free()
            P.barrier()
            wo = [mem.bf16('a_woh', n_dc * D), mem.bf16('a_wol', n_dc * D)]
            ld_wt('o', wo)
            bias_bcast('o', bb['o'], brow)
            Pu = [mem.f32(f'a_Pu{i}', S) for i in range(2)]
            x1r = [mem.f32(f'a_x1r{i}', D) for i in range(2)]
            Phi = mem.bf16('a_Phi', S)
            Plo = mem.bf16('a_Plo', S)
            PhiT = mem.bf16('a_PhiT', n_kc * 128)
            PloT = mem.bf16('a_PloT', n_kc * 128)
            qh_ = [mem.bf16(f'a_q3h{i}', n_dc * 128) for i in range(1)]
            ql_ = [mem.bf16(f'a_q3l{i}', n_dc * 128) for i in range(1)]
            athi = mem.bf16('a_athi', D)
            atlo = mem.bf16('a_atlo', D)
            atFh = mem.bf16('a_atFh', n_dc * 128)
            atFl = mem.bf16('a_atFl', n_dc * 128)
            sc = 1.0 / math.sqrt(D)

            def st_scores(i):
                ri = i % 2
                a_glob = seq * ntile_seq + i
                for dl, dstb, rd in ((d_qh, qh_[0], 'a_q3h0'),
                                     (d_ql, ql_[0], 'a_q3l0')):
                    P.dma('y', dstb.ap().rearrange('p (dc c) -> p dc c',
                          dc=n_dc), dl[a_glob].rearrange('dc p c -> p dc c'),
                          reads=['qd'], writes=[rd], label=f'qld{a_glob}')
                P.dma('y', x1r[ri].ap(),
                      d_x1[a_glob * 128:(a_glob + 1) * 128, :],
                      reads=['x1d'], writes=[f'a_x1r{ri}'],
                      label=f'a3x{a_glob}')
                cmb = lambda c: stb[i % 2].ap(c, c + 1)
                rst = f'a_st{i % 2}'
                for g in range(n_kg):
                    nn = min(512, S - g * 512)
                    pq, pqr = psum_mm[g], f'pb{g}'
                    qacc = Acc(3 * n_dc)
                    for dc in range(n_dc):
                        qh = qh_[0].ap(dc * 128, (dc + 1) * 128)
                        ql = ql_[0].ap(dc * 128, (dc + 1) * 128)
                        for stat, statr, mov, movr in (
                                (qh, 'a_q3h0', Khi, 'a_Khi'),
                                (qh, 'a_q3h0', Klo, 'a_Klo'),
                                (ql, 'a_q3l0', Khi, 'a_Khi')):
                            st, sp = qacc.nxt()
                            P.record('t', lambda e, o=pq.ap()[:, 0:nn], l=stat,
                                     r=mov.ap(dc * S + g * 512,
                                              dc * S + g * 512 + nn),
                                     st=st, sp=sp: e.matmul(o, l, r, start=st,
                                     stop=sp), reads=[statr, movr],
                                     writes=[pqr], accum=('sc', seq, i, g),
                                     label=f'sc_{seq}_{i}_{dc}_{g}')
                    P.record('v', lambda e, o=cmb(g),
                             i_=pq.ap()[:, 0:nn]: e.reduce_max(
                             o, i_, AX.X), reads=[pqr], writes=[rst + f'm{g}'])
                    P.record('v', lambda e, o=cmb(4 + g), i_=cmb(g):
                             e.tensor_scalar(o, i_, -sc, None, ALU.mult),
                             reads=[rst + f'm{g}'], writes=[rst + f'b{g}'])
                    P.record('s', lambda e, o=Pu[i % 2].ap(g * 512,
                             g * 512 + nn), i_=pq.ap()[:, 0:nn], b=cmb(4 + g),
                             a=cmb(8 + g): e.activation(
                             o, i_, ACT.Exp, bias=b, scale=sc, accum_out=a),
                             reads=[pqr, rst + f'b{g}'],
                             writes=[f'a_Pu{i % 2}', rst + f's{g}'])

            def st_rest(i):
                ri = i % 2
                a_glob = seq * ntile_seq + i
                cmb = lambda c: stb[i % 2].ap(c, c + 1)
                rst = f'a_st{i % 2}'
                P.record('v', lambda e, o=cmb(12), i_=stb[i % 2].ap(0, n_kg):
                         e.reduce_max(o, i_, AX.X),
                         reads=[rst + f'm{g2}' for g2 in range(n_kg)],
                         writes=[rst + 'gm'])
                for g in range(n_kg):
                    P.record('v', lambda e, o=cmb(16 + g), i_=cmb(g),
                             m=cmb(12): e.tensor_scalar(o, i_, m, None,
                             ALU.subtract),
                             reads=[rst + f'm{g}', rst + 'gm'],
                             writes=[rst + f'd{g}'])
                P.record('s', lambda e, o=stb[i % 2].ap(16, 16 + n_kg),
                         i_=stb[i % 2].ap(16, 16 + n_kg): e.activation(
                         o, i_, ACT.Exp, scale=sc),
                         reads=[rst + f'd{g2}' for g2 in range(n_kg)],
                         writes=[rst + 'f'])
                P.record('v', lambda e, o=stb[i % 2].ap(8, 8 + n_kg),
                         i0=stb[i % 2].ap(8, 8 + n_kg),
                         i1=stb[i % 2].ap(16, 16 + n_kg): e.tensor_tensor(
                         o, i0, i1, ALU.mult),
                         reads=[rst + 'f'] + [rst + f's{g2}' for g2 in
                                              range(n_kg)],
                         writes=[rst + 'ws'])
                P.record('v', lambda e, o=cmb(13), i_=stb[i % 2].ap(8, 8 + n_kg):
                         e.reduce_sum(o, i_, AX.X), reads=[rst + 'ws'],
                         writes=[rst + 'tot'])
                P.record('v', lambda e, o=cmb(14), i_=cmb(13): e.reciprocal(
                    o, i_), reads=[rst + 'tot'], writes=[rst + 'r'])
                P.record('v', lambda e, o=stb[i % 2].ap(16, 16 + n_kg),
                         i0=stb[i % 2].ap(16, 16 + n_kg), s1=cmb(14):
                         e.tensor_scalar(o, i0, s1, None, ALU.mult),
                         reads=[rst + 'f', rst + 'r'], writes=[rst + 'rf'])
                for g in range(n_kg):
                    nn = min(512, S - g * 512)
                    P.record('v', lambda e, o=Pu[i % 2].ap(g * 512,
                             g * 512 + nn), s1=cmb(16 + g): e.tensor_scalar(
                             o, o, s1, None, ALU.mult),
                             reads=[f'a_Pu{i % 2}', rst + 'rf'],
                             writes=[f'a_Pu{i % 2}'])
                P.record('v', lambda e, o=Phi.ap(), i=Pu[i % 2].ap():
                         e.tensor_copy(o, i), reads=[f'a_Pu{i % 2}'],
                         writes=['a_Phi'])
                P.record('v', lambda e, o=Plo.ap(), i0=Pu[i % 2].ap(),
                         i1=Phi.ap(): e.tensor_tensor(o, i0, i1, ALU.subtract),
                         reads=[f'a_Pu{i % 2}', 'a_Phi'], writes=['a_Plo'])
                for src, rs, dstb, rd in ((Phi, 'a_Phi', PhiT, 'a_PhiT'),
                                          (Plo, 'a_Plo', PloT, 'a_PloT')):
                    nb = 8 if n_kc >= 8 else n_kc
                    for cb in range(0, n_kc, nb):
                        blks = [src.ap(kc * 128, (kc + 1) * 128)
                                for kc in range(cb, cb + nb)]
                        tr_tile(blks, rs, dstb.ap3(cb * 128, 128, nb, 128),
                                rd, nb, ('PT', seq, i, cb, rd))
                vaccs = [Acc(3 * n_kc) for _ in range(nhalf)]
                for kc in range(n_kc):
                    ph = PhiT.ap(kc * 128, (kc + 1) * 128)
                    pl = PloT.ap(kc * 128, (kc + 1) * 128)
                    for h2 in range(nhalf):
                        nn = min(512, D - h2 * 512)
                        pv = psum_mm[4 + h2] if nhalf > 1 else psum_mm[4]
                        pvr = f'pb{4 + h2}' if nhalf > 1 else 'pb4'
                        for stat, statr, mov, movr in (
                                (ph, 'a_PhiT', Vhi, 'a_Vhi'),
                                (ph, 'a_PhiT', Vlo, 'a_Vlo'),
                                (pl, 'a_PloT', Vhi, 'a_Vhi')):
                            st, sp = vaccs[h2].nxt()
                            P.record('t', lambda e, o=pv.ap()[:, 0:nn], l=stat,
                                     r=mov.ap(kc * D + h2 * 512,
                                              kc * D + h2 * 512 + nn),
                                     st=st, sp=sp: e.matmul(o, l, r, start=st,
                                     stop=sp), reads=[statr, movr],
                                     writes=[pvr], accum=('pv', seq, i, h2),
                                     label=f'pv_{seq}_{i}_{kc}_{h2}')
                for h2 in range(nhalf):
                    nn = min(512, D - h2 * 512)
                    pv = psum_mm[4 + h2] if nhalf > 1 else psum_mm[4]
                    pvr = f'pb{4 + h2}' if nhalf > 1 else 'pb4'
                    P.record('v', lambda e, o=athi.ap(h2 * 512, h2 * 512 + nn),
                             i=pv.ap()[:, 0:nn]: e.tensor_copy(o, i),
                             reads=[pvr], writes=['a_athi'])
                    P.record('v', lambda e, o=atlo.ap(h2 * 512, h2 * 512 + nn),
                             i0=pv.ap()[:, 0:nn],
                             i1=athi.ap(h2 * 512, h2 * 512 + nn):
                             e.tensor_tensor(o, i0, i1, ALU.subtract),
                             reads=[pvr, 'a_athi'], writes=['a_atlo'])
                for src, rs, dstb, rd in ((athi, 'a_athi', atFh, 'a_atFh'),
                                          (atlo, 'a_atlo', atFl, 'a_atFl')):
                    blks = [src.ap(dc * 128, (dc + 1) * 128)
                            for dc in range(n_dc)]
                    tr_tile(blks, rs, dstb.ap3(0, 128, n_dc, 128), rd, n_dc,
                            ('AT', seq, i, rd))
                oaccs = [Acc(3 * n_dc) for _ in range(nhalf)]
                for dv in range(n_dc):
                    ah = atFh.ap(dv * 128, (dv + 1) * 128)
                    al = atFl.ap(dv * 128, (dv + 1) * 128)
                    for h2 in range(nhalf):
                        nn = min(512, D - h2 * 512)
                        po = psum_mm[4 + h2] if nhalf > 1 else psum_mm[4]
                        por = f'pb{4 + h2}' if nhalf > 1 else 'pb4'
                        for stat, statr, mov, movr in (
                                (ah, 'a_atFh', wo[0], 'aw_oh'),
                                (ah, 'a_atFh', wo[1], 'aw_ol'),
                                (al, 'a_atFl', wo[0], 'aw_oh')):
                            st, sp = oaccs[h2].nxt()
                            P.record('t', lambda e, o=po.ap()[:, 0:nn], l=stat,
                                     r=mov.ap(dv * D + h2 * 512,
                                              dv * D + h2 * 512 + nn),
                                     st=st, sp=sp: e.matmul(o, l, r, start=st,
                                     stop=sp), reads=[statr, movr],
                                     writes=[por], accum=('O', seq, i, h2),
                                     label=f'O_{seq}_{i}_{dv}_{h2}')
                for h2 in range(nhalf):
                    nn = min(512, D - h2 * 512)
                    po = psum_mm[4 + h2] if nhalf > 1 else psum_mm[4]
                    por = f'pb{4 + h2}' if nhalf > 1 else 'pb4'
                    P.record('v', lambda e, o=x1r[ri].ap(h2 * 512,
                             h2 * 512 + nn), i0=po.ap()[:, 0:nn],
                             i1=x1r[ri].ap(h2 * 512, h2 * 512 + nn):
                             e.tensor_tensor(o, i0, i1, ALU.add),
                             reads=[por, f'a_x1r{ri}'],
                             writes=[f'a_x1r{ri}'])
                    P.record('v', lambda e, o=x1r[ri].ap(h2 * 512,
                             h2 * 512 + nn), i1=bb['o'].ap(h2 * 512,
                             h2 * 512 + nn): e.tensor_tensor(o, o, i1,
                             ALU.add), reads=['a_bb', f'a_x1r{ri}'],
                             writes=[f'a_x1r{ri}'])
                P.dma('s', d_x2[a_glob * 128:(a_glob + 1) * 128, :],
                      x1r[ri].ap(), reads=[f'a_x1r{ri}'], writes=['x2d'],
                      label=f'x2st{a_glob}')

            st_scores(0)
            for i in range(n_kc):
                if i + 1 < n_kc:
                    st_scores(i + 1)
                st_rest(i)
            for b in wo + qh_ + ql_ + x1r:
                b.free()
            for b in Pu + [Phi, Plo, PhiT, PloT, athi, atlo, atFh, atFl]:
                b.free()
            P.barrier()
        for b in [Khi, Klo, Vhi, Vlo, bbuf] + stb:
            b.free()

    emit_wquant(d_w1, H, D, d_m1T, 'd_w1mT', cdesc1, 'w1_cdesc', 'w1')
    P.barrier()
    emit_wquant(d_w2, D, H, d_m2T, 'd_w2mT', cdesc2, 'w2_cdesc', 'w2')
    P.barrier()
    if PH == 'f1':
        emit_ffn(d_x, d_out, 'x_in', 'outd', 'F1')
    else:
        emit_ffn(d_x, d_x1, 'x_in', 'x1d', 'F1')
        P.barrier()
        emit_attn_all()
        P.barrier()
        emit_ffn(d_x2, d_out, 'x2d', 'outd', 'F2')
    P.record('y', lambda e: None, reads=['outd'], writes=[], label='final')

    from contextlib import ExitStack
    with ExitStack() as stack:
        P.materialize(stack)
    nstats = {e: len(P.ops[e]) for e in ENGINES}
    return nc, nstats


_NCORE = 8


def _mk_inputs_common(Wq, bq, Wk, bk, Wv, bv, Wo, bo, w1, conv_w, conv_b, w2):
    f32 = np.float32
    cw = np.ascontiguousarray(np.asarray(conv_w, dtype=f32)).reshape(-1, 3)
    com = {
        'w1': np.ascontiguousarray(np.asarray(w1, dtype=f32)),
        'w2': np.ascontiguousarray(np.asarray(w2, dtype=f32)),
        'cw0': np.ascontiguousarray(cw[:, 0].reshape(-1, 128).T),
        'cw1': np.ascontiguousarray(cw[:, 1].reshape(-1, 128).T),
        'cw2': np.ascontiguousarray(cw[:, 2].reshape(-1, 128).T),
        'cb': np.ascontiguousarray(
            np.asarray(conv_b, dtype=f32).reshape(-1, 128).T),
        'wq': np.ascontiguousarray(np.asarray(Wq, dtype=f32)),
        'wk': np.ascontiguousarray(np.asarray(Wk, dtype=f32)),
        'wv': np.ascontiguousarray(np.asarray(Wv, dtype=f32)),
        'wo': np.ascontiguousarray(np.asarray(Wo, dtype=f32)),
        'bq': np.ascontiguousarray(np.asarray(bq, dtype=f32).reshape(1, -1)),
        'bk': np.ascontiguousarray(np.asarray(bk, dtype=f32).reshape(1, -1)),
        'bv': np.ascontiguousarray(np.asarray(bv, dtype=f32).reshape(1, -1)),
        'bo': np.ascontiguousarray(np.asarray(bo, dtype=f32).reshape(1, -1)),
        'ident': np.eye(128, dtype=ml_dtypes.bfloat16),
    }
    return com


def kernel(L0_attn_x, L0_attn_Wq, L0_attn_bq, L0_attn_Wk, L0_attn_bk,
           L0_attn_Wv, L0_attn_bv, L0_attn_Wo, L0_attn_bo, L0_ffn_w1,
           L0_ffn_conv_w, L0_ffn_conv_b, L0_ffn_w2):
    x = np.ascontiguousarray(np.asarray(L0_attn_x, dtype=np.float32))
    B, S, D = x.shape
    nper = B // _NCORE
    cfg = dict(S=S, D=D, H=L0_ffn_w1.shape[0], NSEQ=nper, TF=256)
    nc, _ = build_core(cfg)
    com = _mk_inputs_common(L0_attn_Wq, L0_attn_bq, L0_attn_Wk, L0_attn_bk,
                            L0_attn_Wv, L0_attn_bv, L0_attn_Wo, L0_attn_bo,
                            L0_ffn_w1, L0_ffn_conv_w, L0_ffn_conv_b, L0_ffn_w2)
    in_maps = []
    for i in range(_NCORE):
        m = dict(com)
        m['x_in'] = np.ascontiguousarray(
            x[i * nper:(i + 1) * nper].reshape(nper * S, D))
        in_maps.append(m)
    res = run_bass_kernel_spmd(nc, in_maps, core_ids=list(range(_NCORE)))
    out = np.concatenate(
        [np.asarray(r['out']).reshape(nper, S, D) for r in res.results], axis=0)
    return out.astype(np.float32)


# version 09
# speedup vs baseline: 1.1759x; 1.0517x over previous
import math
import numpy as np
import ml_dtypes
import concourse.bass as bass
import concourse.mybir as mybir
from concourse.bass_utils import run_bass_kernel_spmd

F32 = mybir.dt.float32
BF16 = mybir.dt.bfloat16
ALU = mybir.AluOpType
ACT = mybir.ActivationFunctionType
AX = mybir.AxisListType

M1 = 12582912.0
M2 = 6755399441055744.0

ENGINES = ('t', 'v', 's', 'g', 'y')


class Op:
    __slots__ = ('eng', 'fn', 'reads', 'writes', 'idx', 'is_dma', 'tag',
                 'seq', 'signals', 'waits', 'drain_before', 'accum', 'label',
                 '_sigvals', 'clock')

    def __init__(self, eng, fn, reads, writes, is_dma, tag, seq, accum, label):
        self.eng = eng
        self.fn = fn
        self.reads = reads
        self.writes = writes
        self.is_dma = is_dma
        self.tag = tag
        self.seq = seq
        self.accum = accum
        self.label = label
        self.idx = -1
        self.signals = []
        self.waits = []
        self.drain_before = False
        self.clock = {}


class Prog:
    def __init__(self, nc):
        self.nc = nc
        self.ops = {e: [] for e in ENGINES}
        self.all_ops = []
        self.last_write = {}
        self.readers = {}
        self.dma_count = {}
        self.dma_written_res = {}
        self.barrier_count = 0
        self.pair_used = set()
        self.tag_used = set()
        self.gidx = {}
        self.pair_water = {}
        self.dma_water = {}
        self.drain_cover = {e: -1 for e in ENGINES}

    def _collect(self, targets, op):
        eprods = []
        dprods = []
        for t in targets:
            if t is None or t[0] == 'B':
                continue
            if t[0] == 'E':
                eprods.append(t[1])
            else:
                dprods.append(t)
        eprods.sort(key=lambda p: -self.gidx[id(p)])
        ck = op.clock
        for prod in eprods:
            if prod.eng == op.eng and op.eng == 't':
                continue
            if ck.get(prod.eng, -1) >= prod.idx:
                continue
            key = (prod.eng, op.eng)
            self.pair_used.add(key)
            prod.signals.append(key)
            op.waits.append(('P', key, prod))
            for k, v in prod.clock.items():
                if ck.get(k, -1) < v:
                    ck[k] = v
            if ck.get(prod.eng, -1) < prod.idx:
                ck[prod.eng] = prod.idx
        for (_, tag, seq, dop) in dprods:
            if ck.get(('D', tag), 0) >= seq:
                continue
            self.tag_used.add(tag)
            op.waits.append(('D', tag, 16 * seq))
            ck[('D', tag)] = seq
            if dop is not None:
                for k, v in dop.clock.items():
                    if ck.get(k, -1) < v:
                        ck[k] = v

    def record(self, eng, fn, reads=(), writes=(), is_dma=False, dma_res=None,
               accum=None, label=''):
        reads = list(reads)
        writes = list(writes)
        tag = seq = None
        if is_dma:
            assert dma_res is not None
            tag = dma_res
            seq = self.dma_count.get(tag, 0) + 1
            self.dma_count[tag] = seq
        op = Op(eng, fn, reads, writes, is_dma, tag, seq, accum, label)
        op.idx = len(self.ops[eng])
        self.gidx[id(op)] = len(self.all_ops)
        targets = []
        for r in reads:
            targets.append(self.last_write.get(r))
        for r in writes:
            lw = self.last_write.get(r)
            if lw is not None and not (accum is not None and lw[0] == 'E'
                                       and lw[1].accum == accum):
                targets.append(lw)
            for rd in self.readers.get(r, ()):
                targets.append(rd)
        self._collect(targets, op)
        self.ops[eng].append(op)
        self.all_ops.append(op)
        me = ('D', tag, seq, op) if is_dma else ('E', op)
        for r in reads:
            self.readers.setdefault(r, []).append(me)
        for r in writes:
            self.last_write[r] = me
            self.readers[r] = []
        return op

    def dma(self, queue_eng, out_ap, in_ap, reads, writes, label=''):
        def fn(eng, _o=out_ap, _i=in_ap):
            return eng.dma_start(out=_o, in_=_i)
        return self.record(queue_eng, fn, reads=reads, writes=writes,
                           is_dma=True, dma_res=writes[0], label=label)

    def barrier(self):
        self.barrier_count += 1
        k = self.barrier_count
        pending = [(tag, 16 * cnt) for tag, cnt in self.dma_count.items()]
        for tag, val in pending:
            self.tag_used.add(tag)

        def mk(eng_name):
            def fn(eng, _k=k, _pending=pending, _e=eng_name):
                if _e == 'y':
                    for tag, val in _pending:
                        if self.dma_water2.get(('y', tag), 0) < val:
                            eng.wait_ge(self._tag_sem[tag], val)
                            self.dma_water2[('y', tag)] = val
                eng.drain()
                eng.nop().then_inc(self._bar_sem, 1)
                eng.wait_ge(self._bar_sem, 5 * _k)
                return None
            return fn
        for e in ENGINES:
            op = Op(e, mk(e), [], [], False, None, None, None, f'barrier{k}')
            op.idx = len(self.ops[e])
            self.ops[e].append(op)
            self.all_ops.append(op)
        for r in self.last_write:
            self.last_write[r] = ('B',)
        self.readers = {}
        for e in ENGINES:
            self.drain_cover[e] = len(self.ops[e]) - 1

    def materialize(self, stack):
        nc = self.nc
        self._bar_sem = stack.enter_context(nc.semaphore('bar'))
        self._pair_sem = {}
        for key in sorted(self.pair_used):
            self._pair_sem[key] = stack.enter_context(
                nc.semaphore(f'p_{key[0]}{key[1]}'))
        self._tag_sem = {}
        for tag in sorted(self.tag_used):
            self._tag_sem[tag] = stack.enter_context(nc.semaphore(f'd_{tag}'))
        pair_counter = {k: 0 for k in self.pair_used}
        for e in ENGINES:
            for op in self.ops[e]:
                sig = sorted(set(op.signals))
                op.signals = sig
                op._sigvals = {}
                for key in sig:
                    pair_counter[key] += 1
                    op._sigvals[key] = pair_counter[key]
        blk = self.nc.Block()
        self._block = blk

        def emit_engine(eng_handle, ename):
            for op in self.ops[ename]:
                for w in op.waits:
                    if w[0] == 'P':
                        _, key, prod = w
                        val = prod._sigvals[key]
                        if self.pair_water2.get((ename, key), 0) < val:
                            eng_handle.wait_ge(self._pair_sem[key], val)
                            self.pair_water2[(ename, key)] = val
                    else:
                        _, tag, val = w
                        if self.dma_water2.get((ename, tag), 0) < val:
                            eng_handle.wait_ge(self._tag_sem[tag], val)
                            self.dma_water2[(ename, tag)] = val
                if op.drain_before:
                    eng_handle.drain()
                inst = op.fn(eng_handle)
                if op.is_dma:
                    assert inst is not None, f'dma op {op.label} returned None'
                    inst.then_inc(self._tag_sem[op.tag], 16)
                    if op.signals:
                        for key in op.signals:
                            eng_handle.nop().then_inc(self._pair_sem[key], 1)
                elif op.signals:
                    if inst is not None:
                        inst.then_inc(self._pair_sem[op.signals[0]], 1)
                        rest = op.signals[1:]
                    else:
                        rest = op.signals
                    if rest:
                        eng_handle.drain()
                        for key in rest:
                            eng_handle.nop().then_inc(self._pair_sem[key], 1)

        self.pair_water2 = {}
        self.dma_water2 = {}
        with blk as block:
            block.tensor(lambda e: emit_engine(e, 't'))
            block.vector(lambda e: emit_engine(e, 'v'))
            block.scalar(lambda e: emit_engine(e, 's'))
            block.gpsimd(lambda e: emit_engine(e, 'g'))
            block.sync(lambda e: emit_engine(e, 'y'))


class Buf:
    __slots__ = ('mem', 'byte_start', 'cols', 'name', 'dtype', 'freed', 't',
                 'nalloc')

    def __init__(self, mem, byte_start, cols, name, dtype):
        self.mem = mem
        self.byte_start = byte_start
        self.cols = cols
        self.name = name
        self.dtype = dtype
        self.freed = False
        mem.uid += 1
        self.t = mem.nc.alloc_sbuf_tensor_at(
            f'c{mem.uid}_{name}', [128, cols], dtype,
            offset=mem.addr + byte_start)

    def ap(self, c0=0, c1=None, p0=0, p1=128):
        assert not self.freed, self.name
        if c1 is None:
            c1 = self.cols
        assert 0 <= c0 <= c1 <= self.cols, (self.name, c0, c1, self.cols)
        return self.t.ap()[p0:p1, c0:c1]

    def ap3(self, start, stride, nblk, width, p0=0, p1=128):
        assert not self.freed, self.name
        assert start + (nblk - 1) * stride + width <= self.cols
        assert p0 == 0 and p1 == 128
        import dataclasses
        full = self.t.ap()
        row_stride = full.ap[0][0]
        return dataclasses.replace(
            full, offset=full.offset + start,
            ap=type(full.ap)([[row_stride, 128], [stride, nblk], [1, width]]))

    def free(self):
        assert not self.freed
        self.mem._free(self.byte_start, self.nalloc)
        self.freed = True


class Mem:
    def __init__(self, nc, nbytes):
        self.nc = nc
        self.nbytes = nbytes
        self.uid = 0
        base = nc.alloc_sbuf_tensor('arena8', [128, nbytes], mybir.dt.uint8)
        self.addr = nc.lookup_mloc(base).addr
        assert self.addr % 64 == 0, self.addr
        self.free_list = [(0, nbytes)]

    def _alloc(self, n, align=64):
        if n <= 2048:
            for i in range(len(self.free_list) - 1, -1, -1):
                st, sz = self.free_list[i]
                a = (st + sz - n) // align * align
                if a >= st:
                    repl = [p for p in ((st, a - st), (a + n, st + sz - a - n))
                            if p[1] > 0]
                    self.free_list[i:i + 1] = repl
                    return a
        for i, (st, sz) in enumerate(self.free_list):
            a = (st + align - 1) // align * align
            if a + n <= st + sz:
                repl = [p for p in ((st, a - st), (a + n, st + sz - a - n))
                        if p[1] > 0]
                self.free_list[i:i + 1] = repl
                return a
        used = self.nbytes - sum(sz for _, sz in self.free_list)
        raise MemoryError(f'arena: cannot alloc {n}B (used={used}B, '
                          f'free={self.free_list[:6]})')

    def _free(self, start, n):
        self.free_list.append((start, n))
        self.free_list.sort()
        merged = []
        for st, sz in self.free_list:
            if merged and merged[-1][0] + merged[-1][1] == st:
                merged[-1] = (merged[-1][0], merged[-1][1] + sz)
            else:
                merged.append((st, sz))
        self.free_list = merged

    def f32(self, name, cols):
        al = 64 if cols > 8 else 32
        n = (cols * 4 + al - 1) // al * al
        b = Buf(self, self._alloc(n, al), cols, name, F32)
        b.nalloc = n
        return b

    def bf16(self, name, cols):
        al = 64 if cols > 16 else 32
        n = (cols * 2 + al - 1) // al * al
        b = Buf(self, self._alloc(n, al), cols, name, BF16)
        b.nalloc = n
        return b


def ceil_div(a, b):
    return (a + b - 1) // b




def ceil_div(a, b):
    return (a + b - 1) // b


def _ring_pieces(g0, length, RK):
    out = []
    g = g0
    rem = length
    while rem > 0:
        r = g % RK
        ln = min(rem, RK - r)
        out.append((g, r, ln))
        g += ln
        rem -= ln
    return out


def _chunk_rows(d_ap, nchunk):
    return d_ap.rearrange('o (j p) -> (o p) j', p=128)


def build_core(cfg):
    S = cfg['S']
    D = cfg['D']
    H = cfg['H']
    NSEQ = cfg['NSEQ']
    TF = cfg['TF']
    NT = S * NSEQ
    n_dc = D // 128
    n_hc = H // 128
    TPG = TF // 128
    NHG = S // TF
    ntile_seq = S // 128
    ntiles = NT // 128
    RK = TF + 128
    PH = cfg.get('PHASES', 'full')

    nc = bass.Bass('TRN2', target_bir_lowering=False,
                   dynamic_dma_scratch_size=64)

    d_x = nc.dram_tensor('x_in', [NT, D], F32, kind='ExternalInput').ap()
    d_w1 = nc.dram_tensor('w1', [H, D], F32, kind='ExternalInput').ap()
    d_w2 = nc.dram_tensor('w2', [D, H], F32, kind='ExternalInput').ap()
    d_cw = [nc.dram_tensor(f'cw{i}', [128, H // 128], F32,
                           kind='ExternalInput').ap() for i in range(3)]
    d_cb = nc.dram_tensor('cb', [128, H // 128], F32,
                          kind='ExternalInput').ap()
    d_wx = {w: nc.dram_tensor('w' + w, [D, D], F32, kind='ExternalInput').ap()
            for w in 'qkvo'}
    d_bx = {w: nc.dram_tensor('b' + w, [1, D], F32, kind='ExternalInput').ap()
            for w in 'qkvo'}
    d_ident = nc.dram_tensor('ident', [128, 128], BF16,
                             kind='ExternalInput').ap()
    d_out = nc.dram_tensor('out', [NT, D], F32, kind='ExternalOutput').ap()

    d_x1 = nc.dram_tensor('x1d', [NT, D], F32).ap()
    d_x2 = nc.dram_tensor('x2d', [NT, D], F32).ap()
    d_m1T = nc.dram_tensor('m1Td', [128, n_dc * H], BF16).ap()
    d_m2T = nc.dram_tensor('m2Td', [128, n_hc * D], BF16).ap()
    d_lnh = nc.dram_tensor('lnhd', [ntiles, n_dc, 128, 128], BF16).ap()
    d_lnl = nc.dram_tensor('lnld', [ntiles, n_dc, 128, 128], BF16).ap()
    d_qh = nc.dram_tensor('qhd', [ntiles, n_dc, 128, 128], BF16).ap()
    d_ql = nc.dram_tensor('qld', [ntiles, n_dc, 128, 128], BF16).ap()
    d_wT = {}
    for w in 'qkvo':
        for hl in 'hl':
            d_wT[w + hl] = nc.dram_tensor(f'w{w}T{hl}d', [128, n_dc * D],
                                          BF16).ap()

    psum_mm = [nc.alloc_psum_tensor(f'pmm{i}', [128, 512], F32)
               for i in range(6)]
    psum_tb = nc.alloc_psum_tensor('ptb', [128, 1024], BF16)
    psum_st = nc.alloc_psum_tensor('pst', [128, 512], F32)

    mem = Mem(nc, nc.sbuf_bytes_remaining // 64 * 64)
    P = Prog(nc)

    ident = mem.bf16('ident', 128)
    onesr = mem.f32('onesr', 128)
    onesc = mem.f32('onesc', 1)
    magic = mem.f32('magic', 1)
    kapb = mem.f32('kap', 1)
    cdesc1 = mem.f32('cdesc1', 1)
    cdesc2 = mem.f32('cdesc2', 1)
    convc = [mem.f32(f'convc{i}', n_hc) for i in range(3)]
    convb = mem.f32('convb', n_hc)
    eps_b = mem.f32('epsb', 1)

    identf = mem.f32('identf', 128)
    i_ap = ident.ap()
    if_ap = identf.ap()
    ones_row = onesr.ap(p0=0, p1=1)

    P.dma('y', i_ap, d_ident, reads=['d_ident'], writes=['ident'],
          label='ld_ident')
    P.record('v', lambda e, o=if_ap, i=i_ap: e.tensor_copy(o, i),
             reads=['ident'], writes=['identf'])
    for i in range(3):
        P.dma('y', convc[i].ap(), d_cw[i], reads=[f'd_cw{i}'],
              writes=[f'convc{i}'], label=f'ld_cw{i}')
    P.dma('y', convb.ap(), d_cb, reads=['d_cb'], writes=['convb'],
          label='ld_cb')
    P.record('v', lambda e, o=onesr.ap(p0=0, p1=1): e.memset(o, 1.0),
             writes=['ones1'])
    P.record('v', lambda e, o=onesc.ap(): e.memset(o, 1.0), writes=['onesc'])
    P.record('v', lambda e, o=eps_b.ap(): e.memset(o, 1e-5), writes=['epsb'])
    kap_ap = kapb.ap()
    magic_ap = magic.ap()
    P.record('v', lambda e, o=kap_ap: e.memset(o, 1.5), writes=['kap'])
    P.record('v', lambda e, o=kap_ap: e.tensor_scalar(o, o, M2, M2, ALU.add,
             ALU.subtract), reads=['kap'], writes=['kap'])
    P.record('v', lambda e, o=magic_ap, i=kap_ap: e.tensor_scalar(
        o, i, (M2 - M1) / 2.0, M1, ALU.mult, ALU.add),
        reads=['kap'], writes=['magic'])

    def bcast_row(row_ap, row_res, out_ap, out_res, n):
        ps = psum_st.ap()[:, 0:n]
        P.record('t', lambda e, o=ps, l=ones_row, r=row_ap: e.matmul(
            o, l, r, start=True, stop=True), reads=['ones1', row_res],
            writes=['ps_st'], label=f'bc_{out_res}')
        P.record('v', lambda e, o=out_ap, i=ps: e.tensor_copy(o, i),
                 reads=['ps_st'], writes=[out_res])

    def col2row(col_ap, col_res, row_ap, row_res, wp=128):
        ps = psum_st.ap()[0:1, 0:wp]
        P.record('t', lambda e, o=ps, i=col_ap, idn=if_ap[0:wp, 0:wp]:
                 e.transpose(o, i, idn),
                 reads=[col_res, 'identf'], writes=['ps_st'])
        P.record('v', lambda e, o=row_ap, i=ps: e.tensor_copy(o, i),
                 reads=['ps_st'], writes=[row_res])

    def row2col(row_ap, row_res, col_ap, col_res, wp=128):
        ps = psum_st.ap()[0:wp, 0:1]
        P.record('t', lambda e, o=ps, i=row_ap, idn=if_ap[0:1, 0:1]:
                 e.transpose(o, i, idn),
                 reads=[row_res, 'identf'], writes=['ps_st'])
        P.record('v', lambda e, o=col_ap, i=ps: e.tensor_copy(o, i),
                 reads=['ps_st'], writes=[col_res])

    def tr_tile(src_aps, src_res, dst3_ap, dst_res, nblk, key, evac='v'):
        for c in range(nblk):
            pt = psum_tb.ap()[:, c * 128:(c + 1) * 128]
            P.record('t', lambda e, o=pt, i=src_aps[c], idn=i_ap:
                     e.transpose(o, i, idn), reads=[src_res, 'ident'],
                     writes=['ps_tb'], accum=key)
        src = psum_tb.ap()[:, 0:nblk * 128].rearrange('p (c n) -> p c n',
                                                      c=nblk)
        P.record(evac, lambda e, o=dst3_ap, i=src: e.tensor_copy(o, i),
                 reads=['ps_tb'], writes=[dst_res])

    def emit_wquant(d_w, rows, cols, d_mT, dmT_res, cdesc_buf, cdesc_res,
                    wname):
        n_rt = rows // 128
        n_cc = cols // 128
        stage = mem.f32(f'{wname}_stage', n_rt * cols)
        acc = mem.f32(f'{wname}_acc', 1)
        red = mem.f32(f'{wname}_red', 1)
        scl = mem.f32(f'{wname}_scl', 1)
        mq = mem.bf16(f'{wname}_mq', cols)
        mT_sb = mem.bf16(f'{wname}_mT', n_cc * rows)
        sres = f'{wname}_stage'
        for t in range(n_rt):
            P.dma('y', stage.ap(t * cols, (t + 1) * cols),
                  d_w[t * 128:(t + 1) * 128, :], reads=[f'd_{wname}'],
                  writes=[sres], label=f'{wname}_ld{t}')
        for t in range(n_rt):
            sap = stage.ap(t * cols, (t + 1) * cols)
            P.record('v', lambda e, o=red.ap(), i=sap: e.reduce_sum(
                o, i, AX.X, apply_absolute_value=True),
                reads=[sres], writes=[f'{wname}_red'])
            if t == 0:
                P.record('v', lambda e, o=acc.ap(), i=red.ap():
                         e.tensor_copy(o, i), reads=[f'{wname}_red'],
                         writes=[f'{wname}_acc'])
            else:
                P.record('v', lambda e, o=acc.ap(), i=red.ap():
                         e.tensor_add(o, o, i),
                         reads=[f'{wname}_red', f'{wname}_acc'],
                         writes=[f'{wname}_acc'])
        ps = psum_st.ap()[0:1, 0:1]
        P.record('t', lambda e, o=ps, l=acc.ap(), r=onesc.ap(): e.matmul(
            o, l, r, start=True, stop=True),
            reads=[f'{wname}_acc', 'onesc'], writes=['ps_st'])
        sc1 = scl.ap(p0=0, p1=1)[:, 0:1]
        P.record('v', lambda e, o=sc1, i=ps: e.tensor_scalar(
            o, i, 1.0 / (rows * cols), 1e-5, ALU.mult, ALU.max),
            reads=['ps_st'], writes=[f'{wname}_c'])
        P.record('v', lambda e, o=sc1: e.reciprocal(o, o),
                 reads=[f'{wname}_c'], writes=[f'{wname}_c'])
        bcast_row(sc1, f'{wname}_c', scl.ap(), f'{wname}_s128', 1)
        P.record('v', lambda e, o=cdesc_buf.ap(), i=scl.ap():
                 e.reciprocal(o, i), reads=[f'{wname}_s128'],
                 writes=[cdesc_res])
        for t in range(n_rt):
            sap = stage.ap(t * cols, (t + 1) * cols)
            P.record('v', lambda e, o=sap, s=scl.ap(), m=magic_ap:
                     e.tensor_scalar(o, o, s, m, ALU.mult, ALU.add),
                     reads=[sres, f'{wname}_s128', 'magic'], writes=[sres])
            P.record('v', lambda e, o=sap, m=magic_ap: e.tensor_scalar(
                o, o, m, None, ALU.subtract), reads=[sres, 'magic'],
                writes=[sres])
            P.record('v', lambda e, o=mq.ap(), i=sap: e.tensor_scalar(
                o, i, -1.0, 1.0, ALU.max, ALU.min),
                reads=[sres], writes=[f'{wname}_mq'])
            nb = 8 if n_cc >= 8 else n_cc
            for cb in range(0, n_cc, nb):
                blks = [mq.ap(c * 128, (c + 1) * 128)
                        for c in range(cb, cb + nb)]
                dst3 = mT_sb.ap3(cb * rows + t * 128, rows, nb, 128)
                tr_tile(blks, f'{wname}_mq', dst3, f'{wname}_mT', nb,
                        (wname, t, cb))
        P.dma('s', d_mT, mT_sb.ap(), reads=[f'{wname}_mT'], writes=[dmT_res],
              label=f'{wname}_spill')
        stage.free(); acc.free(); red.free(); scl.free(); mq.free()
        mT_sb.free()

    def emit_ffn(in_ap, out_ap, in_res, out_res, fid):
        m1T = mem.bf16(f'{fid}_m1T', n_dc * H)
        nq2 = min(4, n_hc)
        assert nq2 % 2 == 0
        JQ2 = n_hc // nq2
        m2T = mem.bf16(f'{fid}_m2T', 2 * JQ2 * D)
        nq = min(4, n_dc)
        for q in range(nq):
            lo, hi = q * (n_dc // nq) * H, (q + 1) * (n_dc // nq) * H
            P.dma('y', m1T.ap(lo, hi), d_m1T[:, lo:hi], reads=['d_w1mT'],
                  writes=[f'{fid}m1Tq{q}'], label=f'{fid}_m1Tld{q}')
        P.dma('y', m2T.ap(0, JQ2 * D), d_m2T[:, 0:JQ2 * D], reads=['d_w2mT'],
              writes=[f'{fid}m2Ts0'], label=f'{fid}_m2Tld0')

        ygrp = mem.f32(f'{fid}_ygrp', n_hc * (TF + 1))
        k2r = mem.bf16(f'{fid}_k2r', n_hc * RK)
        xr = [mem.f32(f'{fid}_xr{i}', D) for i in range(3)]
        sq = [mem.f32(f'{fid}_sq{i}', D) for i in range(1)]
        k1fm = [mem.bf16(f'{fid}_k1fm{i}', n_dc * TF) for i in range(2)]
        k1tm = [mem.bf16(f'{fid}_k1tm{i}', D) for i in range(1)]
        Bs = [mem.f32(f'{fid}_Bs{i}', TF + 3) for i in range(4)]
        g1row = mem.f32(f'{fid}_g1row', TF)
        g1bc = [mem.f32(f'{fid}_g1bc{i}', TF) for i in range(1)]
        carry = mem.f32(f'{fid}_carry', 2 * n_hc)
        yab = [mem.f32(f'{fid}_yab{i}', TF + 1) for i in range(1)]
        maxacc = mem.f32(f'{fid}_maxacc', TF + 1)
        s2row = mem.f32(f'{fid}_s2row', TF + 1)
        fring = mem.f32(f'{fid}_fring', RK)
        s2bc = [mem.f32(f'{fid}_s2bc{i}', TF + 1) for i in range(1)]
        stt_ = [mem.f32(f'{fid}_stt{i}', 24) for i in range(2)]

        gctr = 0
        for seq in range(NSEQ):
            done_tiles = 0
            for hg in range(NHG):
                t0 = hg * TF
                ws = max(t0 - 1, 0)
                we = S if hg == NHG - 1 else t0 + TF - 1
                W = we - ws
                ridx = gctr % 2
                for i in range(TPG):
                    a_glob = (seq * S + t0) // 128 + i
                    xi = xr[(gctr * TPG + i) % 3]
                    xires = f'{fid}_xr{(gctr * TPG + i) % 3}'
                    P.dma('y', xi.ap(), in_ap[a_glob * 128:(a_glob + 1) * 128, :],
                          reads=[in_res], writes=[xires],
                          label=f'{fid}_xld{a_glob}')
                    st_i = stt_[i % 2]
                    rS = f'{fid}_stt{i % 2}'
                    sq_i, rQ = sq[0], f'{fid}_sq0'
                    k1_i, rK = k1tm[0], f'{fid}_k1tm0'
                    cm = lambda c: st_i.ap(c, c + 1)
                    P.record('v', lambda e, o=cm(0), x=xi.ap(): e.reduce_sum(
                        o, x, AX.X), reads=[xires], writes=[rS + 'a'])
                    P.record('v', lambda e, o=cm(1), i_=cm(0): e.tensor_scalar(
                        o, i_, -1.0 / D, None, ALU.mult), reads=[rS + 'a'],
                        writes=[rS + 'm'])
                    P.record('s', lambda e, o=sq_i.ap(), x=xi.ap(), b=cm(1),
                             a=cm(2): e.activation(o, x, ACT.Square, bias=b,
                             accum_out=a),
                             reads=[xires, rS + 'm'], writes=[rQ, rS + 'v'])
                    P.record('v', lambda e, o=cm(3), i_=cm(2): e.tensor_scalar(
                        o, i_, 1.0 / D, None, ALU.mult), reads=[rS + 'v'],
                        writes=[rS + 'v2'])
                    P.record('s', lambda e, o=cm(4), i_=cm(3), b=eps_b.ap():
                             e.activation(o, i_, ACT.Sqrt, bias=b),
                             reads=[rS + 'v2', 'epsb'], writes=[rS + 'sd'])
                    P.record('v', lambda e, o=cm(5), i_=cm(4): e.reciprocal(o, i_),
                             reads=[rS + 'sd'], writes=[rS + 'r'])
                    P.record('v', lambda e, o=cm(6), i_=xi.ap(): e.reduce_max(
                        o, i_, AX.X), reads=[xires], writes=[rS + 'x'])
                    P.record('v', lambda e, o=cm(11), i_=xi.ap():
                             e.tensor_reduce(o, i_, AX.X, ALU.min),
                             reads=[xires], writes=[rS + 'n'])
                    P.record('v', lambda e, o=cm(6), m=cm(1):
                             e.tensor_scalar(o, o, m, None, ALU.add),
                             reads=[rS + 'x', rS + 'm'], writes=[rS + 'x'])
                    P.record('v', lambda e, o=cm(11), m=cm(1):
                             e.tensor_scalar(o, o, m, -1.0, ALU.add, ALU.mult),
                             reads=[rS + 'n', rS + 'm'], writes=[rS + 'n'])
                    P.record('v', lambda e, o=cm(6), i1=cm(11):
                             e.tensor_tensor(o, o, i1, ALU.max),
                             reads=[rS + 'x', rS + 'n'], writes=[rS + 'x'])
                    P.record('v', lambda e, o=cm(7), i_=cm(6), r=cm(5):
                             e.tensor_scalar(o, i_, r, 1e-5, ALU.mult, ALU.max),
                             reads=[rS + 'x', rS + 'r'], writes=[rS + 'u'])
                    P.record('v', lambda e, o=cm(8), i_=cm(7): e.reciprocal(o, i_),
                             reads=[rS + 'u'], writes=[rS + 'u2'])
                    P.record('v', lambda e, o=cm(9), i_=cm(8), r=cm(5):
                             e.tensor_scalar(o, i_, 127.0, r, ALU.mult, ALU.mult),
                             reads=[rS + 'u2', rS + 'r'], writes=[rS + 's1'])
                    P.record('v', lambda e, o=sq_i.ap(), x=xi.ap(), m=cm(1),
                             s=cm(9): e.tensor_scalar(o, x, m, s, ALU.add,
                             ALU.mult), reads=[xires, rS + 'm', rS + 's1'],
                             writes=[rQ])
                    P.record('v', lambda e, o=k1_i.ap(), i_=sq_i.ap(),
                             m=magic_ap: e.tensor_scalar(o, i_, m, m, ALU.add,
                             ALU.subtract), reads=[rQ, 'magic'], writes=[rK])
                    P.record('v', lambda e, o=cm(10), i_=cm(7), c=cdesc1.ap():
                             e.tensor_scalar(o, i_, c, 1.0 / 127.0, ALU.mult,
                             ALU.mult), reads=[rS + 'u', 'w1_cdesc'],
                             writes=[rS + 'g'])
                    col2row(cm(10), rS + 'g',
                            g1row.ap(i * 128, (i + 1) * 128, p0=0, p1=1),
                            f'{fid}_g1row', 128)
                    blks = [k1_i.ap(dc * 128, (dc + 1) * 128)
                            for dc in range(n_dc)]
                    dst3 = k1fm[ridx].ap3(i * 128, TF, n_dc, 128)
                    tr_tile(blks, rK, dst3, f'{fid}_k1fm{ridx}', n_dc,
                            (fid, 'k1t', gctr, i))
                bcast_row(g1row.ap(0, TF, p0=0, p1=1), f'{fid}_g1row',
                          g1bc[0].ap(0, TF), f'{fid}_g1bc0', TF)
                co = ws - t0 + 2

                def stA(j):
                    pb = psum_mm[j % 2]
                    pbres = f'pb{j % 2}'
                    for dc in range(n_dc):
                        P.record('t', lambda e, o=pb.ap()[:, 0:TF],
                                 l=m1T.ap(dc * H + j * 128,
                                          dc * H + (j + 1) * 128),
                                 r=k1fm[ridx].ap(dc * TF, (dc + 1) * TF),
                                 st=(dc == 0), sp=(dc == n_dc - 1):
                                 e.matmul(o, l, r, start=st, stop=sp),
                                 reads=[f'{fid}m1Tq{dc * nq // n_dc}',
                                        f'{fid}_k1fm{ridx}'],
                                 writes=[pbres], accum=(fid, 1, seq, hg, j),
                                 label=f'{fid}_mm1_{seq}_{hg}_{j}')

                def stB(j):
                    pb = psum_mm[j % 2]
                    pbres = f'pb{j % 2}'
                    Bj, Bres = Bs[j % 4], f'{fid}_Bs{j % 4}'
                    if hg == 0:
                        P.record('v', lambda e, o=Bj.ap(0, 2): e.memset(o, 0.0),
                                 writes=[Bres])
                    else:
                        P.record('v', lambda e, o=Bj.ap(0, 2),
                                 i_=carry.ap(2 * j, 2 * j + 2):
                                 e.tensor_copy(o, i_), reads=[f'{fid}_carry'],
                                 writes=[Bres])
                    if hg == NHG - 1:
                        P.record('v', lambda e, o=Bj.ap(TF + 2, TF + 3):
                                 e.memset(o, 0.0), writes=[Bres])
                    P.record('v', lambda e, o=Bj.ap(2, TF + 2),
                             i0=pb.ap()[:, 0:TF], i1=g1bc[0].ap(0, TF):
                             e.tensor_tensor(o, i0, i1, ALU.mult),
                             reads=[pbres, f'{fid}_g1bc0'], writes=[Bres])

                def stC(j):
                    Bj, Bres = Bs[j % 4], f'{fid}_Bs{j % 4}'
                    yj = ygrp.ap(j * (TF + 1), j * (TF + 1) + W)
                    yjres = f'{fid}_y{j}'
                    P.record('s', lambda e, o=yj, i_=Bj.ap(co - 1, co - 1 + W),
                             c=convc[0].ap(j, j + 1), b=convb.ap(j, j + 1):
                             e.activation(o, i_, ACT.Identity, bias=b, scale=c),
                             reads=[Bres, 'convc0', 'convb'], writes=[yjres])

                def stD(j):
                    Bj, Bres = Bs[j % 4], f'{fid}_Bs{j % 4}'
                    yj = ygrp.ap(j * (TF + 1), j * (TF + 1) + W)
                    yjres = f'{fid}_y{j}'
                    P.record('v', lambda e, o=yj, i0=Bj.ap(co, co + W),
                             c=convc[1].ap(j, j + 1), i1=yj:
                             e.scalar_tensor_tensor(o, i0, c, i1, ALU.mult,
                             ALU.add), reads=[Bres, 'convc1', yjres],
                             writes=[yjres])

                def stD2(j):
                    Bj, Bres = Bs[j % 4], f'{fid}_Bs{j % 4}'
                    yj = ygrp.ap(j * (TF + 1), j * (TF + 1) + W)
                    yjres = f'{fid}_y{j}'
                    P.record('v', lambda e, o=yj, i0=Bj.ap(co + 1, co + 1 + W),
                             c=convc[2].ap(j, j + 1), i1=yj:
                             e.scalar_tensor_tensor(o, i0, c, i1, ALU.mult,
                             ALU.add), reads=[Bres, 'convc2', yjres],
                             writes=[yjres])
                    if hg < NHG - 1:
                        P.record('v', lambda e, o=carry.ap(2 * j, 2 * j + 2),
                                 i_=Bj.ap(TF, TF + 2): e.tensor_copy(o, i_),
                                 reads=[Bres], writes=[f'{fid}_carry'])

                def stE(j):
                    yj = ygrp.ap(j * (TF + 1), j * (TF + 1) + W)
                    yjres = f'{fid}_y{j}'
                    ya, yares = yab[0], f'{fid}_yab0'
                    P.record('s', lambda e, o=yj: e.activation(o, o, ACT.Silu),
                             reads=[yjres], writes=[yjres])
                    P.record('s', lambda e, o=ya.ap(0, W), i_=yj: e.activation(
                        o, i_, ACT.Abs), reads=[yjres], writes=[yares])

                def stF(j):
                    ya, yares = yab[0], f'{fid}_yab0'
                    if j == 0:
                        P.record('v', lambda e, o=maxacc.ap(0, W),
                                 i_=ya.ap(0, W): e.tensor_copy(o, i_),
                                 reads=[yares], writes=[f'{fid}_mx'])
                    else:
                        P.record('v', lambda e, o=maxacc.ap(0, W),
                                 i_=ya.ap(0, W): e.tensor_tensor(o, o, i_,
                                 ALU.max), reads=[yares, f'{fid}_mx'],
                                 writes=[f'{fid}_mx'])

                for jj in range(n_hc + 6):
                    if jj < n_hc:
                        stA(jj)
                    if 0 <= jj - 2 < n_hc:
                        stC(jj - 2)
                    if 0 <= jj - 6 < n_hc:
                        stF(jj - 6)
                    if 0 <= jj - 5 < n_hc:
                        stE(jj - 5)
                    if 0 <= jj - 3 < n_hc:
                        stD(jj - 3)
                    if 0 <= jj - 4 < n_hc:
                        stD2(jj - 4)
                    if 0 <= jj - 1 < n_hc:
                        stB(jj - 1)
                for w in range(ceil_div(W, 128)):
                    wp = min(128, W - w * 128)
                    pst = psum_st.ap()[0:wp, 0:128]
                    P.record('t', lambda e, o=pst, i_=maxacc.ap(
                        w * 128, w * 128 + wp), idn=if_ap: e.transpose(
                        o, i_, idn), reads=[f'{fid}_mx', 'identf'],
                        writes=['ps_st'])
                    P.record('v', lambda e, o=stt_[0].ap(16, 17, p1=wp),
                             i_=pst: e.reduce_max(o, i_, AX.X),
                             reads=['ps_st'], writes=['am_tm'])
                    P.record('v', lambda e, o=stt_[0].ap(17, 18, p1=wp),
                             i_=stt_[0].ap(16, 17, p1=wp): e.tensor_scalar(
                             o, i_, 1e-5, None, ALU.max), reads=['am_tm'],
                             writes=['u2f'])
                    P.record('v', lambda e, o=stt_[0].ap(18, 19, p1=wp),
                             i_=stt_[0].ap(17, 18, p1=wp): e.reciprocal(o, i_),
                             reads=['u2f'], writes=['r2f'])
                    P.record('v', lambda e, o=stt_[0].ap(19, 20, p1=wp),
                             i_=stt_[0].ap(18, 19, p1=wp): e.tensor_scalar(
                             o, i_, 127.0, None, ALU.mult), reads=['r2f'],
                             writes=['s2tm'])
                    P.record('v', lambda e, o=stt_[0].ap(20, 21, p1=wp),
                             i_=stt_[0].ap(17, 18, p1=wp),
                             c=cdesc2.ap(0, 1, p1=wp): e.tensor_scalar(
                             o, i_, c, 0.5 / 127.0, ALU.mult, ALU.mult),
                             reads=['u2f', 'w2_cdesc'], writes=['ftm0'])
                    col2row(stt_[0].ap(19, 20, p1=wp), 's2tm',
                            s2row.ap(w * 128, w * 128 + wp, p0=0, p1=1),
                            f'{fid}_s2row', wp)
                    col2row(stt_[0].ap(20, 21, p1=wp), 'ftm0',
                            yab[0].ap(0, wp, p0=0, p1=1), f'{fid}_yab0', wp)
                    for (gg, rr, ln) in _ring_pieces(ws + w * 128, wp, RK):
                        o0 = gg - ws - w * 128
                        P.record('v', lambda e, o=fring.ap(rr, rr + ln, p0=0,
                                 p1=1), i_=yab[0].ap(o0, o0 + ln, p0=0, p1=1):
                                 e.tensor_copy(o, i_),
                                 reads=[f'{fid}_yab0'], writes=['fring'])
                bcast_row(s2row.ap(0, W, p0=0, p1=1), f'{fid}_s2row',
                          s2bc[0].ap(0, W), f'{fid}_s2bc0', W)
                ready = S // 128 if hg == NHG - 1 else (we + 1 - 128) // 128
                nhalf = ceil_div(D, 512)
                tset = []
                for idx, a in enumerate(range(done_tiles, ready)):
                    bb2 = (2 + 2 * idx) % 6
                    tset.append((a, [psum_mm[bb2], psum_mm[bb2 + 1]],
                                 [f'pb{bb2}', f'pb{bb2 + 1}'],
                                 (a * 128) % RK))

                def q2(j):
                    yj = ygrp.ap(j * (TF + 1), j * (TF + 1) + W)
                    ya, yares = yab[0], f'{fid}_yab0'
                    P.record('v', lambda e, o=ya.ap(0, W), i0=yj,
                             i1=s2bc[0].ap(0, W): e.tensor_tensor(
                             o, i0, i1, ALU.mult),
                             reads=[f'{fid}_y{j}', f'{fid}_s2bc0'],
                             writes=[yares])
                    for (gg, rr, ln) in _ring_pieces(ws, W, RK):
                        P.record('v', lambda e, o=k2r.ap(j * RK + rr,
                                 j * RK + rr + ln), i_=ya.ap(gg - ws,
                                 gg - ws + ln), m=magic_ap: e.tensor_scalar(
                                 o, i_, m, m, ALU.add, ALU.subtract),
                                 reads=[yares, 'magic'],
                                 writes=[f'{fid}_k2r{j}'])

                def mm2j(j):
                    if j % JQ2 == 0:
                        q = j // JQ2
                        qn = (q + 1) % nq2
                        sl2 = qn % 2
                        if not (seq == NSEQ - 1 and hg == NHG - 1
                                and q + 1 == nq2):
                            P.dma('y', m2T.ap(sl2 * JQ2 * D,
                                  (sl2 + 1) * JQ2 * D),
                                  d_m2T[:, qn * JQ2 * D:(qn + 1) * JQ2 * D],
                                  reads=['d_w2mT'], writes=[f'{fid}m2Ts{sl2}'],
                                  label=f'{fid}_m2Tld_{seq}_{hg}_{qn}')
                    sl = (j // JQ2) % 2
                    for (a, pbt, pbr, rc) in tset:
                        for h2 in range(nhalf):
                            nn = min(512, D - h2 * 512)
                            P.record('t', lambda e, o=pbt[h2].ap()[:, 0:nn],
                                     l=k2r.ap(j * RK + rc, j * RK + rc + 128),
                                     r=m2T.ap(sl * JQ2 * D + (j % JQ2) * D
                                              + h2 * 512,
                                              sl * JQ2 * D + (j % JQ2) * D
                                              + h2 * 512 + nn),
                                     st=(j == 0), sp=(j == n_hc - 1):
                                     e.matmul(o, l, r, start=st, stop=sp),
                                     reads=[f'{fid}_k2r{j}',
                                            f'{fid}m2Ts{sl}'],
                                     writes=[pbr[h2]],
                                     accum=(fid, 2, seq, a, h2),
                                     label=f'{fid}_mm2_{seq}_{a}_{j}_{h2}')
                for j in range(n_hc + 1):
                    if j < n_hc:
                        q2(j)
                    if j >= 1:
                        mm2j(j - 1)
                for (a, pbt, pbr, rc) in tset:
                    a_glob = seq * ntile_seq + a
                    ga = seq * ntile_seq + a
                    xsi, xsres = xr[ga % 3], f'{fid}_xr{ga % 3}'
                    fi, fires = None, f'{fid}_ftm{a % 2}'
                    row2col(fring.ap(rc, rc + 128, p0=0, p1=1), 'fring',
                            stt_[a % 2].ap(22, 23), fires, 128)
                    for h2 in range(nhalf):
                        nn = min(512, D - h2 * 512)
                        P.record('v', lambda e, o=xsi.ap(h2 * 512, h2 * 512 + nn),
                                 i0=pbt[h2].ap()[:, 0:nn],
                                 f=stt_[a % 2].ap(22, 23):
                                 e.scalar_tensor_tensor(o, i0, f, o, ALU.mult,
                                 ALU.add), reads=[pbr[h2], fires, xsres],
                                 writes=[xsres])
                    P.dma('s', out_ap[a_glob * 128:(a_glob + 1) * 128, :],
                          xsi.ap(), reads=[xsres], writes=[out_res],
                          label=f'{fid}_xst{a_glob}')
                done_tiles = ready
                gctr += 1
        for b in (m1T, m2T, g1row, carry, ygrp, maxacc, s2row, fring, k2r):
            b.free()
        for lst in (xr, sq, stt_, k1tm, k1fm, g1bc, Bs, yab, s2bc):
            for b in lst:
                b.free()

    n_kc = S // 128
    n_kg = ceil_div(S, 512)
    nhalf = ceil_div(D, 512)

    class Acc:
        def __init__(self, total):
            self.total = total
            self.i = 0

        def nxt(self):
            st = self.i == 0
            sp = self.i == self.total - 1
            self.i += 1
            return st, sp

    def emit_wprep(w):
        wth = mem.bf16(f'a_{w}th', n_dc * D)
        wtl = mem.bf16(f'a_{w}tl', n_dc * D)
        stg = [mem.f32(f'a_{w}stg{i}', D) for i in range(2)]
        hib = [mem.bf16(f'a_{w}hi{i}', D) for i in range(2)]
        lob = [mem.bf16(f'a_{w}lo{i}', D) for i in range(2)]
        for dt in range(n_dc):
            sg, rs = stg[dt % 2], f'a_{w}stg{dt % 2}'
            hi, rh = hib[dt % 2], f'a_{w}hi{dt % 2}'
            lo, rl = lob[dt % 2], f'a_{w}lo{dt % 2}'
            P.dma('y', sg.ap(), d_wx[w][dt * 128:(dt + 1) * 128, :],
                  reads=[f'd_w{w}'], writes=[rs], label=f'{w}ld{dt}')
            P.record('v', lambda e, o=hi.ap(), i=sg.ap(): e.tensor_copy(o, i),
                     reads=[rs], writes=[rh])
            P.record('v', lambda e, o=lo.ap(), i0=sg.ap(), i1=hi.ap():
                     e.tensor_tensor(o, i0, i1, ALU.subtract),
                     reads=[rs, rh], writes=[rl])
            for src, dst, res_s, res_d in ((hi, wth, rh, f'a_{w}th'),
                                           (lo, wtl, rl, f'a_{w}tl')):
                nb = min(8, n_dc)
                for cb in range(0, n_dc, nb):
                    blks = [src.ap(ec * 128, (ec + 1) * 128)
                            for ec in range(cb, cb + nb)]
                    dst3 = dst.ap3(cb * D + dt * 128, D, nb, 128)
                    tr_tile(blks, res_s, dst3, res_d, nb, (w, dt, cb, res_d))
        P.dma('s', d_wT[w + 'h'], wth.ap(), reads=[f'a_{w}th'],
              writes=[f'd_wT{w}h'], label=f'{w}spillh')
        P.dma('s', d_wT[w + 'l'], wtl.ap(), reads=[f'a_{w}tl'],
              writes=[f'd_wT{w}l'], label=f'{w}spilll')
        for b in stg + hib + lob + [wth, wtl]:
            b.free()

    def ld_wt(w, bufs):
        for hl, b in zip('hl', bufs):
            P.dma('y', b.ap(), d_wT[w + hl], reads=[f'd_wT{w}{hl}'],
                  writes=[f'aw_{w}{hl}'], label=f'ld_wT{w}{hl}')

    def bias_bcast(w, buf, row=None):
        P.dma('y', buf.ap(0, D, p0=0, p1=1), d_bx[w], reads=[f'd_b{w}'],
              writes=['a_bb'], label=f'ld_b{w}')
        for h2 in range(nhalf):
            nn = min(512, D - h2 * 512)
            bcast_row(buf.ap(h2 * 512, h2 * 512 + nn, p0=0, p1=1),
                      'a_bb', buf.ap(h2 * 512, h2 * 512 + nn),
                      'a_bb', nn)

    def emit_ln_split(xi, xires, st_i, rS, sq_i, rQ, lnf, rLF,
                      lnhi, rLH, lnlo, rLL):
        cm = lambda c: st_i.ap(c, c + 1)
        P.record('v', lambda e, o=cm(0), x=xi.ap(): e.reduce_sum(
            o, x, AX.X), reads=[xires], writes=[rS + 'a'])
        P.record('v', lambda e, o=cm(1), i_=cm(0): e.tensor_scalar(
            o, i_, -1.0 / D, None, ALU.mult), reads=[rS + 'a'],
            writes=[rS + 'm'])
        P.record('s', lambda e, o=sq_i.ap(), x=xi.ap(), b=cm(1), a=cm(2):
                 e.activation(o, x, ACT.Square, bias=b, accum_out=a),
                 reads=[xires, rS + 'm'], writes=[rQ, rS + 'v'])
        P.record('v', lambda e, o=cm(3), i_=cm(2): e.tensor_scalar(
            o, i_, 1.0 / D, None, ALU.mult), reads=[rS + 'v'],
            writes=[rS + 'v2'])
        P.record('s', lambda e, o=cm(4), i_=cm(3), b=eps_b.ap():
                 e.activation(o, i_, ACT.Sqrt, bias=b),
                 reads=[rS + 'v2', 'epsb'], writes=[rS + 'sd'])
        P.record('v', lambda e, o=cm(5), i_=cm(4): e.reciprocal(o, i_),
                 reads=[rS + 'sd'], writes=[rS + 'r'])
        P.record('v', lambda e, o=lnf.ap(), x=xi.ap(), m=cm(1), r=cm(5):
                 e.tensor_scalar(o, x, m, r, ALU.add, ALU.mult),
                 reads=[xires, rS + 'm', rS + 'r'], writes=[rLF])
        P.record('v', lambda e, o=lnhi.ap(), i_=lnf.ap(): e.tensor_copy(o, i_),
                 reads=[rLF], writes=[rLH])
        P.record('v', lambda e, o=lnlo.ap(), i0=lnf.ap(), i1=lnhi.ap():
                 e.tensor_tensor(o, i0, i1, ALU.subtract),
                 reads=[rLF, rLH], writes=[rLL])

    def emit_attn_all():
        for w in 'qkvo':
            emit_wprep(w)
            P.barrier()
        Khi = mem.bf16('a_Khi', n_dc * S)
        Klo = mem.bf16('a_Klo', n_dc * S)
        Vhi = mem.bf16('a_Vhi', n_kc * D)
        Vlo = mem.bf16('a_Vlo', n_kc * D)
        bbuf = mem.f32('a_bb', D)
        brow = None
        bb = {w: bbuf for w in 'qkvo'}
        stb = [mem.f32(f'a_stt{i}', 24) for i in range(2)]
        for seq in range(NSEQ):
            wk = [mem.bf16('a_wkh', n_dc * D), mem.bf16('a_wkl', n_dc * D)]
            ld_wt('k', wk)
            bias_bcast('k', bb['k'], brow)
            xr_ = [mem.f32(f'a_xr{i}', D) for i in range(2)]
            sq_ = [mem.f32(f'a_sq{i}', D) for i in range(1)]
            lnf_ = [mem.f32(f'a_lnf{i}', D) for i in range(1)]
            lnh_ = [mem.bf16(f'a_lnh{i}', D) for i in range(2)]
            lnl_ = [mem.bf16(f'a_lnl{i}', D) for i in range(2)]
            lnhF = [mem.bf16(f'a_lnhF{i}', n_dc * 128) for i in range(2)]
            lnlF = [mem.bf16(f'a_lnlF{i}', n_dc * 128) for i in range(2)]
            ktm = mem.f32('a_ktm', D)
            khi_t = mem.bf16('a_khit', D)
            klo_t = mem.bf16('a_klot', D)
            for i in range(n_kc):
                a_glob = seq * ntile_seq + i
                ri = i % 2
                xi, xires = xr_[ri], f'a_xr{ri}'
                P.dma('y', xi.ap(), d_x1[a_glob * 128:(a_glob + 1) * 128, :],
                      reads=['x1d'], writes=[xires], label=f'a1x{a_glob}')
                emit_ln_split(xi, xires, stb[ri], f'a_stt{ri}',
                              sq_[0], 'a_sq0', lnf_[0],
                              'a_lnf0', lnh_[ri], f'a_lnh{ri}',
                              lnl_[ri], f'a_lnl{ri}')
                for src, dstb, rs, rd in ((lnh_[ri], lnhF[ri], f'a_lnh{ri}',
                                           f'a_lnhF{ri}'),
                                          (lnl_[ri], lnlF[ri], f'a_lnl{ri}',
                                           f'a_lnlF{ri}')):
                    blks = [src.ap(dc * 128, (dc + 1) * 128)
                            for dc in range(n_dc)]
                    tr_tile(blks, rs, dstb.ap3(0, 128, n_dc, 128), rd, n_dc,
                            ('lnF', seq, i, rd))
                for dl, src, rs in ((d_lnh, lnhF[ri], f'a_lnhF{ri}'),
                                    (d_lnl, lnlF[ri], f'a_lnlF{ri}')):
                    P.dma('s', dl[a_glob].rearrange('dc p c -> p dc c'),
                          src.ap().rearrange('p (dc c) -> p dc c', dc=n_dc),
                          reads=[rs], writes=['lnd'], label=f'lnst{a_glob}')
                accs = [Acc(3 * n_dc) for _ in range(nhalf)]
                for ec in range(n_dc):
                    for sihi, sires in ((lnhF[ri], f'a_lnhF{ri}'),
                                        (lnlF[ri], f'a_lnlF{ri}')):
                        pass
                for ec in range(n_dc):
                    lh = lnhF[ri].ap(ec * 128, (ec + 1) * 128)
                    ll = lnlF[ri].ap(ec * 128, (ec + 1) * 128)
                    for h2 in range(nhalf):
                        nn = min(512, D - h2 * 512)
                        bk2 = 2 * (i % 2) + h2
                        pk, pkr = psum_mm[bk2], f'pb{bk2}'
                        for stat, statr, mov, movr in (
                                (lh, f'a_lnhF{ri}', wk[0], 'aw_kh'),
                                (lh, f'a_lnhF{ri}', wk[1], 'aw_kl'),
                                (ll, f'a_lnlF{ri}', wk[0], 'aw_kh')):
                            st, sp = accs[h2].nxt()
                            P.record('t', lambda e, o=pk.ap()[:, 0:nn], l=stat,
                                     r=mov.ap(ec * D + h2 * 512,
                                              ec * D + h2 * 512 + nn),
                                     st=st, sp=sp: e.matmul(o, l, r, start=st,
                                     stop=sp), reads=[statr, movr],
                                     writes=[pkr], accum=('K', seq, i, h2),
                                     label=f'K_{seq}_{i}_{ec}_{h2}')
                for h2 in range(nhalf):
                    nn = min(512, D - h2 * 512)
                    bk2 = 2 * (i % 2) + h2
                    P.record('v', lambda e, o=ktm.ap(h2 * 512, h2 * 512 + nn),
                             i0=psum_mm[bk2].ap()[:, 0:nn],
                             i1=bb['k'].ap(h2 * 512, h2 * 512 + nn):
                             e.tensor_tensor(o, i0, i1, ALU.add),
                             reads=[f'pb{bk2}', 'a_bb'], writes=['a_ktm'])
                P.record('v', lambda e, o=khi_t.ap(), i=ktm.ap():
                         e.tensor_copy(o, i), reads=['a_ktm'],
                         writes=['a_khit'])
                P.record('v', lambda e, o=klo_t.ap(), i0=ktm.ap(),
                         i1=khi_t.ap(): e.tensor_tensor(o, i0, i1,
                         ALU.subtract), reads=['a_ktm', 'a_khit'],
                         writes=['a_klot'])
                for src, dstb, rs, rd in ((khi_t, Khi, 'a_khit', 'a_Khi'),
                                          (klo_t, Klo, 'a_klot', 'a_Klo')):
                    blks = [src.ap(dc * 128, (dc + 1) * 128)
                            for dc in range(n_dc)]
                    tr_tile(blks, rs, dstb.ap3(i * 128, S, n_dc, 128), rd,
                            n_dc, ('KT', seq, i, rd))
            for b in wk + xr_ + sq_ + lnf_ + lnh_ + lnl_:
                b.free()
            ktm.free(); khi_t.free(); klo_t.free()
            P.barrier()
            wv = [mem.bf16('a_wvh', n_dc * D), mem.bf16('a_wvl', n_dc * D)]
            ld_wt('v', wv)
            bias_bcast('v', bb['v'], brow)
            vtm = mem.f32('a_vtm', D)
            for i in range(n_kc):
                a_glob = seq * ntile_seq + i
                ri = i % 2
                for dl, dstb, rd in ((d_lnh, lnhF[ri], f'a_lnhF{ri}'),
                                     (d_lnl, lnlF[ri], f'a_lnlF{ri}')):
                    P.dma('y', dstb.ap().rearrange('p (dc c) -> p dc c',
                          dc=n_dc), dl[a_glob].rearrange('dc p c -> p dc c'),
                          reads=['lnd'], writes=[rd], label=f'lnld{a_glob}')
                accs = [Acc(3 * n_dc) for _ in range(nhalf)]
                for ec in range(n_dc):
                    lh = lnhF[ri].ap(ec * 128, (ec + 1) * 128)
                    ll = lnlF[ri].ap(ec * 128, (ec + 1) * 128)
                    for h2 in range(nhalf):
                        nn = min(512, D - h2 * 512)
                        bk2 = 2 * (i % 2) + h2
                        pk, pkr = psum_mm[bk2], f'pb{bk2}'
                        for stat, statr, mov, movr in (
                                (lh, f'a_lnhF{ri}', wv[0], 'aw_vh'),
                                (lh, f'a_lnhF{ri}', wv[1], 'aw_vl'),
                                (ll, f'a_lnlF{ri}', wv[0], 'aw_vh')):
                            st, sp = accs[h2].nxt()
                            P.record('t', lambda e, o=pk.ap()[:, 0:nn], l=stat,
                                     r=mov.ap(ec * D + h2 * 512,
                                              ec * D + h2 * 512 + nn),
                                     st=st, sp=sp: e.matmul(o, l, r, start=st,
                                     stop=sp), reads=[statr, movr],
                                     writes=[pkr], accum=('V', seq, i, h2),
                                     label=f'V_{seq}_{i}_{ec}_{h2}')
                for h2 in range(nhalf):
                    nn = min(512, D - h2 * 512)
                    bk2 = 2 * (i % 2) + h2
                    P.record('v', lambda e, o=vtm.ap(h2 * 512, h2 * 512 + nn),
                             i0=psum_mm[bk2].ap()[:, 0:nn],
                             i1=bb['v'].ap(h2 * 512, h2 * 512 + nn):
                             e.tensor_tensor(o, i0, i1, ALU.add),
                             reads=[f'pb{bk2}', 'a_bb'], writes=['a_vtm'])
                P.record('v', lambda e, o=Vhi.ap(i * D, (i + 1) * D),
                         i=vtm.ap(): e.tensor_copy(o, i), reads=['a_vtm'],
                         writes=['a_Vhi'])
                P.record('v', lambda e, o=Vlo.ap(i * D, (i + 1) * D),
                         i0=vtm.ap(), i1=Vhi.ap(i * D, (i + 1) * D):
                         e.tensor_tensor(o, i0, i1, ALU.subtract),
                         reads=['a_vtm', 'a_Vhi'], writes=['a_Vlo'])
            for b in wv:
                b.free()
            vtm.free()
            P.barrier()
            wq = [mem.bf16('a_wqh', n_dc * D), mem.bf16('a_wql', n_dc * D)]
            ld_wt('q', wq)
            bias_bcast('q', bb['q'], brow)
            qtm = mem.f32('a_qtm', D)
            qhi_t = mem.bf16('a_qhit', D)
            qlo_t = mem.bf16('a_qlot', D)
            qF = [mem.bf16(f'a_qF{i}', n_dc * 128) for i in range(2)]
            for i in range(n_kc):
                a_glob = seq * ntile_seq + i
                ri = i % 2
                for dl, dstb, rd in ((d_lnh, lnhF[ri], f'a_lnhF{ri}'),
                                     (d_lnl, lnlF[ri], f'a_lnlF{ri}')):
                    P.dma('y', dstb.ap().rearrange('p (dc c) -> p dc c',
                          dc=n_dc), dl[a_glob].rearrange('dc p c -> p dc c'),
                          reads=['lnd'], writes=[rd], label=f'lnld2{a_glob}')
                accs = [Acc(3 * n_dc) for _ in range(nhalf)]
                for ec in range(n_dc):
                    lh = lnhF[ri].ap(ec * 128, (ec + 1) * 128)
                    ll = lnlF[ri].ap(ec * 128, (ec + 1) * 128)
                    for h2 in range(nhalf):
                        nn = min(512, D - h2 * 512)
                        bk2 = 2 * (i % 2) + h2
                        pk, pkr = psum_mm[bk2], f'pb{bk2}'
                        for stat, statr, mov, movr in (
                                (lh, f'a_lnhF{ri}', wq[0], 'aw_qh'),
                                (lh, f'a_lnhF{ri}', wq[1], 'aw_ql'),
                                (ll, f'a_lnlF{ri}', wq[0], 'aw_qh')):
                            st, sp = accs[h2].nxt()
                            P.record('t', lambda e, o=pk.ap()[:, 0:nn], l=stat,
                                     r=mov.ap(ec * D + h2 * 512,
                                              ec * D + h2 * 512 + nn),
                                     st=st, sp=sp: e.matmul(o, l, r, start=st,
                                     stop=sp), reads=[statr, movr],
                                     writes=[pkr], accum=('Q', seq, i, h2),
                                     label=f'Q_{seq}_{i}_{ec}_{h2}')
                for h2 in range(nhalf):
                    nn = min(512, D - h2 * 512)
                    bk2 = 2 * (i % 2) + h2
                    P.record('v', lambda e, o=qtm.ap(h2 * 512, h2 * 512 + nn),
                             i0=psum_mm[bk2].ap()[:, 0:nn],
                             i1=bb['q'].ap(h2 * 512, h2 * 512 + nn):
                             e.tensor_tensor(o, i0, i1, ALU.add),
                             reads=[f'pb{bk2}', 'a_bb'], writes=['a_qtm'])
                P.record('v', lambda e, o=qhi_t.ap(), i=qtm.ap():
                         e.tensor_copy(o, i), reads=['a_qtm'],
                         writes=['a_qhit'])
                P.record('v', lambda e, o=qlo_t.ap(), i0=qtm.ap(),
                         i1=qhi_t.ap(): e.tensor_tensor(o, i0, i1,
                         ALU.subtract), reads=['a_qtm', 'a_qhit'],
                         writes=['a_qlot'])
                for src, rs, dl in ((qhi_t, 'a_qhit', d_qh),
                                    (qlo_t, 'a_qlot', d_ql)):
                    qf, qfres = qF[ri], f'a_qF{ri}'
                    blks = [src.ap(dc * 128, (dc + 1) * 128)
                            for dc in range(n_dc)]
                    tr_tile(blks, rs, qf.ap3(0, 128, n_dc, 128), qfres, n_dc,
                            ('QT', seq, i, rs))
                    P.dma('s', dl[a_glob].rearrange('dc p c -> p dc c'),
                          qf.ap().rearrange('p (dc c) -> p dc c', dc=n_dc),
                          reads=[qfres], writes=['qd'], label=f'qst{a_glob}')
            for b in wq + qF + lnhF + lnlF:
                b.free()
            qtm.free(); qhi_t.free(); qlo_t.free()
            P.barrier()
            wo = [mem.bf16('a_woh', n_dc * D), mem.bf16('a_wol', n_dc * D)]
            ld_wt('o', wo)
            bias_bcast('o', bb['o'], brow)
            Pu = [mem.f32(f'a_Pu{i}', S) for i in range(2)]
            x1r = [mem.f32(f'a_x1r{i}', D) for i in range(2)]
            Phi = mem.bf16('a_Phi', S)
            Plo = mem.bf16('a_Plo', S)
            PhiT = mem.bf16('a_PhiT', n_kc * 128)
            PloT = mem.bf16('a_PloT', n_kc * 128)
            qh_ = [mem.bf16(f'a_q3h{i}', n_dc * 128) for i in range(1)]
            ql_ = [mem.bf16(f'a_q3l{i}', n_dc * 128) for i in range(1)]
            athi = mem.bf16('a_athi', D)
            atlo = mem.bf16('a_atlo', D)
            atFh = mem.bf16('a_atFh', n_dc * 128)
            atFl = mem.bf16('a_atFl', n_dc * 128)
            sc = 1.0 / math.sqrt(D)

            def st_scores(i):
                ri = i % 2
                a_glob = seq * ntile_seq + i
                for dl, dstb, rd in ((d_qh, qh_[0], 'a_q3h0'),
                                     (d_ql, ql_[0], 'a_q3l0')):
                    P.dma('y', dstb.ap().rearrange('p (dc c) -> p dc c',
                          dc=n_dc), dl[a_glob].rearrange('dc p c -> p dc c'),
                          reads=['qd'], writes=[rd], label=f'qld{a_glob}')
                P.dma('y', x1r[ri].ap(),
                      d_x1[a_glob * 128:(a_glob + 1) * 128, :],
                      reads=['x1d'], writes=[f'a_x1r{ri}'],
                      label=f'a3x{a_glob}')
                cmb = lambda c: stb[i % 2].ap(c, c + 1)
                rst = f'a_st{i % 2}'
                for g in range(n_kg):
                    nn = min(512, S - g * 512)
                    pq, pqr = psum_mm[g], f'pb{g}'
                    qacc = Acc(3 * n_dc)
                    for dc in range(n_dc):
                        qh = qh_[0].ap(dc * 128, (dc + 1) * 128)
                        ql = ql_[0].ap(dc * 128, (dc + 1) * 128)
                        for stat, statr, mov, movr in (
                                (qh, 'a_q3h0', Khi, 'a_Khi'),
                                (qh, 'a_q3h0', Klo, 'a_Klo'),
                                (ql, 'a_q3l0', Khi, 'a_Khi')):
                            st, sp = qacc.nxt()
                            P.record('t', lambda e, o=pq.ap()[:, 0:nn], l=stat,
                                     r=mov.ap(dc * S + g * 512,
                                              dc * S + g * 512 + nn),
                                     st=st, sp=sp: e.matmul(o, l, r, start=st,
                                     stop=sp), reads=[statr, movr],
                                     writes=[pqr], accum=('sc', seq, i, g),
                                     label=f'sc_{seq}_{i}_{dc}_{g}')
                    P.record('v', lambda e, o=cmb(g),
                             i_=pq.ap()[:, 0:nn]: e.reduce_max(
                             o, i_, AX.X), reads=[pqr], writes=[rst + f'm{g}'])
                    P.record('v', lambda e, o=cmb(4 + g), i_=cmb(g):
                             e.tensor_scalar(o, i_, -sc, None, ALU.mult),
                             reads=[rst + f'm{g}'], writes=[rst + f'b{g}'])
                    P.record('s', lambda e, o=Pu[i % 2].ap(g * 512,
                             g * 512 + nn), i_=pq.ap()[:, 0:nn], b=cmb(4 + g),
                             a=cmb(8 + g): e.activation(
                             o, i_, ACT.Exp, bias=b, scale=sc, accum_out=a),
                             reads=[pqr, rst + f'b{g}'],
                             writes=[f'a_Pu{i % 2}', rst + f's{g}'])

            def st_rest(i):
                ri = i % 2
                a_glob = seq * ntile_seq + i
                cmb = lambda c: stb[i % 2].ap(c, c + 1)
                rst = f'a_st{i % 2}'
                P.record('v', lambda e, o=cmb(12), i_=stb[i % 2].ap(0, n_kg):
                         e.reduce_max(o, i_, AX.X),
                         reads=[rst + f'm{g2}' for g2 in range(n_kg)],
                         writes=[rst + 'gm'])
                for g in range(n_kg):
                    P.record('v', lambda e, o=cmb(16 + g), i_=cmb(g),
                             m=cmb(12): e.tensor_scalar(o, i_, m, None,
                             ALU.subtract),
                             reads=[rst + f'm{g}', rst + 'gm'],
                             writes=[rst + f'd{g}'])
                P.record('s', lambda e, o=stb[i % 2].ap(16, 16 + n_kg),
                         i_=stb[i % 2].ap(16, 16 + n_kg): e.activation(
                         o, i_, ACT.Exp, scale=sc),
                         reads=[rst + f'd{g2}' for g2 in range(n_kg)],
                         writes=[rst + 'f'])
                P.record('v', lambda e, o=stb[i % 2].ap(8, 8 + n_kg),
                         i0=stb[i % 2].ap(8, 8 + n_kg),
                         i1=stb[i % 2].ap(16, 16 + n_kg): e.tensor_tensor(
                         o, i0, i1, ALU.mult),
                         reads=[rst + 'f'] + [rst + f's{g2}' for g2 in
                                              range(n_kg)],
                         writes=[rst + 'ws'])
                P.record('v', lambda e, o=cmb(13), i_=stb[i % 2].ap(8, 8 + n_kg):
                         e.reduce_sum(o, i_, AX.X), reads=[rst + 'ws'],
                         writes=[rst + 'tot'])
                P.record('v', lambda e, o=cmb(14), i_=cmb(13): e.reciprocal(
                    o, i_), reads=[rst + 'tot'], writes=[rst + 'r'])
                P.record('v', lambda e, o=stb[i % 2].ap(16, 16 + n_kg),
                         i0=stb[i % 2].ap(16, 16 + n_kg), s1=cmb(14):
                         e.tensor_scalar(o, i0, s1, None, ALU.mult),
                         reads=[rst + 'f', rst + 'r'], writes=[rst + 'rf'])
                for g in range(n_kg):
                    nn = min(512, S - g * 512)
                    P.record('v', lambda e, o=Pu[i % 2].ap(g * 512,
                             g * 512 + nn), s1=cmb(16 + g): e.tensor_scalar(
                             o, o, s1, None, ALU.mult),
                             reads=[f'a_Pu{i % 2}', rst + 'rf'],
                             writes=[f'a_Pu{i % 2}'])
                P.record('v', lambda e, o=Phi.ap(), i=Pu[i % 2].ap():
                         e.tensor_copy(o, i), reads=[f'a_Pu{i % 2}'],
                         writes=['a_Phi'])
                P.record('v', lambda e, o=Plo.ap(), i0=Pu[i % 2].ap(),
                         i1=Phi.ap(): e.tensor_tensor(o, i0, i1, ALU.subtract),
                         reads=[f'a_Pu{i % 2}', 'a_Phi'], writes=['a_Plo'])
                for src, rs, dstb, rd in ((Phi, 'a_Phi', PhiT, 'a_PhiT'),
                                          (Plo, 'a_Plo', PloT, 'a_PloT')):
                    nb = 8 if n_kc >= 8 else n_kc
                    for cb in range(0, n_kc, nb):
                        blks = [src.ap(kc * 128, (kc + 1) * 128)
                                for kc in range(cb, cb + nb)]
                        tr_tile(blks, rs, dstb.ap3(cb * 128, 128, nb, 128),
                                rd, nb, ('PT', seq, i, cb, rd))
                vaccs = [Acc(3 * n_kc) for _ in range(nhalf)]
                for kc in range(n_kc):
                    ph = PhiT.ap(kc * 128, (kc + 1) * 128)
                    pl = PloT.ap(kc * 128, (kc + 1) * 128)
                    for h2 in range(nhalf):
                        nn = min(512, D - h2 * 512)
                        pv = psum_mm[4 + h2] if nhalf > 1 else psum_mm[4]
                        pvr = f'pb{4 + h2}' if nhalf > 1 else 'pb4'
                        for stat, statr, mov, movr in (
                                (ph, 'a_PhiT', Vhi, 'a_Vhi'),
                                (ph, 'a_PhiT', Vlo, 'a_Vlo'),
                                (pl, 'a_PloT', Vhi, 'a_Vhi')):
                            st, sp = vaccs[h2].nxt()
                            P.record('t', lambda e, o=pv.ap()[:, 0:nn], l=stat,
                                     r=mov.ap(kc * D + h2 * 512,
                                              kc * D + h2 * 512 + nn),
                                     st=st, sp=sp: e.matmul(o, l, r, start=st,
                                     stop=sp), reads=[statr, movr],
                                     writes=[pvr], accum=('pv', seq, i, h2),
                                     label=f'pv_{seq}_{i}_{kc}_{h2}')
                for h2 in range(nhalf):
                    nn = min(512, D - h2 * 512)
                    pv = psum_mm[4 + h2] if nhalf > 1 else psum_mm[4]
                    pvr = f'pb{4 + h2}' if nhalf > 1 else 'pb4'
                    P.record('v', lambda e, o=athi.ap(h2 * 512, h2 * 512 + nn),
                             i=pv.ap()[:, 0:nn]: e.tensor_copy(o, i),
                             reads=[pvr], writes=['a_athi'])
                    P.record('v', lambda e, o=atlo.ap(h2 * 512, h2 * 512 + nn),
                             i0=pv.ap()[:, 0:nn],
                             i1=athi.ap(h2 * 512, h2 * 512 + nn):
                             e.tensor_tensor(o, i0, i1, ALU.subtract),
                             reads=[pvr, 'a_athi'], writes=['a_atlo'])
                for src, rs, dstb, rd in ((athi, 'a_athi', atFh, 'a_atFh'),
                                          (atlo, 'a_atlo', atFl, 'a_atFl')):
                    blks = [src.ap(dc * 128, (dc + 1) * 128)
                            for dc in range(n_dc)]
                    tr_tile(blks, rs, dstb.ap3(0, 128, n_dc, 128), rd, n_dc,
                            ('AT', seq, i, rd))
                oaccs = [Acc(3 * n_dc) for _ in range(nhalf)]
                for dv in range(n_dc):
                    ah = atFh.ap(dv * 128, (dv + 1) * 128)
                    al = atFl.ap(dv * 128, (dv + 1) * 128)
                    for h2 in range(nhalf):
                        nn = min(512, D - h2 * 512)
                        po = psum_mm[4 + h2] if nhalf > 1 else psum_mm[4]
                        por = f'pb{4 + h2}' if nhalf > 1 else 'pb4'
                        for stat, statr, mov, movr in (
                                (ah, 'a_atFh', wo[0], 'aw_oh'),
                                (ah, 'a_atFh', wo[1], 'aw_ol'),
                                (al, 'a_atFl', wo[0], 'aw_oh')):
                            st, sp = oaccs[h2].nxt()
                            P.record('t', lambda e, o=po.ap()[:, 0:nn], l=stat,
                                     r=mov.ap(dv * D + h2 * 512,
                                              dv * D + h2 * 512 + nn),
                                     st=st, sp=sp: e.matmul(o, l, r, start=st,
                                     stop=sp), reads=[statr, movr],
                                     writes=[por], accum=('O', seq, i, h2),
                                     label=f'O_{seq}_{i}_{dv}_{h2}')
                for h2 in range(nhalf):
                    nn = min(512, D - h2 * 512)
                    po = psum_mm[4 + h2] if nhalf > 1 else psum_mm[4]
                    por = f'pb{4 + h2}' if nhalf > 1 else 'pb4'
                    P.record('v', lambda e, o=x1r[ri].ap(h2 * 512,
                             h2 * 512 + nn), i0=po.ap()[:, 0:nn],
                             i1=x1r[ri].ap(h2 * 512, h2 * 512 + nn):
                             e.tensor_tensor(o, i0, i1, ALU.add),
                             reads=[por, f'a_x1r{ri}'],
                             writes=[f'a_x1r{ri}'])
                    P.record('v', lambda e, o=x1r[ri].ap(h2 * 512,
                             h2 * 512 + nn), i1=bb['o'].ap(h2 * 512,
                             h2 * 512 + nn): e.tensor_tensor(o, o, i1,
                             ALU.add), reads=['a_bb', f'a_x1r{ri}'],
                             writes=[f'a_x1r{ri}'])
                P.dma('s', d_x2[a_glob * 128:(a_glob + 1) * 128, :],
                      x1r[ri].ap(), reads=[f'a_x1r{ri}'], writes=['x2d'],
                      label=f'x2st{a_glob}')

            st_scores(0)
            for i in range(n_kc):
                if i + 1 < n_kc:
                    st_scores(i + 1)
                st_rest(i)
            for b in wo + qh_ + ql_ + x1r:
                b.free()
            for b in Pu + [Phi, Plo, PhiT, PloT, athi, atlo, atFh, atFl]:
                b.free()
            P.barrier()
        for b in [Khi, Klo, Vhi, Vlo, bbuf] + stb:
            b.free()

    emit_wquant(d_w1, H, D, d_m1T, 'd_w1mT', cdesc1, 'w1_cdesc', 'w1')
    P.barrier()
    emit_wquant(d_w2, D, H, d_m2T, 'd_w2mT', cdesc2, 'w2_cdesc', 'w2')
    P.barrier()
    if PH == 'f1':
        emit_ffn(d_x, d_out, 'x_in', 'outd', 'F1')
    else:
        emit_ffn(d_x, d_x1, 'x_in', 'x1d', 'F1')
        P.barrier()
        emit_attn_all()
        P.barrier()
        emit_ffn(d_x2, d_out, 'x2d', 'outd', 'F2')
    P.record('y', lambda e: None, reads=['outd'], writes=[], label='final')

    from contextlib import ExitStack
    with ExitStack() as stack:
        P.materialize(stack)
    nstats = {e: len(P.ops[e]) for e in ENGINES}
    return nc, nstats


_NCORE = 8


def _mk_inputs_common(Wq, bq, Wk, bk, Wv, bv, Wo, bo, w1, conv_w, conv_b, w2):
    f32 = np.float32
    cw = np.ascontiguousarray(np.asarray(conv_w, dtype=f32)).reshape(-1, 3)
    com = {
        'w1': np.ascontiguousarray(np.asarray(w1, dtype=f32)),
        'w2': np.ascontiguousarray(np.asarray(w2, dtype=f32)),
        'cw0': np.ascontiguousarray(cw[:, 0].reshape(-1, 128).T),
        'cw1': np.ascontiguousarray(cw[:, 1].reshape(-1, 128).T),
        'cw2': np.ascontiguousarray(cw[:, 2].reshape(-1, 128).T),
        'cb': np.ascontiguousarray(
            np.asarray(conv_b, dtype=f32).reshape(-1, 128).T),
        'wq': np.ascontiguousarray(np.asarray(Wq, dtype=f32)),
        'wk': np.ascontiguousarray(np.asarray(Wk, dtype=f32)),
        'wv': np.ascontiguousarray(np.asarray(Wv, dtype=f32)),
        'wo': np.ascontiguousarray(np.asarray(Wo, dtype=f32)),
        'bq': np.ascontiguousarray(np.asarray(bq, dtype=f32).reshape(1, -1)),
        'bk': np.ascontiguousarray(np.asarray(bk, dtype=f32).reshape(1, -1)),
        'bv': np.ascontiguousarray(np.asarray(bv, dtype=f32).reshape(1, -1)),
        'bo': np.ascontiguousarray(np.asarray(bo, dtype=f32).reshape(1, -1)),
        'ident': np.eye(128, dtype=ml_dtypes.bfloat16),
    }
    return com


def kernel(L0_attn_x, L0_attn_Wq, L0_attn_bq, L0_attn_Wk, L0_attn_bk,
           L0_attn_Wv, L0_attn_bv, L0_attn_Wo, L0_attn_bo, L0_ffn_w1,
           L0_ffn_conv_w, L0_ffn_conv_b, L0_ffn_w2):
    x = np.ascontiguousarray(np.asarray(L0_attn_x, dtype=np.float32))
    B, S, D = x.shape
    nper = B // _NCORE
    cfg = dict(S=S, D=D, H=L0_ffn_w1.shape[0], NSEQ=nper, TF=256)
    nc, _ = build_core(cfg)
    com = _mk_inputs_common(L0_attn_Wq, L0_attn_bq, L0_attn_Wk, L0_attn_bk,
                            L0_attn_Wv, L0_attn_bv, L0_attn_Wo, L0_attn_bo,
                            L0_ffn_w1, L0_ffn_conv_w, L0_ffn_conv_b, L0_ffn_w2)
    in_maps = []
    for i in range(_NCORE):
        m = dict(com)
        m['x_in'] = np.ascontiguousarray(
            x[i * nper:(i + 1) * nper].reshape(nper * S, D))
        in_maps.append(m)
    res = run_bass_kernel_spmd(nc, in_maps, core_ids=list(range(_NCORE)))
    out = np.concatenate(
        [np.asarray(r['out']).reshape(nper, S, D) for r in res.results], axis=0)
    return out.astype(np.float32)


# version 10
# speedup vs baseline: 1.2550x; 1.0673x over previous
import math
import numpy as np
import ml_dtypes
import concourse.bass as bass
import concourse.mybir as mybir
from concourse.bass_utils import run_bass_kernel_spmd

F32 = mybir.dt.float32
BF16 = mybir.dt.bfloat16
ALU = mybir.AluOpType
ACT = mybir.ActivationFunctionType
AX = mybir.AxisListType

M1 = 12582912.0
M2 = 6755399441055744.0

ENGINES = ('t', 'v', 's', 'g', 'y')


class Op:
    __slots__ = ('eng', 'fn', 'reads', 'writes', 'idx', 'is_dma', 'tag',
                 'seq', 'signals', 'waits', 'drain_before', 'accum', 'label',
                 '_sigvals', 'clock')

    def __init__(self, eng, fn, reads, writes, is_dma, tag, seq, accum, label):
        self.eng = eng
        self.fn = fn
        self.reads = reads
        self.writes = writes
        self.is_dma = is_dma
        self.tag = tag
        self.seq = seq
        self.accum = accum
        self.label = label
        self.idx = -1
        self.signals = []
        self.waits = []
        self.drain_before = False
        self.clock = {}


class Prog:
    def __init__(self, nc):
        self.nc = nc
        self.ops = {e: [] for e in ENGINES}
        self.all_ops = []
        self.last_write = {}
        self.readers = {}
        self.dma_count = {}
        self.dma_written_res = {}
        self.barrier_count = 0
        self.pair_used = set()
        self.tag_used = set()
        self.gidx = {}
        self.pair_water = {}
        self.dma_water = {}
        self.drain_cover = {e: -1 for e in ENGINES}

    def _collect(self, targets, op):
        eprods = []
        dprods = []
        for t in targets:
            if t is None or t[0] == 'B':
                continue
            if t[0] == 'E':
                eprods.append(t[1])
            else:
                dprods.append(t)
        eprods.sort(key=lambda p: -self.gidx[id(p)])
        ck = op.clock
        for prod in eprods:
            if prod.eng == op.eng and op.eng == 't':
                continue
            if ck.get(prod.eng, -1) >= prod.idx:
                continue
            key = (prod.eng, op.eng)
            self.pair_used.add(key)
            prod.signals.append(key)
            op.waits.append(('P', key, prod))
            for k, v in prod.clock.items():
                if ck.get(k, -1) < v:
                    ck[k] = v
            if ck.get(prod.eng, -1) < prod.idx:
                ck[prod.eng] = prod.idx
        for (_, tag, seq, dop) in dprods:
            if ck.get(('D', tag), 0) >= seq:
                continue
            self.tag_used.add(tag)
            op.waits.append(('D', tag, 16 * seq))
            ck[('D', tag)] = seq
            if dop is not None:
                for k, v in dop.clock.items():
                    if ck.get(k, -1) < v:
                        ck[k] = v

    def record(self, eng, fn, reads=(), writes=(), is_dma=False, dma_res=None,
               accum=None, label=''):
        reads = list(reads)
        writes = list(writes)
        tag = seq = None
        if is_dma:
            assert dma_res is not None
            tag = dma_res
            seq = self.dma_count.get(tag, 0) + 1
            self.dma_count[tag] = seq
        op = Op(eng, fn, reads, writes, is_dma, tag, seq, accum, label)
        op.idx = len(self.ops[eng])
        self.gidx[id(op)] = len(self.all_ops)
        targets = []
        for r in reads:
            targets.append(self.last_write.get(r))
        for r in writes:
            lw = self.last_write.get(r)
            if lw is not None and not (accum is not None and lw[0] == 'E'
                                       and lw[1].accum == accum):
                targets.append(lw)
            for rd in self.readers.get(r, ()):
                targets.append(rd)
        self._collect(targets, op)
        self.ops[eng].append(op)
        self.all_ops.append(op)
        me = ('D', tag, seq, op) if is_dma else ('E', op)
        for r in reads:
            self.readers.setdefault(r, []).append(me)
        for r in writes:
            self.last_write[r] = me
            self.readers[r] = []
        return op

    def dma(self, queue_eng, out_ap, in_ap, reads, writes, label=''):
        def fn(eng, _o=out_ap, _i=in_ap):
            return eng.dma_start(out=_o, in_=_i)
        return self.record(queue_eng, fn, reads=reads, writes=writes,
                           is_dma=True, dma_res=writes[0], label=label)

    def barrier(self):
        self.barrier_count += 1
        k = self.barrier_count
        pending = [(tag, 16 * cnt) for tag, cnt in self.dma_count.items()]
        for tag, val in pending:
            self.tag_used.add(tag)

        def mk(eng_name):
            def fn(eng, _k=k, _pending=pending, _e=eng_name):
                if _e == 'y':
                    for tag, val in _pending:
                        if self.dma_water2.get(('y', tag), 0) < val:
                            eng.wait_ge(self._tag_sem[tag], val)
                            self.dma_water2[('y', tag)] = val
                eng.drain()
                eng.nop().then_inc(self._bar_sem, 1)
                eng.wait_ge(self._bar_sem, 5 * _k)
                return None
            return fn
        for e in ENGINES:
            op = Op(e, mk(e), [], [], False, None, None, None, f'barrier{k}')
            op.idx = len(self.ops[e])
            self.ops[e].append(op)
            self.all_ops.append(op)
        for r in self.last_write:
            self.last_write[r] = ('B',)
        self.readers = {}
        for e in ENGINES:
            self.drain_cover[e] = len(self.ops[e]) - 1

    def materialize(self, stack):
        nc = self.nc
        self._bar_sem = stack.enter_context(nc.semaphore('bar'))
        self._eng_sem = {}
        for e in ENGINES:
            self._eng_sem[e] = stack.enter_context(nc.semaphore(f'q_{e}'))
        self._tag_sem = {}
        for tag in sorted(self.tag_used):
            self._tag_sem[tag] = stack.enter_context(nc.semaphore(f'd_{tag}'))
        for e in ENGINES:
            cnt = 0
            for op in self.ops[e]:
                if op.is_dma or op.label.startswith('barrier'):
                    op._sigvals = None
                else:
                    cnt += 1
                    op._sigvals = cnt
        blk = self.nc.Block()
        self._block = blk

        def emit_engine(eng_handle, ename):
            for op in self.ops[ename]:
                pw = {}
                dw = {}
                for w in op.waits:
                    if w[0] == 'P':
                        prod = w[2]
                        v = prod._sigvals
                        if pw.get(prod.eng, 0) < v:
                            pw[prod.eng] = v
                    else:
                        _, tag, val = w
                        if dw.get(tag, 0) < val:
                            dw[tag] = val
                for pe in sorted(pw):
                    val = pw[pe]
                    if self.pair_water2.get((ename, pe), 0) < val:
                        eng_handle.wait_ge(self._eng_sem[pe], val)
                        self.pair_water2[(ename, pe)] = val
                for tag in sorted(dw):
                    val = dw[tag]
                    if self.dma_water2.get((ename, tag), 0) < val:
                        eng_handle.wait_ge(self._tag_sem[tag], val)
                        self.dma_water2[(ename, tag)] = val
                inst = op.fn(eng_handle)
                if op.is_dma:
                    assert inst is not None, f'dma op {op.label} returned None'
                    inst.then_inc(self._tag_sem[op.tag], 16)
                elif op._sigvals is not None:
                    if inst is not None:
                        inst.then_inc(self._eng_sem[ename], 1)
                    else:
                        eng_handle.nop().then_inc(self._eng_sem[ename], 1)

        self.pair_water2 = {}
        self.dma_water2 = {}
        with blk as block:
            block.tensor(lambda e: emit_engine(e, 't'))
            block.vector(lambda e: emit_engine(e, 'v'))
            block.scalar(lambda e: emit_engine(e, 's'))
            block.gpsimd(lambda e: emit_engine(e, 'g'))
            block.sync(lambda e: emit_engine(e, 'y'))


class Buf:
    __slots__ = ('mem', 'byte_start', 'cols', 'name', 'dtype', 'freed', 't',
                 'nalloc')

    def __init__(self, mem, byte_start, cols, name, dtype):
        self.mem = mem
        self.byte_start = byte_start
        self.cols = cols
        self.name = name
        self.dtype = dtype
        self.freed = False
        mem.uid += 1
        self.t = mem.nc.alloc_sbuf_tensor_at(
            f'c{mem.uid}_{name}', [128, cols], dtype,
            offset=mem.addr + byte_start)

    def ap(self, c0=0, c1=None, p0=0, p1=128):
        assert not self.freed, self.name
        if c1 is None:
            c1 = self.cols
        assert 0 <= c0 <= c1 <= self.cols, (self.name, c0, c1, self.cols)
        return self.t.ap()[p0:p1, c0:c1]

    def ap3(self, start, stride, nblk, width, p0=0, p1=128):
        assert not self.freed, self.name
        assert start + (nblk - 1) * stride + width <= self.cols
        assert p0 == 0 and p1 == 128
        import dataclasses
        full = self.t.ap()
        row_stride = full.ap[0][0]
        return dataclasses.replace(
            full, offset=full.offset + start,
            ap=type(full.ap)([[row_stride, 128], [stride, nblk], [1, width]]))

    def free(self):
        assert not self.freed
        self.mem._free(self.byte_start, self.nalloc)
        self.freed = True


class Mem:
    def __init__(self, nc, nbytes):
        self.nc = nc
        self.nbytes = nbytes
        self.uid = 0
        base = nc.alloc_sbuf_tensor('arena8', [128, nbytes], mybir.dt.uint8)
        self.addr = nc.lookup_mloc(base).addr
        assert self.addr % 64 == 0, self.addr
        self.free_list = [(0, nbytes)]

    def _alloc(self, n, align=64):
        if n <= 2048:
            for i in range(len(self.free_list) - 1, -1, -1):
                st, sz = self.free_list[i]
                a = (st + sz - n) // align * align
                if a >= st:
                    repl = [p for p in ((st, a - st), (a + n, st + sz - a - n))
                            if p[1] > 0]
                    self.free_list[i:i + 1] = repl
                    return a
        for i, (st, sz) in enumerate(self.free_list):
            a = (st + align - 1) // align * align
            if a + n <= st + sz:
                repl = [p for p in ((st, a - st), (a + n, st + sz - a - n))
                        if p[1] > 0]
                self.free_list[i:i + 1] = repl
                return a
        used = self.nbytes - sum(sz for _, sz in self.free_list)
        raise MemoryError(f'arena: cannot alloc {n}B (used={used}B, '
                          f'free={self.free_list[:6]})')

    def _free(self, start, n):
        self.free_list.append((start, n))
        self.free_list.sort()
        merged = []
        for st, sz in self.free_list:
            if merged and merged[-1][0] + merged[-1][1] == st:
                merged[-1] = (merged[-1][0], merged[-1][1] + sz)
            else:
                merged.append((st, sz))
        self.free_list = merged

    def f32(self, name, cols):
        al = 64 if cols > 8 else 32
        n = (cols * 4 + al - 1) // al * al
        b = Buf(self, self._alloc(n, al), cols, name, F32)
        b.nalloc = n
        return b

    def bf16(self, name, cols):
        al = 64 if cols > 16 else 32
        n = (cols * 2 + al - 1) // al * al
        b = Buf(self, self._alloc(n, al), cols, name, BF16)
        b.nalloc = n
        return b


def ceil_div(a, b):
    return (a + b - 1) // b




def ceil_div(a, b):
    return (a + b - 1) // b


def _ring_pieces(g0, length, RK):
    out = []
    g = g0
    rem = length
    while rem > 0:
        r = g % RK
        ln = min(rem, RK - r)
        out.append((g, r, ln))
        g += ln
        rem -= ln
    return out


def _chunk_rows(d_ap, nchunk):
    return d_ap.rearrange('o (j p) -> (o p) j', p=128)


def build_core(cfg):
    S = cfg['S']
    D = cfg['D']
    H = cfg['H']
    NSEQ = cfg['NSEQ']
    TF = cfg['TF']
    NT = S * NSEQ
    n_dc = D // 128
    n_hc = H // 128
    TPG = TF // 128
    NHG = S // TF
    ntile_seq = S // 128
    ntiles = NT // 128
    RK = TF + 128
    PH = cfg.get('PHASES', 'full')

    nc = bass.Bass('TRN2', target_bir_lowering=False,
                   dynamic_dma_scratch_size=64)

    d_x = nc.dram_tensor('x_in', [NT, D], F32, kind='ExternalInput').ap()
    d_w1 = nc.dram_tensor('w1', [H, D], F32, kind='ExternalInput').ap()
    d_w2 = nc.dram_tensor('w2', [D, H], F32, kind='ExternalInput').ap()
    d_cw = [nc.dram_tensor(f'cw{i}', [128, H // 128], F32,
                           kind='ExternalInput').ap() for i in range(3)]
    d_cb = nc.dram_tensor('cb', [128, H // 128], F32,
                          kind='ExternalInput').ap()
    d_wx = {w: nc.dram_tensor('w' + w, [D, D], F32, kind='ExternalInput').ap()
            for w in 'qkvo'}
    d_bx = {w: nc.dram_tensor('b' + w, [1, D], F32, kind='ExternalInput').ap()
            for w in 'qkvo'}
    d_ident = nc.dram_tensor('ident', [128, 128], BF16,
                             kind='ExternalInput').ap()
    d_out = nc.dram_tensor('out', [NT, D], F32, kind='ExternalOutput').ap()

    d_x1 = nc.dram_tensor('x1d', [NT, D], F32).ap()
    d_x2 = nc.dram_tensor('x2d', [NT, D], F32).ap()
    d_m1T = nc.dram_tensor('m1Td', [128, n_dc * H], BF16).ap()
    d_m2T = nc.dram_tensor('m2Td', [128, n_hc * D], BF16).ap()
    d_lnh = nc.dram_tensor('lnhd', [ntiles, n_dc, 128, 128], BF16).ap()
    d_lnl = nc.dram_tensor('lnld', [ntiles, n_dc, 128, 128], BF16).ap()
    d_qh = nc.dram_tensor('qhd', [ntiles, n_dc, 128, 128], BF16).ap()
    d_ql = nc.dram_tensor('qld', [ntiles, n_dc, 128, 128], BF16).ap()
    d_wT = {}
    for w in 'qkvo':
        for hl in 'hl':
            d_wT[w + hl] = nc.dram_tensor(f'w{w}T{hl}d', [128, n_dc * D],
                                          BF16).ap()

    psum_mm = [nc.alloc_psum_tensor(f'pmm{i}', [128, 512], F32)
               for i in range(6)]
    psum_tb = nc.alloc_psum_tensor('ptb', [128, 1024], BF16)
    psum_st = nc.alloc_psum_tensor('pst', [128, 512], F32)

    mem = Mem(nc, nc.sbuf_bytes_remaining // 64 * 64)
    P = Prog(nc)

    ident = mem.bf16('ident', 128)
    onesr = mem.f32('onesr', 128)
    onesc = mem.f32('onesc', 1)
    magic = mem.f32('magic', 1)
    kapb = mem.f32('kap', 1)
    cdesc1 = mem.f32('cdesc1', 1)
    cdesc2 = mem.f32('cdesc2', 1)
    convc = [mem.f32(f'convc{i}', n_hc) for i in range(3)]
    convb = mem.f32('convb', n_hc)
    eps_b = mem.f32('epsb', 1)

    identf = mem.f32('identf', 128)
    i_ap = ident.ap()
    if_ap = identf.ap()
    ones_row = onesr.ap(p0=0, p1=1)

    P.dma('y', i_ap, d_ident, reads=['d_ident'], writes=['ident'],
          label='ld_ident')
    P.record('v', lambda e, o=if_ap, i=i_ap: e.tensor_copy(o, i),
             reads=['ident'], writes=['identf'])
    for i in range(3):
        P.dma('y', convc[i].ap(), d_cw[i], reads=[f'd_cw{i}'],
              writes=[f'convc{i}'], label=f'ld_cw{i}')
    P.dma('y', convb.ap(), d_cb, reads=['d_cb'], writes=['convb'],
          label='ld_cb')
    P.record('v', lambda e, o=onesr.ap(p0=0, p1=1): e.memset(o, 1.0),
             writes=['ones1'])
    P.record('v', lambda e, o=onesc.ap(): e.memset(o, 1.0), writes=['onesc'])
    P.record('v', lambda e, o=eps_b.ap(): e.memset(o, 1e-5), writes=['epsb'])
    kap_ap = kapb.ap()
    magic_ap = magic.ap()
    P.record('v', lambda e, o=kap_ap: e.memset(o, 1.5), writes=['kap'])
    P.record('v', lambda e, o=kap_ap: e.tensor_scalar(o, o, M2, M2, ALU.add,
             ALU.subtract), reads=['kap'], writes=['kap'])
    P.record('v', lambda e, o=magic_ap, i=kap_ap: e.tensor_scalar(
        o, i, (M2 - M1) / 2.0, M1, ALU.mult, ALU.add),
        reads=['kap'], writes=['magic'])

    def bcast_row(row_ap, row_res, out_ap, out_res, n):
        ps = psum_st.ap()[:, 0:n]
        P.record('t', lambda e, o=ps, l=ones_row, r=row_ap: e.matmul(
            o, l, r, start=True, stop=True), reads=['ones1', row_res],
            writes=['ps_st'], label=f'bc_{out_res}')
        P.record('v', lambda e, o=out_ap, i=ps: e.tensor_copy(o, i),
                 reads=['ps_st'], writes=[out_res])

    def col2row(col_ap, col_res, row_ap, row_res, wp=128):
        ps = psum_st.ap()[0:1, 0:wp]
        P.record('t', lambda e, o=ps, i=col_ap, idn=if_ap[0:wp, 0:wp]:
                 e.transpose(o, i, idn),
                 reads=[col_res, 'identf'], writes=['ps_st'])
        P.record('v', lambda e, o=row_ap, i=ps: e.tensor_copy(o, i),
                 reads=['ps_st'], writes=[row_res])

    def row2col(row_ap, row_res, col_ap, col_res, wp=128):
        ps = psum_st.ap()[0:wp, 0:1]
        P.record('t', lambda e, o=ps, i=row_ap, idn=if_ap[0:1, 0:1]:
                 e.transpose(o, i, idn),
                 reads=[row_res, 'identf'], writes=['ps_st'])
        P.record('v', lambda e, o=col_ap, i=ps: e.tensor_copy(o, i),
                 reads=['ps_st'], writes=[col_res])

    def tr_tile(src_aps, src_res, dst3_ap, dst_res, nblk, key, evac='v'):
        for c in range(nblk):
            pt = psum_tb.ap()[:, c * 128:(c + 1) * 128]
            P.record('t', lambda e, o=pt, i=src_aps[c], idn=i_ap:
                     e.transpose(o, i, idn), reads=[src_res, 'ident'],
                     writes=['ps_tb'], accum=key)
        src = psum_tb.ap()[:, 0:nblk * 128].rearrange('p (c n) -> p c n',
                                                      c=nblk)
        P.record(evac, lambda e, o=dst3_ap, i=src: e.tensor_copy(o, i),
                 reads=['ps_tb'], writes=[dst_res])

    def emit_wquant(d_w, rows, cols, d_mT, dmT_res, cdesc_buf, cdesc_res,
                    wname):
        n_rt = rows // 128
        n_cc = cols // 128
        stage = mem.f32(f'{wname}_stage', n_rt * cols)
        acc = mem.f32(f'{wname}_acc', 1)
        red = mem.f32(f'{wname}_red', 1)
        scl = mem.f32(f'{wname}_scl', 1)
        mq = mem.bf16(f'{wname}_mq', cols)
        mT_sb = mem.bf16(f'{wname}_mT', n_cc * rows)
        sres = f'{wname}_stage'
        for t in range(n_rt):
            P.dma('y', stage.ap(t * cols, (t + 1) * cols),
                  d_w[t * 128:(t + 1) * 128, :], reads=[f'd_{wname}'],
                  writes=[sres], label=f'{wname}_ld{t}')
        for t in range(n_rt):
            sap = stage.ap(t * cols, (t + 1) * cols)
            P.record('v', lambda e, o=red.ap(), i=sap: e.reduce_sum(
                o, i, AX.X, apply_absolute_value=True),
                reads=[sres], writes=[f'{wname}_red'])
            if t == 0:
                P.record('v', lambda e, o=acc.ap(), i=red.ap():
                         e.tensor_copy(o, i), reads=[f'{wname}_red'],
                         writes=[f'{wname}_acc'])
            else:
                P.record('v', lambda e, o=acc.ap(), i=red.ap():
                         e.tensor_add(o, o, i),
                         reads=[f'{wname}_red', f'{wname}_acc'],
                         writes=[f'{wname}_acc'])
        ps = psum_st.ap()[0:1, 0:1]
        P.record('t', lambda e, o=ps, l=acc.ap(), r=onesc.ap(): e.matmul(
            o, l, r, start=True, stop=True),
            reads=[f'{wname}_acc', 'onesc'], writes=['ps_st'])
        sc1 = scl.ap(p0=0, p1=1)[:, 0:1]
        P.record('v', lambda e, o=sc1, i=ps: e.tensor_scalar(
            o, i, 1.0 / (rows * cols), 1e-5, ALU.mult, ALU.max),
            reads=['ps_st'], writes=[f'{wname}_c'])
        P.record('v', lambda e, o=sc1: e.reciprocal(o, o),
                 reads=[f'{wname}_c'], writes=[f'{wname}_c'])
        bcast_row(sc1, f'{wname}_c', scl.ap(), f'{wname}_s128', 1)
        P.record('v', lambda e, o=cdesc_buf.ap(), i=scl.ap():
                 e.reciprocal(o, i), reads=[f'{wname}_s128'],
                 writes=[cdesc_res])
        for t in range(n_rt):
            sap = stage.ap(t * cols, (t + 1) * cols)
            P.record('v', lambda e, o=sap, s=scl.ap(), m=magic_ap:
                     e.tensor_scalar(o, o, s, m, ALU.mult, ALU.add),
                     reads=[sres, f'{wname}_s128', 'magic'], writes=[sres])
            P.record('v', lambda e, o=sap, m=magic_ap: e.tensor_scalar(
                o, o, m, None, ALU.subtract), reads=[sres, 'magic'],
                writes=[sres])
            P.record('v', lambda e, o=mq.ap(), i=sap: e.tensor_scalar(
                o, i, -1.0, 1.0, ALU.max, ALU.min),
                reads=[sres], writes=[f'{wname}_mq'])
            nb = 8 if n_cc >= 8 else n_cc
            for cb in range(0, n_cc, nb):
                blks = [mq.ap(c * 128, (c + 1) * 128)
                        for c in range(cb, cb + nb)]
                dst3 = mT_sb.ap3(cb * rows + t * 128, rows, nb, 128)
                tr_tile(blks, f'{wname}_mq', dst3, f'{wname}_mT', nb,
                        (wname, t, cb))
        P.dma('s', d_mT, mT_sb.ap(), reads=[f'{wname}_mT'], writes=[dmT_res],
              label=f'{wname}_spill')
        stage.free(); acc.free(); red.free(); scl.free(); mq.free()
        mT_sb.free()

    def emit_ffn(in_ap, out_ap, in_res, out_res, fid):
        m1T = mem.bf16(f'{fid}_m1T', n_dc * H)
        nq2 = min(4, n_hc)
        assert nq2 % 2 == 0
        JQ2 = n_hc // nq2
        m2T = mem.bf16(f'{fid}_m2T', 2 * JQ2 * D)
        nq = min(4, n_dc)
        for q in range(nq):
            lo, hi = q * (n_dc // nq) * H, (q + 1) * (n_dc // nq) * H
            P.dma('y', m1T.ap(lo, hi), d_m1T[:, lo:hi], reads=['d_w1mT'],
                  writes=[f'{fid}m1Tq{q}'], label=f'{fid}_m1Tld{q}')
        P.dma('y', m2T.ap(0, JQ2 * D), d_m2T[:, 0:JQ2 * D], reads=['d_w2mT'],
              writes=[f'{fid}m2Ts0'], label=f'{fid}_m2Tld0')

        ygrp = mem.f32(f'{fid}_ygrp', n_hc * (TF + 1))
        k2r = mem.bf16(f'{fid}_k2r', n_hc * RK)
        xr = [mem.f32(f'{fid}_xr{i}', D) for i in range(3)]
        sq = [mem.f32(f'{fid}_sq{i}', D) for i in range(1)]
        k1fm = [mem.bf16(f'{fid}_k1fm{i}', n_dc * TF) for i in range(2)]
        k1tm = [mem.bf16(f'{fid}_k1tm{i}', D) for i in range(1)]
        Bs = [mem.f32(f'{fid}_Bs{i}', TF + 3) for i in range(4)]
        g1row = mem.f32(f'{fid}_g1row', TF)
        g1bc = [mem.f32(f'{fid}_g1bc{i}', TF) for i in range(1)]
        carry = mem.f32(f'{fid}_carry', 2 * n_hc)
        yab = [mem.f32(f'{fid}_yab{i}', TF + 1) for i in range(1)]
        maxacc = mem.f32(f'{fid}_maxacc', TF + 1)
        s2row = mem.f32(f'{fid}_s2row', TF + 1)
        fring = mem.f32(f'{fid}_fring', RK)
        s2bc = [mem.f32(f'{fid}_s2bc{i}', TF + 1) for i in range(1)]
        stt_ = [mem.f32(f'{fid}_stt{i}', 24) for i in range(2)]

        gctr = 0
        for seq in range(NSEQ):
            done_tiles = 0
            for hg in range(NHG):
                t0 = hg * TF
                ws = max(t0 - 1, 0)
                we = S if hg == NHG - 1 else t0 + TF - 1
                W = we - ws
                ridx = gctr % 2
                for i in range(TPG):
                    a_glob = (seq * S + t0) // 128 + i
                    xi = xr[(gctr * TPG + i) % 3]
                    xires = f'{fid}_xr{(gctr * TPG + i) % 3}'
                    P.dma('y', xi.ap(), in_ap[a_glob * 128:(a_glob + 1) * 128, :],
                          reads=[in_res], writes=[xires],
                          label=f'{fid}_xld{a_glob}')
                    st_i = stt_[i % 2]
                    rS = f'{fid}_stt{i % 2}'
                    sq_i, rQ = sq[0], f'{fid}_sq0'
                    k1_i, rK = k1tm[0], f'{fid}_k1tm0'
                    cm = lambda c: st_i.ap(c, c + 1)
                    P.record('v', lambda e, o=cm(0), x=xi.ap(): e.reduce_sum(
                        o, x, AX.X), reads=[xires], writes=[rS + 'a'])
                    P.record('v', lambda e, o=cm(1), i_=cm(0): e.tensor_scalar(
                        o, i_, -1.0 / D, None, ALU.mult), reads=[rS + 'a'],
                        writes=[rS + 'm'])
                    P.record('s', lambda e, o=sq_i.ap(), x=xi.ap(), b=cm(1),
                             a=cm(2): e.activation(o, x, ACT.Square, bias=b,
                             accum_out=a),
                             reads=[xires, rS + 'm'], writes=[rQ, rS + 'v'])
                    P.record('v', lambda e, o=cm(3), i_=cm(2): e.tensor_scalar(
                        o, i_, 1.0 / D, None, ALU.mult), reads=[rS + 'v'],
                        writes=[rS + 'v2'])
                    P.record('s', lambda e, o=cm(4), i_=cm(3), b=eps_b.ap():
                             e.activation(o, i_, ACT.Sqrt, bias=b),
                             reads=[rS + 'v2', 'epsb'], writes=[rS + 'sd'])
                    P.record('v', lambda e, o=cm(5), i_=cm(4): e.reciprocal(o, i_),
                             reads=[rS + 'sd'], writes=[rS + 'r'])
                    P.record('v', lambda e, o=cm(6), i_=xi.ap(): e.reduce_max(
                        o, i_, AX.X), reads=[xires], writes=[rS + 'x'])
                    P.record('v', lambda e, o=cm(11), i_=xi.ap():
                             e.tensor_reduce(o, i_, AX.X, ALU.min),
                             reads=[xires], writes=[rS + 'n'])
                    P.record('v', lambda e, o=cm(6), m=cm(1):
                             e.tensor_scalar(o, o, m, None, ALU.add),
                             reads=[rS + 'x', rS + 'm'], writes=[rS + 'x'])
                    P.record('v', lambda e, o=cm(11), m=cm(1):
                             e.tensor_scalar(o, o, m, -1.0, ALU.add, ALU.mult),
                             reads=[rS + 'n', rS + 'm'], writes=[rS + 'n'])
                    P.record('v', lambda e, o=cm(6), i1=cm(11):
                             e.tensor_tensor(o, o, i1, ALU.max),
                             reads=[rS + 'x', rS + 'n'], writes=[rS + 'x'])
                    P.record('v', lambda e, o=cm(7), i_=cm(6), r=cm(5):
                             e.tensor_scalar(o, i_, r, 1e-5, ALU.mult, ALU.max),
                             reads=[rS + 'x', rS + 'r'], writes=[rS + 'u'])
                    P.record('v', lambda e, o=cm(8), i_=cm(7): e.reciprocal(o, i_),
                             reads=[rS + 'u'], writes=[rS + 'u2'])
                    P.record('v', lambda e, o=cm(9), i_=cm(8), r=cm(5):
                             e.tensor_scalar(o, i_, 127.0, r, ALU.mult, ALU.mult),
                             reads=[rS + 'u2', rS + 'r'], writes=[rS + 's1'])
                    P.record('v', lambda e, o=sq_i.ap(), x=xi.ap(), m=cm(1),
                             s=cm(9): e.tensor_scalar(o, x, m, s, ALU.add,
                             ALU.mult), reads=[xires, rS + 'm', rS + 's1'],
                             writes=[rQ])
                    P.record('v', lambda e, o=k1_i.ap(), i_=sq_i.ap(),
                             m=magic_ap: e.tensor_scalar(o, i_, m, m, ALU.add,
                             ALU.subtract), reads=[rQ, 'magic'], writes=[rK])
                    P.record('v', lambda e, o=cm(10), i_=cm(7), c=cdesc1.ap():
                             e.tensor_scalar(o, i_, c, 1.0 / 127.0, ALU.mult,
                             ALU.mult), reads=[rS + 'u', 'w1_cdesc'],
                             writes=[rS + 'g'])
                    col2row(cm(10), rS + 'g',
                            g1row.ap(i * 128, (i + 1) * 128, p0=0, p1=1),
                            f'{fid}_g1row', 128)
                    blks = [k1_i.ap(dc * 128, (dc + 1) * 128)
                            for dc in range(n_dc)]
                    dst3 = k1fm[ridx].ap3(i * 128, TF, n_dc, 128)
                    tr_tile(blks, rK, dst3, f'{fid}_k1fm{ridx}', n_dc,
                            (fid, 'k1t', gctr, i))
                bcast_row(g1row.ap(0, TF, p0=0, p1=1), f'{fid}_g1row',
                          g1bc[0].ap(0, TF), f'{fid}_g1bc0', TF)
                co = ws - t0 + 2

                def stA(j):
                    pb = psum_mm[j % 2]
                    pbres = f'pb{j % 2}'
                    for dc in range(n_dc):
                        P.record('t', lambda e, o=pb.ap()[:, 0:TF],
                                 l=m1T.ap(dc * H + j * 128,
                                          dc * H + (j + 1) * 128),
                                 r=k1fm[ridx].ap(dc * TF, (dc + 1) * TF),
                                 st=(dc == 0), sp=(dc == n_dc - 1):
                                 e.matmul(o, l, r, start=st, stop=sp),
                                 reads=[f'{fid}m1Tq{dc * nq // n_dc}',
                                        f'{fid}_k1fm{ridx}'],
                                 writes=[pbres], accum=(fid, 1, seq, hg, j),
                                 label=f'{fid}_mm1_{seq}_{hg}_{j}')

                def stB(j):
                    pb = psum_mm[j % 2]
                    pbres = f'pb{j % 2}'
                    Bj, Bres = Bs[j % 4], f'{fid}_Bs{j % 4}'
                    if hg == 0:
                        P.record('v', lambda e, o=Bj.ap(0, 2): e.memset(o, 0.0),
                                 writes=[Bres])
                    else:
                        P.record('v', lambda e, o=Bj.ap(0, 2),
                                 i_=carry.ap(2 * j, 2 * j + 2):
                                 e.tensor_copy(o, i_), reads=[f'{fid}_carry'],
                                 writes=[Bres])
                    if hg == NHG - 1:
                        P.record('v', lambda e, o=Bj.ap(TF + 2, TF + 3):
                                 e.memset(o, 0.0), writes=[Bres])
                    P.record('v', lambda e, o=Bj.ap(2, TF + 2),
                             i0=pb.ap()[:, 0:TF], i1=g1bc[0].ap(0, TF):
                             e.tensor_tensor(o, i0, i1, ALU.mult),
                             reads=[pbres, f'{fid}_g1bc0'], writes=[Bres])

                def stC(j):
                    Bj, Bres = Bs[j % 4], f'{fid}_Bs{j % 4}'
                    yj = ygrp.ap(j * (TF + 1), j * (TF + 1) + W)
                    yjres = f'{fid}_y{j}'
                    P.record('s', lambda e, o=yj, i_=Bj.ap(co - 1, co - 1 + W),
                             c=convc[0].ap(j, j + 1), b=convb.ap(j, j + 1):
                             e.activation(o, i_, ACT.Identity, bias=b, scale=c),
                             reads=[Bres, 'convc0', 'convb'], writes=[yjres])

                def stD(j):
                    Bj, Bres = Bs[j % 4], f'{fid}_Bs{j % 4}'
                    yj = ygrp.ap(j * (TF + 1), j * (TF + 1) + W)
                    yjres = f'{fid}_y{j}'
                    P.record('v', lambda e, o=yj, i0=Bj.ap(co, co + W),
                             c=convc[1].ap(j, j + 1), i1=yj:
                             e.scalar_tensor_tensor(o, i0, c, i1, ALU.mult,
                             ALU.add), reads=[Bres, 'convc1', yjres],
                             writes=[yjres])

                def stD2(j):
                    Bj, Bres = Bs[j % 4], f'{fid}_Bs{j % 4}'
                    yj = ygrp.ap(j * (TF + 1), j * (TF + 1) + W)
                    yjres = f'{fid}_y{j}'
                    P.record('v', lambda e, o=yj, i0=Bj.ap(co + 1, co + 1 + W),
                             c=convc[2].ap(j, j + 1), i1=yj:
                             e.scalar_tensor_tensor(o, i0, c, i1, ALU.mult,
                             ALU.add), reads=[Bres, 'convc2', yjres],
                             writes=[yjres])
                    if hg < NHG - 1:
                        P.record('v', lambda e, o=carry.ap(2 * j, 2 * j + 2),
                                 i_=Bj.ap(TF, TF + 2): e.tensor_copy(o, i_),
                                 reads=[Bres], writes=[f'{fid}_carry'])

                def stE(j):
                    yj = ygrp.ap(j * (TF + 1), j * (TF + 1) + W)
                    yjres = f'{fid}_y{j}'
                    ya, yares = yab[0], f'{fid}_yab0'
                    P.record('s', lambda e, o=yj: e.activation(o, o, ACT.Silu),
                             reads=[yjres], writes=[yjres])
                    P.record('s', lambda e, o=ya.ap(0, W), i_=yj: e.activation(
                        o, i_, ACT.Abs), reads=[yjres], writes=[yares])

                def stF(j):
                    ya, yares = yab[0], f'{fid}_yab0'
                    if j == 0:
                        P.record('v', lambda e, o=maxacc.ap(0, W),
                                 i_=ya.ap(0, W): e.tensor_copy(o, i_),
                                 reads=[yares], writes=[f'{fid}_mx'])
                    else:
                        P.record('v', lambda e, o=maxacc.ap(0, W),
                                 i_=ya.ap(0, W): e.tensor_tensor(o, o, i_,
                                 ALU.max), reads=[yares, f'{fid}_mx'],
                                 writes=[f'{fid}_mx'])

                for jj in range(n_hc + 6):
                    if jj < n_hc:
                        stA(jj)
                    if 0 <= jj - 2 < n_hc:
                        stC(jj - 2)
                    if 0 <= jj - 6 < n_hc:
                        stF(jj - 6)
                    if 0 <= jj - 5 < n_hc:
                        stE(jj - 5)
                    if 0 <= jj - 3 < n_hc:
                        stD(jj - 3)
                    if 0 <= jj - 4 < n_hc:
                        stD2(jj - 4)
                    if 0 <= jj - 1 < n_hc:
                        stB(jj - 1)
                for w in range(ceil_div(W, 128)):
                    wp = min(128, W - w * 128)
                    pst = psum_st.ap()[0:wp, 0:128]
                    P.record('t', lambda e, o=pst, i_=maxacc.ap(
                        w * 128, w * 128 + wp), idn=if_ap: e.transpose(
                        o, i_, idn), reads=[f'{fid}_mx', 'identf'],
                        writes=['ps_st'])
                    P.record('v', lambda e, o=stt_[0].ap(16, 17, p1=wp),
                             i_=pst: e.reduce_max(o, i_, AX.X),
                             reads=['ps_st'], writes=['am_tm'])
                    P.record('v', lambda e, o=stt_[0].ap(17, 18, p1=wp),
                             i_=stt_[0].ap(16, 17, p1=wp): e.tensor_scalar(
                             o, i_, 1e-5, None, ALU.max), reads=['am_tm'],
                             writes=['u2f'])
                    P.record('v', lambda e, o=stt_[0].ap(18, 19, p1=wp),
                             i_=stt_[0].ap(17, 18, p1=wp): e.reciprocal(o, i_),
                             reads=['u2f'], writes=['r2f'])
                    P.record('v', lambda e, o=stt_[0].ap(19, 20, p1=wp),
                             i_=stt_[0].ap(18, 19, p1=wp): e.tensor_scalar(
                             o, i_, 127.0, None, ALU.mult), reads=['r2f'],
                             writes=['s2tm'])
                    P.record('v', lambda e, o=stt_[0].ap(20, 21, p1=wp),
                             i_=stt_[0].ap(17, 18, p1=wp),
                             c=cdesc2.ap(0, 1, p1=wp): e.tensor_scalar(
                             o, i_, c, 0.5 / 127.0, ALU.mult, ALU.mult),
                             reads=['u2f', 'w2_cdesc'], writes=['ftm0'])
                    col2row(stt_[0].ap(19, 20, p1=wp), 's2tm',
                            s2row.ap(w * 128, w * 128 + wp, p0=0, p1=1),
                            f'{fid}_s2row', wp)
                    col2row(stt_[0].ap(20, 21, p1=wp), 'ftm0',
                            yab[0].ap(0, wp, p0=0, p1=1), f'{fid}_yab0', wp)
                    for (gg, rr, ln) in _ring_pieces(ws + w * 128, wp, RK):
                        o0 = gg - ws - w * 128
                        P.record('v', lambda e, o=fring.ap(rr, rr + ln, p0=0,
                                 p1=1), i_=yab[0].ap(o0, o0 + ln, p0=0, p1=1):
                                 e.tensor_copy(o, i_),
                                 reads=[f'{fid}_yab0'], writes=['fring'])
                bcast_row(s2row.ap(0, W, p0=0, p1=1), f'{fid}_s2row',
                          s2bc[0].ap(0, W), f'{fid}_s2bc0', W)
                ready = S // 128 if hg == NHG - 1 else (we + 1 - 128) // 128
                nhalf = ceil_div(D, 512)
                tset = []
                for idx, a in enumerate(range(done_tiles, ready)):
                    bb2 = (2 + 2 * idx) % 6
                    tset.append((a, [psum_mm[bb2], psum_mm[bb2 + 1]],
                                 [f'pb{bb2}', f'pb{bb2 + 1}'],
                                 (a * 128) % RK))

                def q2(j):
                    yj = ygrp.ap(j * (TF + 1), j * (TF + 1) + W)
                    ya, yares = yab[0], f'{fid}_yab0'
                    P.record('v', lambda e, o=ya.ap(0, W), i0=yj,
                             i1=s2bc[0].ap(0, W): e.tensor_tensor(
                             o, i0, i1, ALU.mult),
                             reads=[f'{fid}_y{j}', f'{fid}_s2bc0'],
                             writes=[yares])
                    for (gg, rr, ln) in _ring_pieces(ws, W, RK):
                        P.record('v', lambda e, o=k2r.ap(j * RK + rr,
                                 j * RK + rr + ln), i_=ya.ap(gg - ws,
                                 gg - ws + ln), m=magic_ap: e.tensor_scalar(
                                 o, i_, m, m, ALU.add, ALU.subtract),
                                 reads=[yares, 'magic'],
                                 writes=[f'{fid}_k2r{j}'])

                def mm2j(j):
                    if j % JQ2 == 0:
                        q = j // JQ2
                        qn = (q + 1) % nq2
                        sl2 = qn % 2
                        if not (seq == NSEQ - 1 and hg == NHG - 1
                                and q + 1 == nq2):
                            P.dma('y', m2T.ap(sl2 * JQ2 * D,
                                  (sl2 + 1) * JQ2 * D),
                                  d_m2T[:, qn * JQ2 * D:(qn + 1) * JQ2 * D],
                                  reads=['d_w2mT'], writes=[f'{fid}m2Ts{sl2}'],
                                  label=f'{fid}_m2Tld_{seq}_{hg}_{qn}')
                    sl = (j // JQ2) % 2
                    for (a, pbt, pbr, rc) in tset:
                        for h2 in range(nhalf):
                            nn = min(512, D - h2 * 512)
                            P.record('t', lambda e, o=pbt[h2].ap()[:, 0:nn],
                                     l=k2r.ap(j * RK + rc, j * RK + rc + 128),
                                     r=m2T.ap(sl * JQ2 * D + (j % JQ2) * D
                                              + h2 * 512,
                                              sl * JQ2 * D + (j % JQ2) * D
                                              + h2 * 512 + nn),
                                     st=(j == 0), sp=(j == n_hc - 1):
                                     e.matmul(o, l, r, start=st, stop=sp),
                                     reads=[f'{fid}_k2r{j}',
                                            f'{fid}m2Ts{sl}'],
                                     writes=[pbr[h2]],
                                     accum=(fid, 2, seq, a, h2),
                                     label=f'{fid}_mm2_{seq}_{a}_{j}_{h2}')
                for j in range(n_hc + 1):
                    if j < n_hc:
                        q2(j)
                    if j >= 1:
                        mm2j(j - 1)
                for (a, pbt, pbr, rc) in tset:
                    a_glob = seq * ntile_seq + a
                    ga = seq * ntile_seq + a
                    xsi, xsres = xr[ga % 3], f'{fid}_xr{ga % 3}'
                    fi, fires = None, f'{fid}_ftm{a % 2}'
                    row2col(fring.ap(rc, rc + 128, p0=0, p1=1), 'fring',
                            stt_[a % 2].ap(22, 23), fires, 128)
                    for h2 in range(nhalf):
                        nn = min(512, D - h2 * 512)
                        P.record('v', lambda e, o=xsi.ap(h2 * 512, h2 * 512 + nn),
                                 i0=pbt[h2].ap()[:, 0:nn],
                                 f=stt_[a % 2].ap(22, 23):
                                 e.scalar_tensor_tensor(o, i0, f, o, ALU.mult,
                                 ALU.add), reads=[pbr[h2], fires, xsres],
                                 writes=[xsres])
                    P.dma('s', out_ap[a_glob * 128:(a_glob + 1) * 128, :],
                          xsi.ap(), reads=[xsres], writes=[out_res],
                          label=f'{fid}_xst{a_glob}')
                done_tiles = ready
                gctr += 1
        for b in (m1T, m2T, g1row, carry, ygrp, maxacc, s2row, fring, k2r):
            b.free()
        for lst in (xr, sq, stt_, k1tm, k1fm, g1bc, Bs, yab, s2bc):
            for b in lst:
                b.free()

    n_kc = S // 128
    n_kg = ceil_div(S, 512)
    nhalf = ceil_div(D, 512)

    class Acc:
        def __init__(self, total):
            self.total = total
            self.i = 0

        def nxt(self):
            st = self.i == 0
            sp = self.i == self.total - 1
            self.i += 1
            return st, sp

    def emit_wprep(w):
        wth = mem.bf16(f'a_{w}th', n_dc * D)
        wtl = mem.bf16(f'a_{w}tl', n_dc * D)
        stg = [mem.f32(f'a_{w}stg{i}', D) for i in range(2)]
        hib = [mem.bf16(f'a_{w}hi{i}', D) for i in range(2)]
        lob = [mem.bf16(f'a_{w}lo{i}', D) for i in range(2)]
        for dt in range(n_dc):
            sg, rs = stg[dt % 2], f'a_{w}stg{dt % 2}'
            hi, rh = hib[dt % 2], f'a_{w}hi{dt % 2}'
            lo, rl = lob[dt % 2], f'a_{w}lo{dt % 2}'
            P.dma('y', sg.ap(), d_wx[w][dt * 128:(dt + 1) * 128, :],
                  reads=[f'd_w{w}'], writes=[rs], label=f'{w}ld{dt}')
            P.record('v', lambda e, o=hi.ap(), i=sg.ap(): e.tensor_copy(o, i),
                     reads=[rs], writes=[rh])
            P.record('v', lambda e, o=lo.ap(), i0=sg.ap(), i1=hi.ap():
                     e.tensor_tensor(o, i0, i1, ALU.subtract),
                     reads=[rs, rh], writes=[rl])
            for src, dst, res_s, res_d in ((hi, wth, rh, f'a_{w}th'),
                                           (lo, wtl, rl, f'a_{w}tl')):
                nb = min(8, n_dc)
                for cb in range(0, n_dc, nb):
                    blks = [src.ap(ec * 128, (ec + 1) * 128)
                            for ec in range(cb, cb + nb)]
                    dst3 = dst.ap3(cb * D + dt * 128, D, nb, 128)
                    tr_tile(blks, res_s, dst3, res_d, nb, (w, dt, cb, res_d))
        P.dma('s', d_wT[w + 'h'], wth.ap(), reads=[f'a_{w}th'],
              writes=[f'd_wT{w}h'], label=f'{w}spillh')
        P.dma('s', d_wT[w + 'l'], wtl.ap(), reads=[f'a_{w}tl'],
              writes=[f'd_wT{w}l'], label=f'{w}spilll')
        for b in stg + hib + lob + [wth, wtl]:
            b.free()

    def ld_wt(w, bufs):
        for hl, b in zip('hl', bufs):
            P.dma('y', b.ap(), d_wT[w + hl], reads=[f'd_wT{w}{hl}'],
                  writes=[f'aw_{w}{hl}'], label=f'ld_wT{w}{hl}')

    def bias_bcast(w, buf, row=None):
        P.dma('y', buf.ap(0, D, p0=0, p1=1), d_bx[w], reads=[f'd_b{w}'],
              writes=['a_bb'], label=f'ld_b{w}')
        for h2 in range(nhalf):
            nn = min(512, D - h2 * 512)
            bcast_row(buf.ap(h2 * 512, h2 * 512 + nn, p0=0, p1=1),
                      'a_bb', buf.ap(h2 * 512, h2 * 512 + nn),
                      'a_bb', nn)

    def emit_ln_split(xi, xires, st_i, rS, sq_i, rQ, lnf, rLF,
                      lnhi, rLH, lnlo, rLL):
        cm = lambda c: st_i.ap(c, c + 1)
        P.record('v', lambda e, o=cm(0), x=xi.ap(): e.reduce_sum(
            o, x, AX.X), reads=[xires], writes=[rS + 'a'])
        P.record('v', lambda e, o=cm(1), i_=cm(0): e.tensor_scalar(
            o, i_, -1.0 / D, None, ALU.mult), reads=[rS + 'a'],
            writes=[rS + 'm'])
        P.record('s', lambda e, o=sq_i.ap(), x=xi.ap(), b=cm(1), a=cm(2):
                 e.activation(o, x, ACT.Square, bias=b, accum_out=a),
                 reads=[xires, rS + 'm'], writes=[rQ, rS + 'v'])
        P.record('v', lambda e, o=cm(3), i_=cm(2): e.tensor_scalar(
            o, i_, 1.0 / D, None, ALU.mult), reads=[rS + 'v'],
            writes=[rS + 'v2'])
        P.record('s', lambda e, o=cm(4), i_=cm(3), b=eps_b.ap():
                 e.activation(o, i_, ACT.Sqrt, bias=b),
                 reads=[rS + 'v2', 'epsb'], writes=[rS + 'sd'])
        P.record('v', lambda e, o=cm(5), i_=cm(4): e.reciprocal(o, i_),
                 reads=[rS + 'sd'], writes=[rS + 'r'])
        P.record('v', lambda e, o=lnf.ap(), x=xi.ap(), m=cm(1), r=cm(5):
                 e.tensor_scalar(o, x, m, r, ALU.add, ALU.mult),
                 reads=[xires, rS + 'm', rS + 'r'], writes=[rLF])
        P.record('v', lambda e, o=lnhi.ap(), i_=lnf.ap(): e.tensor_copy(o, i_),
                 reads=[rLF], writes=[rLH])
        P.record('v', lambda e, o=lnlo.ap(), i0=lnf.ap(), i1=lnhi.ap():
                 e.tensor_tensor(o, i0, i1, ALU.subtract),
                 reads=[rLF, rLH], writes=[rLL])

    def emit_attn_all():
        for w in 'qkvo':
            emit_wprep(w)
            P.barrier()
        Khi = mem.bf16('a_Khi', n_dc * S)
        Klo = mem.bf16('a_Klo', n_dc * S)
        Vhi = mem.bf16('a_Vhi', n_kc * D)
        Vlo = mem.bf16('a_Vlo', n_kc * D)
        bbuf = mem.f32('a_bb', D)
        brow = None
        bb = {w: bbuf for w in 'qkvo'}
        stb = [mem.f32(f'a_stt{i}', 24) for i in range(2)]
        for seq in range(NSEQ):
            wk = [mem.bf16('a_wkh', n_dc * D), mem.bf16('a_wkl', n_dc * D)]
            ld_wt('k', wk)
            bias_bcast('k', bb['k'], brow)
            xr_ = [mem.f32(f'a_xr{i}', D) for i in range(2)]
            sq_ = [mem.f32(f'a_sq{i}', D) for i in range(1)]
            lnf_ = [mem.f32(f'a_lnf{i}', D) for i in range(1)]
            lnh_ = [mem.bf16(f'a_lnh{i}', D) for i in range(2)]
            lnl_ = [mem.bf16(f'a_lnl{i}', D) for i in range(2)]
            lnhF = [mem.bf16(f'a_lnhF{i}', n_dc * 128) for i in range(2)]
            lnlF = [mem.bf16(f'a_lnlF{i}', n_dc * 128) for i in range(2)]
            ktm = mem.f32('a_ktm', D)
            khi_t = mem.bf16('a_khit', D)
            klo_t = mem.bf16('a_klot', D)
            for i in range(n_kc):
                a_glob = seq * ntile_seq + i
                ri = i % 2
                xi, xires = xr_[ri], f'a_xr{ri}'
                P.dma('y', xi.ap(), d_x1[a_glob * 128:(a_glob + 1) * 128, :],
                      reads=['x1d'], writes=[xires], label=f'a1x{a_glob}')
                emit_ln_split(xi, xires, stb[ri], f'a_stt{ri}',
                              sq_[0], 'a_sq0', lnf_[0],
                              'a_lnf0', lnh_[ri], f'a_lnh{ri}',
                              lnl_[ri], f'a_lnl{ri}')
                for src, dstb, rs, rd in ((lnh_[ri], lnhF[ri], f'a_lnh{ri}',
                                           f'a_lnhF{ri}'),
                                          (lnl_[ri], lnlF[ri], f'a_lnl{ri}',
                                           f'a_lnlF{ri}')):
                    blks = [src.ap(dc * 128, (dc + 1) * 128)
                            for dc in range(n_dc)]
                    tr_tile(blks, rs, dstb.ap3(0, 128, n_dc, 128), rd, n_dc,
                            ('lnF', seq, i, rd))
                for dl, src, rs in ((d_lnh, lnhF[ri], f'a_lnhF{ri}'),
                                    (d_lnl, lnlF[ri], f'a_lnlF{ri}')):
                    P.dma('s', dl[a_glob].rearrange('dc p c -> p dc c'),
                          src.ap().rearrange('p (dc c) -> p dc c', dc=n_dc),
                          reads=[rs], writes=['lnd'], label=f'lnst{a_glob}')
                accs = [Acc(3 * n_dc) for _ in range(nhalf)]
                for ec in range(n_dc):
                    for sihi, sires in ((lnhF[ri], f'a_lnhF{ri}'),
                                        (lnlF[ri], f'a_lnlF{ri}')):
                        pass
                for ec in range(n_dc):
                    lh = lnhF[ri].ap(ec * 128, (ec + 1) * 128)
                    ll = lnlF[ri].ap(ec * 128, (ec + 1) * 128)
                    for h2 in range(nhalf):
                        nn = min(512, D - h2 * 512)
                        bk2 = 2 * (i % 2) + h2
                        pk, pkr = psum_mm[bk2], f'pb{bk2}'
                        for stat, statr, mov, movr in (
                                (lh, f'a_lnhF{ri}', wk[0], 'aw_kh'),
                                (lh, f'a_lnhF{ri}', wk[1], 'aw_kl'),
                                (ll, f'a_lnlF{ri}', wk[0], 'aw_kh')):
                            st, sp = accs[h2].nxt()
                            P.record('t', lambda e, o=pk.ap()[:, 0:nn], l=stat,
                                     r=mov.ap(ec * D + h2 * 512,
                                              ec * D + h2 * 512 + nn),
                                     st=st, sp=sp: e.matmul(o, l, r, start=st,
                                     stop=sp), reads=[statr, movr],
                                     writes=[pkr], accum=('K', seq, i, h2),
                                     label=f'K_{seq}_{i}_{ec}_{h2}')
                for h2 in range(nhalf):
                    nn = min(512, D - h2 * 512)
                    bk2 = 2 * (i % 2) + h2
                    P.record('v', lambda e, o=ktm.ap(h2 * 512, h2 * 512 + nn),
                             i0=psum_mm[bk2].ap()[:, 0:nn],
                             i1=bb['k'].ap(h2 * 512, h2 * 512 + nn):
                             e.tensor_tensor(o, i0, i1, ALU.add),
                             reads=[f'pb{bk2}', 'a_bb'], writes=['a_ktm'])
                P.record('v', lambda e, o=khi_t.ap(), i=ktm.ap():
                         e.tensor_copy(o, i), reads=['a_ktm'],
                         writes=['a_khit'])
                P.record('v', lambda e, o=klo_t.ap(), i0=ktm.ap(),
                         i1=khi_t.ap(): e.tensor_tensor(o, i0, i1,
                         ALU.subtract), reads=['a_ktm', 'a_khit'],
                         writes=['a_klot'])
                for src, dstb, rs, rd in ((khi_t, Khi, 'a_khit', 'a_Khi'),
                                          (klo_t, Klo, 'a_klot', 'a_Klo')):
                    blks = [src.ap(dc * 128, (dc + 1) * 128)
                            for dc in range(n_dc)]
                    tr_tile(blks, rs, dstb.ap3(i * 128, S, n_dc, 128), rd,
                            n_dc, ('KT', seq, i, rd))
            for b in wk + xr_ + sq_ + lnf_ + lnh_ + lnl_:
                b.free()
            ktm.free(); khi_t.free(); klo_t.free()
            P.barrier()
            wv = [mem.bf16('a_wvh', n_dc * D), mem.bf16('a_wvl', n_dc * D)]
            ld_wt('v', wv)
            bias_bcast('v', bb['v'], brow)
            vtm = mem.f32('a_vtm', D)
            for i in range(n_kc):
                a_glob = seq * ntile_seq + i
                ri = i % 2
                for dl, dstb, rd in ((d_lnh, lnhF[ri], f'a_lnhF{ri}'),
                                     (d_lnl, lnlF[ri], f'a_lnlF{ri}')):
                    P.dma('y', dstb.ap().rearrange('p (dc c) -> p dc c',
                          dc=n_dc), dl[a_glob].rearrange('dc p c -> p dc c'),
                          reads=['lnd'], writes=[rd], label=f'lnld{a_glob}')
                accs = [Acc(3 * n_dc) for _ in range(nhalf)]
                for ec in range(n_dc):
                    lh = lnhF[ri].ap(ec * 128, (ec + 1) * 128)
                    ll = lnlF[ri].ap(ec * 128, (ec + 1) * 128)
                    for h2 in range(nhalf):
                        nn = min(512, D - h2 * 512)
                        bk2 = 2 * (i % 2) + h2
                        pk, pkr = psum_mm[bk2], f'pb{bk2}'
                        for stat, statr, mov, movr in (
                                (lh, f'a_lnhF{ri}', wv[0], 'aw_vh'),
                                (lh, f'a_lnhF{ri}', wv[1], 'aw_vl'),
                                (ll, f'a_lnlF{ri}', wv[0], 'aw_vh')):
                            st, sp = accs[h2].nxt()
                            P.record('t', lambda e, o=pk.ap()[:, 0:nn], l=stat,
                                     r=mov.ap(ec * D + h2 * 512,
                                              ec * D + h2 * 512 + nn),
                                     st=st, sp=sp: e.matmul(o, l, r, start=st,
                                     stop=sp), reads=[statr, movr],
                                     writes=[pkr], accum=('V', seq, i, h2),
                                     label=f'V_{seq}_{i}_{ec}_{h2}')
                for h2 in range(nhalf):
                    nn = min(512, D - h2 * 512)
                    bk2 = 2 * (i % 2) + h2
                    P.record('v', lambda e, o=vtm.ap(h2 * 512, h2 * 512 + nn),
                             i0=psum_mm[bk2].ap()[:, 0:nn],
                             i1=bb['v'].ap(h2 * 512, h2 * 512 + nn):
                             e.tensor_tensor(o, i0, i1, ALU.add),
                             reads=[f'pb{bk2}', 'a_bb'], writes=['a_vtm'])
                P.record('v', lambda e, o=Vhi.ap(i * D, (i + 1) * D),
                         i=vtm.ap(): e.tensor_copy(o, i), reads=['a_vtm'],
                         writes=['a_Vhi'])
                P.record('v', lambda e, o=Vlo.ap(i * D, (i + 1) * D),
                         i0=vtm.ap(), i1=Vhi.ap(i * D, (i + 1) * D):
                         e.tensor_tensor(o, i0, i1, ALU.subtract),
                         reads=['a_vtm', 'a_Vhi'], writes=['a_Vlo'])
            for b in wv:
                b.free()
            vtm.free()
            P.barrier()
            wq = [mem.bf16('a_wqh', n_dc * D), mem.bf16('a_wql', n_dc * D)]
            ld_wt('q', wq)
            bias_bcast('q', bb['q'], brow)
            qtm = mem.f32('a_qtm', D)
            qhi_t = mem.bf16('a_qhit', D)
            qlo_t = mem.bf16('a_qlot', D)
            qF = [mem.bf16(f'a_qF{i}', n_dc * 128) for i in range(2)]
            for i in range(n_kc):
                a_glob = seq * ntile_seq + i
                ri = i % 2
                for dl, dstb, rd in ((d_lnh, lnhF[ri], f'a_lnhF{ri}'),
                                     (d_lnl, lnlF[ri], f'a_lnlF{ri}')):
                    P.dma('y', dstb.ap().rearrange('p (dc c) -> p dc c',
                          dc=n_dc), dl[a_glob].rearrange('dc p c -> p dc c'),
                          reads=['lnd'], writes=[rd], label=f'lnld2{a_glob}')
                accs = [Acc(3 * n_dc) for _ in range(nhalf)]
                for ec in range(n_dc):
                    lh = lnhF[ri].ap(ec * 128, (ec + 1) * 128)
                    ll = lnlF[ri].ap(ec * 128, (ec + 1) * 128)
                    for h2 in range(nhalf):
                        nn = min(512, D - h2 * 512)
                        bk2 = 2 * (i % 2) + h2
                        pk, pkr = psum_mm[bk2], f'pb{bk2}'
                        for stat, statr, mov, movr in (
                                (lh, f'a_lnhF{ri}', wq[0], 'aw_qh'),
                                (lh, f'a_lnhF{ri}', wq[1], 'aw_ql'),
                                (ll, f'a_lnlF{ri}', wq[0], 'aw_qh')):
                            st, sp = accs[h2].nxt()
                            P.record('t', lambda e, o=pk.ap()[:, 0:nn], l=stat,
                                     r=mov.ap(ec * D + h2 * 512,
                                              ec * D + h2 * 512 + nn),
                                     st=st, sp=sp: e.matmul(o, l, r, start=st,
                                     stop=sp), reads=[statr, movr],
                                     writes=[pkr], accum=('Q', seq, i, h2),
                                     label=f'Q_{seq}_{i}_{ec}_{h2}')
                for h2 in range(nhalf):
                    nn = min(512, D - h2 * 512)
                    bk2 = 2 * (i % 2) + h2
                    P.record('v', lambda e, o=qtm.ap(h2 * 512, h2 * 512 + nn),
                             i0=psum_mm[bk2].ap()[:, 0:nn],
                             i1=bb['q'].ap(h2 * 512, h2 * 512 + nn):
                             e.tensor_tensor(o, i0, i1, ALU.add),
                             reads=[f'pb{bk2}', 'a_bb'], writes=['a_qtm'])
                P.record('v', lambda e, o=qhi_t.ap(), i=qtm.ap():
                         e.tensor_copy(o, i), reads=['a_qtm'],
                         writes=['a_qhit'])
                P.record('v', lambda e, o=qlo_t.ap(), i0=qtm.ap(),
                         i1=qhi_t.ap(): e.tensor_tensor(o, i0, i1,
                         ALU.subtract), reads=['a_qtm', 'a_qhit'],
                         writes=['a_qlot'])
                for src, rs, dl in ((qhi_t, 'a_qhit', d_qh),
                                    (qlo_t, 'a_qlot', d_ql)):
                    qf, qfres = qF[ri], f'a_qF{ri}'
                    blks = [src.ap(dc * 128, (dc + 1) * 128)
                            for dc in range(n_dc)]
                    tr_tile(blks, rs, qf.ap3(0, 128, n_dc, 128), qfres, n_dc,
                            ('QT', seq, i, rs))
                    P.dma('s', dl[a_glob].rearrange('dc p c -> p dc c'),
                          qf.ap().rearrange('p (dc c) -> p dc c', dc=n_dc),
                          reads=[qfres], writes=['qd'], label=f'qst{a_glob}')
            for b in wq + qF + lnhF + lnlF:
                b.free()
            qtm.free(); qhi_t.free(); qlo_t.free()
            P.barrier()
            wo = [mem.bf16('a_woh', n_dc * D), mem.bf16('a_wol', n_dc * D)]
            ld_wt('o', wo)
            bias_bcast('o', bb['o'], brow)
            Pu = [mem.f32(f'a_Pu{i}', S) for i in range(2)]
            x1r = [mem.f32(f'a_x1r{i}', D) for i in range(2)]
            Phi = mem.bf16('a_Phi', S)
            Plo = mem.bf16('a_Plo', S)
            PhiT = mem.bf16('a_PhiT', n_kc * 128)
            PloT = mem.bf16('a_PloT', n_kc * 128)
            qh_ = [mem.bf16(f'a_q3h{i}', n_dc * 128) for i in range(1)]
            ql_ = [mem.bf16(f'a_q3l{i}', n_dc * 128) for i in range(1)]
            athi = mem.bf16('a_athi', D)
            atlo = mem.bf16('a_atlo', D)
            atFh = mem.bf16('a_atFh', n_dc * 128)
            atFl = mem.bf16('a_atFl', n_dc * 128)
            sc = 1.0 / math.sqrt(D)

            def st_scores(i):
                ri = i % 2
                a_glob = seq * ntile_seq + i
                for dl, dstb, rd in ((d_qh, qh_[0], 'a_q3h0'),
                                     (d_ql, ql_[0], 'a_q3l0')):
                    P.dma('y', dstb.ap().rearrange('p (dc c) -> p dc c',
                          dc=n_dc), dl[a_glob].rearrange('dc p c -> p dc c'),
                          reads=['qd'], writes=[rd], label=f'qld{a_glob}')
                P.dma('y', x1r[ri].ap(),
                      d_x1[a_glob * 128:(a_glob + 1) * 128, :],
                      reads=['x1d'], writes=[f'a_x1r{ri}'],
                      label=f'a3x{a_glob}')
                cmb = lambda c: stb[i % 2].ap(c, c + 1)
                rst = f'a_st{i % 2}'
                for g in range(n_kg):
                    nn = min(512, S - g * 512)
                    pq, pqr = psum_mm[g], f'pb{g}'
                    qacc = Acc(3 * n_dc)
                    for dc in range(n_dc):
                        qh = qh_[0].ap(dc * 128, (dc + 1) * 128)
                        ql = ql_[0].ap(dc * 128, (dc + 1) * 128)
                        for stat, statr, mov, movr in (
                                (qh, 'a_q3h0', Khi, 'a_Khi'),
                                (qh, 'a_q3h0', Klo, 'a_Klo'),
                                (ql, 'a_q3l0', Khi, 'a_Khi')):
                            st, sp = qacc.nxt()
                            P.record('t', lambda e, o=pq.ap()[:, 0:nn], l=stat,
                                     r=mov.ap(dc * S + g * 512,
                                              dc * S + g * 512 + nn),
                                     st=st, sp=sp: e.matmul(o, l, r, start=st,
                                     stop=sp), reads=[statr, movr],
                                     writes=[pqr], accum=('sc', seq, i, g),
                                     label=f'sc_{seq}_{i}_{dc}_{g}')
                    P.record('v', lambda e, o=cmb(g),
                             i_=pq.ap()[:, 0:nn]: e.reduce_max(
                             o, i_, AX.X), reads=[pqr], writes=[rst + f'm{g}'])
                    P.record('v', lambda e, o=cmb(4 + g), i_=cmb(g):
                             e.tensor_scalar(o, i_, -sc, None, ALU.mult),
                             reads=[rst + f'm{g}'], writes=[rst + f'b{g}'])
                    P.record('s', lambda e, o=Pu[i % 2].ap(g * 512,
                             g * 512 + nn), i_=pq.ap()[:, 0:nn], b=cmb(4 + g),
                             a=cmb(8 + g): e.activation(
                             o, i_, ACT.Exp, bias=b, scale=sc, accum_out=a),
                             reads=[pqr, rst + f'b{g}'],
                             writes=[f'a_Pu{i % 2}', rst + f's{g}'])

            def st_rest(i):
                ri = i % 2
                a_glob = seq * ntile_seq + i
                cmb = lambda c: stb[i % 2].ap(c, c + 1)
                rst = f'a_st{i % 2}'
                P.record('v', lambda e, o=cmb(12), i_=stb[i % 2].ap(0, n_kg):
                         e.reduce_max(o, i_, AX.X),
                         reads=[rst + f'm{g2}' for g2 in range(n_kg)],
                         writes=[rst + 'gm'])
                for g in range(n_kg):
                    P.record('v', lambda e, o=cmb(16 + g), i_=cmb(g),
                             m=cmb(12): e.tensor_scalar(o, i_, m, None,
                             ALU.subtract),
                             reads=[rst + f'm{g}', rst + 'gm'],
                             writes=[rst + f'd{g}'])
                P.record('s', lambda e, o=stb[i % 2].ap(16, 16 + n_kg),
                         i_=stb[i % 2].ap(16, 16 + n_kg): e.activation(
                         o, i_, ACT.Exp, scale=sc),
                         reads=[rst + f'd{g2}' for g2 in range(n_kg)],
                         writes=[rst + 'f'])
                P.record('v', lambda e, o=stb[i % 2].ap(8, 8 + n_kg),
                         i0=stb[i % 2].ap(8, 8 + n_kg),
                         i1=stb[i % 2].ap(16, 16 + n_kg): e.tensor_tensor(
                         o, i0, i1, ALU.mult),
                         reads=[rst + 'f'] + [rst + f's{g2}' for g2 in
                                              range(n_kg)],
                         writes=[rst + 'ws'])
                P.record('v', lambda e, o=cmb(13), i_=stb[i % 2].ap(8, 8 + n_kg):
                         e.reduce_sum(o, i_, AX.X), reads=[rst + 'ws'],
                         writes=[rst + 'tot'])
                P.record('v', lambda e, o=cmb(14), i_=cmb(13): e.reciprocal(
                    o, i_), reads=[rst + 'tot'], writes=[rst + 'r'])
                P.record('v', lambda e, o=stb[i % 2].ap(16, 16 + n_kg),
                         i0=stb[i % 2].ap(16, 16 + n_kg), s1=cmb(14):
                         e.tensor_scalar(o, i0, s1, None, ALU.mult),
                         reads=[rst + 'f', rst + 'r'], writes=[rst + 'rf'])
                for g in range(n_kg):
                    nn = min(512, S - g * 512)
                    P.record('v', lambda e, o=Pu[i % 2].ap(g * 512,
                             g * 512 + nn), s1=cmb(16 + g): e.tensor_scalar(
                             o, o, s1, None, ALU.mult),
                             reads=[f'a_Pu{i % 2}', rst + 'rf'],
                             writes=[f'a_Pu{i % 2}'])
                P.record('v', lambda e, o=Phi.ap(), i=Pu[i % 2].ap():
                         e.tensor_copy(o, i), reads=[f'a_Pu{i % 2}'],
                         writes=['a_Phi'])
                P.record('v', lambda e, o=Plo.ap(), i0=Pu[i % 2].ap(),
                         i1=Phi.ap(): e.tensor_tensor(o, i0, i1, ALU.subtract),
                         reads=[f'a_Pu{i % 2}', 'a_Phi'], writes=['a_Plo'])
                for src, rs, dstb, rd in ((Phi, 'a_Phi', PhiT, 'a_PhiT'),
                                          (Plo, 'a_Plo', PloT, 'a_PloT')):
                    nb = 8 if n_kc >= 8 else n_kc
                    for cb in range(0, n_kc, nb):
                        blks = [src.ap(kc * 128, (kc + 1) * 128)
                                for kc in range(cb, cb + nb)]
                        tr_tile(blks, rs, dstb.ap3(cb * 128, 128, nb, 128),
                                rd, nb, ('PT', seq, i, cb, rd))
                vaccs = [Acc(3 * n_kc) for _ in range(nhalf)]
                for kc in range(n_kc):
                    ph = PhiT.ap(kc * 128, (kc + 1) * 128)
                    pl = PloT.ap(kc * 128, (kc + 1) * 128)
                    for h2 in range(nhalf):
                        nn = min(512, D - h2 * 512)
                        pv = psum_mm[4 + h2] if nhalf > 1 else psum_mm[4]
                        pvr = f'pb{4 + h2}' if nhalf > 1 else 'pb4'
                        for stat, statr, mov, movr in (
                                (ph, 'a_PhiT', Vhi, 'a_Vhi'),
                                (ph, 'a_PhiT', Vlo, 'a_Vlo'),
                                (pl, 'a_PloT', Vhi, 'a_Vhi')):
                            st, sp = vaccs[h2].nxt()
                            P.record('t', lambda e, o=pv.ap()[:, 0:nn], l=stat,
                                     r=mov.ap(kc * D + h2 * 512,
                                              kc * D + h2 * 512 + nn),
                                     st=st, sp=sp: e.matmul(o, l, r, start=st,
                                     stop=sp), reads=[statr, movr],
                                     writes=[pvr], accum=('pv', seq, i, h2),
                                     label=f'pv_{seq}_{i}_{kc}_{h2}')
                for h2 in range(nhalf):
                    nn = min(512, D - h2 * 512)
                    pv = psum_mm[4 + h2] if nhalf > 1 else psum_mm[4]
                    pvr = f'pb{4 + h2}' if nhalf > 1 else 'pb4'
                    P.record('v', lambda e, o=athi.ap(h2 * 512, h2 * 512 + nn),
                             i=pv.ap()[:, 0:nn]: e.tensor_copy(o, i),
                             reads=[pvr], writes=['a_athi'])
                    P.record('v', lambda e, o=atlo.ap(h2 * 512, h2 * 512 + nn),
                             i0=pv.ap()[:, 0:nn],
                             i1=athi.ap(h2 * 512, h2 * 512 + nn):
                             e.tensor_tensor(o, i0, i1, ALU.subtract),
                             reads=[pvr, 'a_athi'], writes=['a_atlo'])
                for src, rs, dstb, rd in ((athi, 'a_athi', atFh, 'a_atFh'),
                                          (atlo, 'a_atlo', atFl, 'a_atFl')):
                    blks = [src.ap(dc * 128, (dc + 1) * 128)
                            for dc in range(n_dc)]
                    tr_tile(blks, rs, dstb.ap3(0, 128, n_dc, 128), rd, n_dc,
                            ('AT', seq, i, rd))
                oaccs = [Acc(3 * n_dc) for _ in range(nhalf)]
                for dv in range(n_dc):
                    ah = atFh.ap(dv * 128, (dv + 1) * 128)
                    al = atFl.ap(dv * 128, (dv + 1) * 128)
                    for h2 in range(nhalf):
                        nn = min(512, D - h2 * 512)
                        po = psum_mm[4 + h2] if nhalf > 1 else psum_mm[4]
                        por = f'pb{4 + h2}' if nhalf > 1 else 'pb4'
                        for stat, statr, mov, movr in (
                                (ah, 'a_atFh', wo[0], 'aw_oh'),
                                (ah, 'a_atFh', wo[1], 'aw_ol'),
                                (al, 'a_atFl', wo[0], 'aw_oh')):
                            st, sp = oaccs[h2].nxt()
                            P.record('t', lambda e, o=po.ap()[:, 0:nn], l=stat,
                                     r=mov.ap(dv * D + h2 * 512,
                                              dv * D + h2 * 512 + nn),
                                     st=st, sp=sp: e.matmul(o, l, r, start=st,
                                     stop=sp), reads=[statr, movr],
                                     writes=[por], accum=('O', seq, i, h2),
                                     label=f'O_{seq}_{i}_{dv}_{h2}')
                for h2 in range(nhalf):
                    nn = min(512, D - h2 * 512)
                    po = psum_mm[4 + h2] if nhalf > 1 else psum_mm[4]
                    por = f'pb{4 + h2}' if nhalf > 1 else 'pb4'
                    P.record('v', lambda e, o=x1r[ri].ap(h2 * 512,
                             h2 * 512 + nn), i0=po.ap()[:, 0:nn],
                             i1=x1r[ri].ap(h2 * 512, h2 * 512 + nn):
                             e.tensor_tensor(o, i0, i1, ALU.add),
                             reads=[por, f'a_x1r{ri}'],
                             writes=[f'a_x1r{ri}'])
                    P.record('v', lambda e, o=x1r[ri].ap(h2 * 512,
                             h2 * 512 + nn), i1=bb['o'].ap(h2 * 512,
                             h2 * 512 + nn): e.tensor_tensor(o, o, i1,
                             ALU.add), reads=['a_bb', f'a_x1r{ri}'],
                             writes=[f'a_x1r{ri}'])
                P.dma('s', d_x2[a_glob * 128:(a_glob + 1) * 128, :],
                      x1r[ri].ap(), reads=[f'a_x1r{ri}'], writes=['x2d'],
                      label=f'x2st{a_glob}')

            st_scores(0)
            for i in range(n_kc):
                if i + 1 < n_kc:
                    st_scores(i + 1)
                st_rest(i)
            for b in wo + qh_ + ql_ + x1r:
                b.free()
            for b in Pu + [Phi, Plo, PhiT, PloT, athi, atlo, atFh, atFl]:
                b.free()
            P.barrier()
        for b in [Khi, Klo, Vhi, Vlo, bbuf] + stb:
            b.free()

    emit_wquant(d_w1, H, D, d_m1T, 'd_w1mT', cdesc1, 'w1_cdesc', 'w1')
    P.barrier()
    emit_wquant(d_w2, D, H, d_m2T, 'd_w2mT', cdesc2, 'w2_cdesc', 'w2')
    P.barrier()
    if PH == 'f1':
        emit_ffn(d_x, d_out, 'x_in', 'outd', 'F1')
    else:
        emit_ffn(d_x, d_x1, 'x_in', 'x1d', 'F1')
        P.barrier()
        emit_attn_all()
        P.barrier()
        emit_ffn(d_x2, d_out, 'x2d', 'outd', 'F2')
    P.record('y', lambda e: None, reads=['outd'], writes=[], label='final')

    from contextlib import ExitStack
    with ExitStack() as stack:
        P.materialize(stack)
    nstats = {e: len(P.ops[e]) for e in ENGINES}
    return nc, nstats


_NCORE = 8


def _mk_inputs_common(Wq, bq, Wk, bk, Wv, bv, Wo, bo, w1, conv_w, conv_b, w2):
    f32 = np.float32
    cw = np.ascontiguousarray(np.asarray(conv_w, dtype=f32)).reshape(-1, 3)
    com = {
        'w1': np.ascontiguousarray(np.asarray(w1, dtype=f32)),
        'w2': np.ascontiguousarray(np.asarray(w2, dtype=f32)),
        'cw0': np.ascontiguousarray(cw[:, 0].reshape(-1, 128).T),
        'cw1': np.ascontiguousarray(cw[:, 1].reshape(-1, 128).T),
        'cw2': np.ascontiguousarray(cw[:, 2].reshape(-1, 128).T),
        'cb': np.ascontiguousarray(
            np.asarray(conv_b, dtype=f32).reshape(-1, 128).T),
        'wq': np.ascontiguousarray(np.asarray(Wq, dtype=f32)),
        'wk': np.ascontiguousarray(np.asarray(Wk, dtype=f32)),
        'wv': np.ascontiguousarray(np.asarray(Wv, dtype=f32)),
        'wo': np.ascontiguousarray(np.asarray(Wo, dtype=f32)),
        'bq': np.ascontiguousarray(np.asarray(bq, dtype=f32).reshape(1, -1)),
        'bk': np.ascontiguousarray(np.asarray(bk, dtype=f32).reshape(1, -1)),
        'bv': np.ascontiguousarray(np.asarray(bv, dtype=f32).reshape(1, -1)),
        'bo': np.ascontiguousarray(np.asarray(bo, dtype=f32).reshape(1, -1)),
        'ident': np.eye(128, dtype=ml_dtypes.bfloat16),
    }
    return com


def kernel(L0_attn_x, L0_attn_Wq, L0_attn_bq, L0_attn_Wk, L0_attn_bk,
           L0_attn_Wv, L0_attn_bv, L0_attn_Wo, L0_attn_bo, L0_ffn_w1,
           L0_ffn_conv_w, L0_ffn_conv_b, L0_ffn_w2):
    x = np.ascontiguousarray(np.asarray(L0_attn_x, dtype=np.float32))
    B, S, D = x.shape
    nper = B // _NCORE
    cfg = dict(S=S, D=D, H=L0_ffn_w1.shape[0], NSEQ=nper, TF=256)
    nc, _ = build_core(cfg)
    com = _mk_inputs_common(L0_attn_Wq, L0_attn_bq, L0_attn_Wk, L0_attn_bk,
                            L0_attn_Wv, L0_attn_bv, L0_attn_Wo, L0_attn_bo,
                            L0_ffn_w1, L0_ffn_conv_w, L0_ffn_conv_b, L0_ffn_w2)
    in_maps = []
    for i in range(_NCORE):
        m = dict(com)
        m['x_in'] = np.ascontiguousarray(
            x[i * nper:(i + 1) * nper].reshape(nper * S, D))
        in_maps.append(m)
    res = run_bass_kernel_spmd(nc, in_maps, core_ids=list(range(_NCORE)))
    out = np.concatenate(
        [np.asarray(r['out']).reshape(nper, S, D) for r in res.results], axis=0)
    return out.astype(np.float32)
